# version 11
# baseline (speedup 1.0000x reference)
import math
import numpy as np
import ml_dtypes
import concourse.bass as bass
import concourse.mybir as mybir
from concourse.bass_utils import run_bass_kernel_spmd

F32 = mybir.dt.float32
BF16 = mybir.dt.bfloat16
AF = mybir.ActivationFunctionType
ALU = mybir.AluOpType
AX = mybir.AxisListType

D = 2048
NH = 16
HD = 128
DC = D // 128
EPS_LN = 1e-5
NEG_BIG = -1.0e30
BF = ml_dtypes.bfloat16

ENGINES = ("tensor", "vector", "scalar", "sync", "gpsimd")
DMA_QUEUES = ("sync", "gpsimd")
_ESZ = {F32: 4, BF16: 2, mybir.dt.int32: 4}
SB_BLK = 512
DR_BLK = 65536
_SB_OFF = {}


def ap_keys(ap):
    t = ap.tensor
    name = t.name
    dims = ap.ap
    sp = str(ap.space)
    if "PSUM" in sp:
        return [("ps", name.rsplit("_", 1)[0] if name not in _PS_NAMES else name)]
    esz = _ESZ[ap.dtype]
    if "DRAM" in sp.upper():
        off = ap.offset
        span = 1
        for st, cnt in dims:
            span += (cnt - 1) * st
        b0 = off // DR_BLK
        b1 = (off + span - 1) // DR_BLK
        return [("d", name, b) for b in range(b0, min(b1, b0 + 63) + 1)]
    pstep = dims[0][0] if dims else 1
    off = ap.offset % pstep if pstep > 0 else ap.offset
    span = 1
    for st, cnt in dims[1:]:
        span += (cnt - 1) * st
    base = _SB_OFF.get(name.rsplit("_", 1)[0], _SB_OFF.get(name, 0))
    lo = base + off * esz
    hi = base + (off + span) * esz
    b0 = lo // SB_BLK
    b1 = (hi - 1) // SB_BLK
    return [("s", b) for b in range(b0, b1 + 1)]


_PS_NAMES = set()
NLANES = 8


class Sched:
    def __init__(self, nc):
        self.nc = nc
        self.ops = []
        self.n_local = {e: 0 for e in ENGINES}
        self.n_dma = {q: 0 for q in DMA_QUEUES}
        self.last_writer = {}
        self.readers = {}
        self.clock = {e: {} for e in ENGINES}
        self.opclock = []

    def add(self, eng, fn, reads=(), writes=(), name="", dma=False):
        i = len(self.ops)
        rkeys = []
        for ap in reads:
            rkeys.extend(ap_keys(ap))
        wkeys = []
        for ap in writes:
            wkeys.extend(ap_keys(ap))
        deps = set()
        for k in rkeys:
            j = self.last_writer.get(k)
            if j is not None:
                deps.add(j)
        for k in wkeys:
            j = self.last_writer.get(k)
            if j is not None:
                deps.add(j)
            for j in self.readers.get(k, ()):
                deps.add(j)
        cl = self.clock[eng]
        lane = None
        pre_wait = None
        if dma:
            idx = self.n_dma[eng]
            lane = (eng, idx % NLANES)
            rnd = idx // NLANES + 1
            if rnd > 1:
                pre_wait = (lane, (rnd - 1) * 16)
                if cl.get(lane, -1) < rnd - 1:
                    cl[lane] = rnd - 1
            self.n_dma[eng] += 1
        need = []
        for j in sorted(deps):
            oj = self.ops[j]
            if cl.get(oj["ckey"], -1) >= oj["cpos"]:
                continue
            need.append(j)
        same_drain = False
        waits = []
        for j in need:
            oj = self.ops[j]
            if oj["eng"] == eng and not oj["dma"]:
                if eng == "tensor":
                    continue
                same_drain = True
            else:
                waits.append(j)
                oj["inc"] = True
        if same_drain:
            cl[eng] = self.n_local[eng] - 1
        for j in waits:
            for e2, v in self.opclock[j].items():
                if cl.get(e2, -1) < v:
                    cl[e2] = v
        ckey = lane if dma else eng
        cpos = (self.n_dma[eng] - 1) // NLANES + 1 if dma else self.n_local[eng]
        op = dict(idx=i, eng=eng, fn=fn, local=self.n_local[eng], dma=dma,
                  inc=dma, waits=waits, drain=same_drain, name=name,
                  ckey=ckey, cpos=cpos, pre_wait=pre_wait)
        self.ops.append(op)
        self.n_local[eng] += 1
        oc = dict(cl)
        oc[ckey] = cpos
        if dma:
            oc[eng] = op["local"]
        self.opclock.append(oc)
        for k in rkeys:
            self.readers.setdefault(k, []).append(i)
        for k in wkeys:
            self.last_writer[k] = i
            self.readers[k] = []
        return i

    def dma(self, queue, out, in_, name="d"):
        return self.add(queue,
                        lambda e, o=out, i_=in_: e.dma_start(out=o, in_=i_),
                        reads=[in_], writes=[out], name="dma:" + name, dma=True)

    def emit(self, blk, final_queue="sync"):
        nc = self.nc
        sems = {e: nc.alloc_semaphore(f"sem_{e}") for e in ENGINES}
        for q in DMA_QUEUES:
            for l in range(NLANES):
                sems[(q, l)] = nc.alloc_semaphore(f"dma_{q}_{l}")
        cnt = {e: 0 for e in ENGINES}
        for op in self.ops:
            if op["inc"]:
                if op["dma"]:
                    op["inc_val"] = op["cpos"] * 16
                    op["sem"] = op["ckey"]
                else:
                    cnt[op["eng"]] += 1
                    op["inc_val"] = cnt[op["eng"]]
                    op["sem"] = op["eng"]
        per_engine = {e: [] for e in ENGINES}
        for op in self.ops:
            per_engine[op["eng"]].append(op)

        def mk_body(eng_ops):
            def body(eh):
                for op in eng_ops:
                    if op["drain"]:
                        eh.drain()
                    if op["pre_wait"] is not None:
                        eh.wait_ge(sems[op["pre_wait"][0]], op["pre_wait"][1])
                    wmax = {}
                    for j in op["waits"]:
                        oj = self.ops[j]
                        sk = oj["sem"]
                        wmax[sk] = max(wmax.get(sk, 0), oj["inc_val"])
                    for key, v in wmax.items():
                        eh.wait_ge(sems[key], v)
                    inst = op["fn"](eh)
                    if op["inc"]:
                        assert inst is not None, op["name"]
                        inst.then_inc(sems[op["sem"]], 16 if op["dma"] else 1)
            return body

        for e in ENGINES:
            if per_engine[e]:
                getattr(blk, e)(mk_body(per_engine[e]))

        finals = {}
        for op in self.ops:
            if op.get("inc_val"):
                finals[op["sem"]] = max(finals.get(op["sem"], 0), op["inc_val"])

        def final_body(eh):
            for sk, v in finals.items():
                eh.wait_ge(sems[sk], v)
        getattr(blk, final_queue)(final_body)


def rotary_tables(S):
    dim = HD
    SCALE_BASE = 512
    inv_freq = 1.0 / 10000.0 ** (np.arange(0, dim, 2, dtype=np.float64) / dim)
    t = np.arange(S, dtype=np.float64)
    freqs = np.einsum("i,j->ij", t, inv_freq)
    freqs = np.concatenate((freqs, freqs), axis=-1)
    scale_base = (np.arange(0, dim, 2, dtype=np.float64) + 0.4 * dim) / (1.4 * dim)
    power = (t - S // 2) / SCALE_BASE
    scale = scale_base[None, :] ** power[:, None]
    scale = np.concatenate((scale, scale), axis=-1)
    pos32 = freqs.astype(np.float32).astype(np.float64)
    sc32 = scale.astype(np.float32).astype(np.float64)
    sw = 1.0 / math.sqrt(HD)
    Aq = (np.cos(pos32) * sc32 * sw).astype(np.float32).T.copy()
    Bq = (np.sin(pos32) * sc32 * sw).astype(np.float32).T.copy()
    Ak = (np.cos(pos32) / sc32 * sw).astype(np.float32).T.copy()
    Bk = (np.sin(pos32) / sc32 * sw).astype(np.float32).T.copy()
    return Aq, Bq, Ak, Bk


def rot_half_matrix():
    R = np.zeros((HD, HD), dtype=np.float32)
    for hp in range(64):
        R[hp + 64, hp] = -1.0
    for hp in range(64, HD):
        R[hp - 64, hp] = 1.0
    return np.asarray(R.astype(BF))


def causal_mask1(SBv):
    pj = np.arange(128)[:, None]
    i = np.arange(SBv)[None, :]
    m0 = np.where(pj > i, np.float32(NEG_BIG), np.float32(0.0))
    full = np.full((128, 384), np.float32(NEG_BIG))
    return np.concatenate([full, m0], axis=1).astype(BF)


def host_constants(S, SBv):
    Aq, Bq, Ak, Bk = rotary_tables(S)
    return {
        "c_rotAq": Aq, "c_rotBq": Bq, "c_rotAk": Ak, "c_rotBk": Bk,
        "c_rotR": rot_half_matrix(),
        "c_mask1": np.asarray(causal_mask1(SBv)),
        "c_ones_bf": np.ones((128, 1), dtype=BF),
        "c_ones_row": np.ones((1, 128), dtype=np.float32),
        "c_ones_row_bf": np.ones((1, 128), dtype=BF),
        "c_iota8": np.broadcast_to(np.arange(8, dtype=np.float32), (128, 8)).copy(),
        "c_ident_f32": np.eye(128, dtype=np.float32),
        "c_ident_bf": np.eye(128, dtype=BF),
        "c_eps": np.broadcast_to(np.array([1e-5, 1.1920929e-07], dtype=np.float32),
                                 (128, 2)).copy(),
    }


class Core:
    def __init__(self, S):
        self.S = S
        self.SB = min(512, S)
        self.NSC = S // self.SB
        self.JC = S // 128
        self.nc = bass.Bass("TRN2", target_bir_lowering=False)
        self.sch = Sched(self.nc)
        self.dram = {}
        self.sb = {}
        self.pool = [self.nc.alloc_psum_tensor(f"pp{i}", [128, 512], F32)
                     for i in range(8)]
        for p in self.pool:
            _PS_NAMES.add(p.name)
        self.pool_bf = [p.ap().bitcast(BF16) for p in self.pool]
        self.used_inputs = []
        self.base = self.nc.bump_sbuf(212352, align_bytes=512)[0]
        self._cursor = 0

    def din(self, name, shape, dt=F32):
        if name not in self.dram:
            self.dram[name] = self.nc.dram_tensor(name, list(shape), dt,
                                                  kind="ExternalInput").ap()
            self.used_inputs.append(name)
        return self.dram[name]

    def dout(self, name, shape, dt=F32):
        self.dram[name] = self.nc.dram_tensor(name, list(shape), dt,
                                              kind="ExternalOutput").ap()
        return self.dram[name]

    def dtmp(self, name, shape, dt=F32):
        if name not in self.dram:
            self.dram[name] = self.nc.dram_tensor(name, list(shape), dt,
                                                  kind="Internal").ap()
        return self.dram[name]

    def sb_at(self, name, shape, dt, offset):
        assert offset % 128 == 0, (name, offset)
        off = self.base + offset
        h = self.nc.alloc_sbuf_tensor_at(name, list(shape), dt, offset=off)
        self.sb[name] = h
        _SB_OFF[name] = off
        _SB_OFF[h.name] = off
        return h

    def alloc(self, name, shape, dt):
        n = shape[-1] * _ESZ[dt]
        for s in shape[1:-1]:
            n *= s
        n = (n + 127) // 128 * 128
        h = self.sb_at(name, shape, dt, self._cursor)
        self._cursor += n
        return h


def build_program(S=2048, layers=(0, 1), dbg=()):
    core = Core(S)
    nc, sch = core.nc, core.sch
    SB, NSC, JC = core.SB, core.NSC, core.JC
    pool = core.pool
    dbg = set(dbg)

    xn = core.sb_at("xn", [128, DC * S], BF16, 0)
    off_big = 32 * S
    heads = core.sb_at("heads", [128, DC * S], BF16, off_big)
    SH = S // 2
    y_moe = core.sb_at("y_moe", [128, DC * SH], F32, off_big)
    y_l1 = core.sb_at("y_l1", [128, DC * S], F32, 0)
    R3 = off_big + 32 * S
    rotA_q = core.sb_at("rotA_q", [128, S], F32, R3)
    rotB_q = core.sb_at("rotB_q", [128, S], F32, R3 + 4 * S)
    rotA_k = core.sb_at("rotA_k", [128, S], F32, R3 + 8 * S)
    rotB_k = core.sb_at("rotB_k", [128, S], F32, R3 + 12 * S)
    slabs = core.sb_at("slabs", [128, 16 * 1024], BF16, R3)
    pT_sb = core.sb_at("pT_sb", [128, 8192], BF16, R3)
    rms_row = core.sb_at("rms_row", [1, 2048], F32, R3 + 16384)
    l_r3 = core.sb_at("l_r3", [128, 1024], F32, R3 + 24576)
    l_r3b = core.sb_at("l_r3b", [128, 2048], BF16, R3 + 24576)
    dn_sc = core.sb_at("dn_sc", [128, 96], F32, R3 + 28672)
    dn_scb = core.sb_at("dn_scb", [128, 96], BF16, R3 + 29184)
    W = R3 + 32768
    ln_xa = core.sb_at("ln_xa", [128, 2 * S], F32, W)
    ln_xbf = core.sb_at("ln_xbf", [128, S], BF16, W + 8 * S)
    ln_xsq = core.sb_at("ln_xsq", [128, S], BF16, W + 10 * S)
    ln_rows = core.sb_at("ln_rows", [1, 2 * S], F32, W + 12 * S)
    ln_tile = core.sb_at("ln_tile", [128, 512], F32, W + 20 * S)
    at_k = core.sb_at("at_k", [128, S], BF16, W)
    at_v = core.sb_at("at_v", [128, S], BF16, W + 2 * S)
    l1_v2 = core.sb_at("l1_v2", [128, 2 * S], BF16, W + 2 * S)
    at_q = core.sb_at("at_q", [128, S], BF16, W + 4 * S)
    at_qr = core.sb_at("at_qr", [128, S], BF16, W + 6 * S)
    at_sq = core.sb_at("at_sq", [128, S], BF16, W + 4 * S)
    FX = W + 8 * S
    at_sh = core.sb_at("at_sh", [128, 1024], F32, FX)
    at_pt = core.sb_at("at_pt", [128, 2048], BF16, FX + 4096)
    at_negc = core.sb_at("at_negc", [128, 512], F32, FX + 2048)
    at_rb = core.sb_at("at_rb", [128, 512], F32, FX + 8192)
    at_rows = core.sb_at("at_rows", [1, 512], F32, FX + 10240)
    rows_bf = core.sb_at("rows_bf", [1, 1024], BF16, FX + 10240)
    at_rows2 = core.sb_at("at_rows2", [1, 512], F32, FX + 12288)
    at_wslab = core.sb_at("at_wslab", [128, 4096], BF16, FX + 14336)
    l1_tc1 = core.sb_at("l1_tc1", [128, 512], F32, FX + 22528)
    l1_tc2 = core.sb_at("l1_tc2", [128, 512], F32, FX + 24576)
    gh = core.sb_at("gh", [128, 8 * 1024], BF16, W)
    ep_g = core.sb_at("ep_g", [128, 1024], F32, W + 16384)
    ep_s = core.sb_at("ep_s", [128, 1024], F32, W + 20480)
    ep_x = core.sb_at("ep_x", [128, 1024], F32, W + 24576)
    web = core.sb_at("web", [128, 1024], F32, W + 28672)
    hf = core.sb_at("hf", [128, 1024], F32, W + 32768)
    gt_w = core.sb_at("gt_w", [128, 8], F32, W + 36864)
    gt_t = core.sb_at("gt_t", [128, 48], F32, W + 37376)
    gt_r = core.sb_at("gt_r", [128, 4], F32, W + 37888)
    wT8 = core.sb_at("wT8", [8, S], F32, W + 16384)
    f1_wbA = core.sb_at("f1_wbA", [128, 4096], BF16, W)
    f1_wbB = core.sb_at("f1_wbB", [128, 4096], BF16, W + 8192)
    f1_gl = core.sb_at("f1_gl", [128, 1024], F32, W + 16384)
    f1_glb = core.sb_at("f1_glb", [128, 2048], BF16, W + 16384)
    f1_wh = core.sb_at("f1_wh", [128, 4096], BF16, W + 20480)
    f1_t1 = core.sb_at("f1_t1", [128, 2048], F32, W + 28672)
    f1_hq = core.sb_at("f1_hq", [128, 2048], BF16, W + 36864)
    f1_sc = core.sb_at("f1_sc", [128, 96], F32, W + 40960)
    f1_scb = core.sb_at("f1_scb", [128, 96], BF16, W + 41472)
    dn_w = core.sb_at("dn_w", [128, 4096], BF16, W + 16384)
    hn_sb = core.sb_at("hn_sb", [128, 8192], BF16, W + 24576)
    dn_g = core.sb_at("dn_g", [128, 512], F32, W + 40960)
    CW = W + 43008
    core._cursor = CW
    mask1 = core.alloc("mask1", [128, 384 + SB], BF16)
    rotR = core.alloc("rotR", [128, 128], BF16)
    ident = core.alloc("ident", [128, 128], F32)
    identb = core.alloc("identb", [128, 128], BF16)
    ones_bf = core.alloc("ones_bf", [128, 1], BF16)
    ones_row = core.alloc("ones_row", [1, 128], F32)
    iota8 = core.alloc("iota8", [128, 8], F32)
    b_col = core.alloc("b_col", [128, 16], F32)
    wg_sb = core.alloc("wg_sb", [128, 128], BF16)
    lam_sb = core.alloc("lam_sb", [128, 32], F32)
    srow_sb = core.alloc("srow_sb", [1, 64], F32)
    ones_row_bf = core.alloc("ones_row_bf", [1, 128], BF16)
    ceps = core.alloc("ceps", [128, 2], F32)
    assert core._cursor <= 212352, core._cursor

    CONST_SHAPES = {
        "c_rotR": ([128, 128], BF16, rotR), "c_ident_f32": ([128, 128], F32, ident),
        "c_ident_bf": ([128, 128], BF16, identb), "c_ones_bf": ([128, 1], BF16, ones_bf),
        "c_ones_row": ([1, 128], F32, ones_row), "c_iota8": ([128, 8], F32, iota8),
        "c_eps": ([128, 2], F32, ceps), "c_mask1": ([128, 384 + SB], BF16, mask1),
        "c_ones_row_bf": ([1, 128], BF16, ones_row_bf),
    }
    for nm, (shp, dt, tile) in CONST_SHAPES.items():
        d = core.din(nm, shp, dt)
        sch.dma("sync", tile[:], d, name=nm)

    def mask_k(k):
        return mask1[:, 384 - 128 * k: 384 - 128 * k + SB]

    def xn_ap(dc, sc=None, s0=None, sl=None):
        if sc is not None:
            return xn[:, dc * S + sc * SB: dc * S + (sc + 1) * SB]
        return xn[:, dc * S + s0: dc * S + s0 + sl]

    def emit_ln(src_name, tag):
        src = core.dram[src_name]
        for dc in range(DC):
            buf = dc % 2
            xav = ln_xa[:, buf * S:(buf + 1) * S]
            sch.dma("sync", xav, src[dc * 128:(dc + 1) * 128, :], name=f"{tag}x{dc}")
            sch.add("vector", lambda e, o=ln_xbf[:], i=xav: e.tensor_copy(o, i),
                    reads=[xav], writes=[ln_xbf[:]], name="ln_cast")
            sch.add("scalar", lambda e, o=ln_xsq[:], i=xav: e.activation(o, i, AF.Square),
                    reads=[xav], writes=[ln_xsq[:]], name="ln_sq")
            for sc in range(NSC):
                o1 = pool[sc % 4][0:1, 0:SB]
                r1 = ln_xbf[:, sc * SB:(sc + 1) * SB]
                sch.add("tensor",
                        lambda e, o=o1, r=r1, st=(dc == 0), sp=(dc == DC - 1):
                        e.matmul(o, ones_bf[:], r, start=st, stop=sp, skip_group_check=True),
                        reads=[r1, ones_bf[:]], writes=[o1], name="ln_mm1")
                o2 = pool[4 + sc % 4][0:1, 0:SB]
                r2 = ln_xsq[:, sc * SB:(sc + 1) * SB]
                sch.add("tensor",
                        lambda e, o=o2, r=r2, st=(dc == 0), sp=(dc == DC - 1):
                        e.matmul(o, ones_bf[:], r, start=st, stop=sp, skip_group_check=True),
                        reads=[r2, ones_bf[:]], writes=[o2], name="ln_mm2")
        for sc in range(NSC):
            mrow = ln_rows[0:1, sc * SB:(sc + 1) * SB]
            vrow = ln_rows[0:1, S + sc * SB: S + (sc + 1) * SB]
            p1 = pool[sc % 4][0:1, 0:SB]
            p2 = pool[4 + sc % 4][0:1, 0:SB]
            sch.add("vector", lambda e, o=mrow, i=p1:
                    e.tensor_scalar(o, i, 1.0 / D, None, ALU.mult),
                    reads=[p1], writes=[mrow], name="ln_m")
            sch.add("vector", lambda e, o=vrow, i=p2:
                    e.tensor_scalar(o, i, 1.0 / D, None, ALU.mult),
                    reads=[p2], writes=[vrow], name="ln_v")
            tr = ln_tile[0:1, 0:SB]
            sch.add("vector", lambda e, o=tr, m=mrow: e.tensor_mul(o, m, m),
                    reads=[mrow], writes=[tr], name="ln_m2")
            sch.add("vector", lambda e, o=vrow, v=vrow, t=tr: e.tensor_sub(o, v, t),
                    reads=[vrow, tr], writes=[vrow], name="ln_var")
            sch.add("scalar", lambda e, o=vrow, i=vrow, b=ceps[0:1, 0:1]:
                    e.activation(o, i, AF.Sqrt, bias=b),
                    reads=[vrow, ceps[0:1, 0:1]], writes=[vrow], name="ln_sqrt")
            sch.add("vector", lambda e, o=vrow, i=vrow: e.reciprocal(o, i),
                    reads=[vrow], writes=[vrow], name="ln_recip")
        for sc in range(NSC):
            mrow = ln_rows[0:1, sc * SB:(sc + 1) * SB]
            vrow = ln_rows[0:1, S + sc * SB: S + (sc + 1) * SB]
            pm = pool[0][:, 0:SB]
            pi = pool[1][:, 0:SB]
            sch.add("tensor", lambda e, o=pm, r=mrow:
                    e.matmul(o, ones_row[:], r, start=True, stop=True),
                    reads=[mrow, ones_row[:]], writes=[pm], name="ln_bm")
            sch.add("tensor", lambda e, o=pi, r=vrow:
                    e.matmul(o, ones_row[:], r, start=True, stop=True),
                    reads=[vrow, ones_row[:]], writes=[pi], name="ln_bi")
            for dc in range(DC):
                buf = dc % 2
                xt = ln_xa[:, buf * S + sc * SB: buf * S + (sc + 1) * SB]
                sch.dma("sync", xt, src[dc * 128:(dc + 1) * 128, sc * SB:(sc + 1) * SB],
                        name=f"{tag}x2_{dc}_{sc}")
                tf = ln_tile[:, 0:SB]
                xo = xn_ap(dc, sc)
                sch.add("vector", lambda e, o=tf, a=xt, b=pm: e.tensor_sub(o, a, b),
                        reads=[xt, pm], writes=[tf], name="ln_n1")
                sch.add("vector", lambda e, o=xo, a=tf, b=pi: e.tensor_mul(o, a, b),
                        reads=[tf, pi], writes=[xo], name="ln_n2")

    def emit_attn0():
        for nm, tile in [("c_rotAq", rotA_q), ("c_rotBq", rotB_q),
                         ("c_rotAk", rotA_k), ("c_rotBk", rotB_k)]:
            d = core.din(nm, [128, S], F32)
            sch.dma("sync", tile[:], d, name=nm)
        wkv = core.din("wkvT_t", [2, D, 128], F32)
        wq_t = core.din("wqT_t", [NH, D, 128], F32)
        wp_t = core.din("wprojT_t", [DC, D, 128], F32)
        bpc = core.din("bproj_col", [128, 16], F32)
        sch.dma("sync", b_col[:], bpc, name="bcol")

        kslab = at_wslab[:, 0:2048]
        vslab = at_wslab[:, 2048:4096]
        sch.dma("gpsimd", kslab.rearrange("p (c n) -> p c n", n=128),
                wkv[0].rearrange("(c p) n -> p c n", p=128), name="wkvk")
        sch.dma("gpsimd", vslab.rearrange("p (c n) -> p c n", n=128),
                wkv[1].rearrange("(c p) n -> p c n", p=128), name="wkvv")

        def rotary(src_bf, dst_bf, tabA, tabB, tag):
            for sc in range(NSC):
                srcv = src_bf[:, sc * SB:(sc + 1) * SB]
                p3 = pool[3][:, 0:SB]
                sch.add("tensor", lambda e, o=p3, r=srcv:
                        e.matmul(o, rotR[:], r, start=True, stop=True),
                        reads=[srcv, rotR[:]], writes=[p3], name=tag + "rot")
                sh = at_sh[:, 0:SB]
                tb = tabB[:, sc * SB:(sc + 1) * SB]
                sch.add("vector", lambda e, o=sh, a=tb, b=p3: e.tensor_mul(o, a, b),
                        reads=[p3, tb], writes=[sh], name=tag + "rb")
                sh2 = at_sh[:, SB:2 * SB]
                ta = tabA[:, sc * SB:(sc + 1) * SB]
                sch.add("vector", lambda e, o=sh2, a=ta, b=srcv: e.tensor_mul(o, a, b),
                        reads=[srcv, ta], writes=[sh2], name=tag + "ra")
                dv = dst_bf[:, sc * SB:(sc + 1) * SB]
                sch.add("vector", lambda e, o=dv, a=sh2, b=sh: e.tensor_add(o, a, b),
                        reads=[sh, sh2], writes=[dv], name=tag + "rs")

        for sc in range(NSC):
            p2 = pool[2][:, 0:SB]
            for dc in range(DC):
                l = kslab[:, dc * 128:(dc + 1) * 128]
                r = xn_ap(dc, sc)
                sch.add("tensor", lambda e, o=p2, l=l, r=r, st=(dc == 0), sp=(dc == DC - 1):
                        e.matmul(o, l, r, start=st, stop=sp),
                        reads=[l, r], writes=[p2], name="kmm")
            kt = at_q[:, sc * SB:(sc + 1) * SB]
            sch.add("scalar", lambda e, o=kt, i=p2: e.activation(o, i, AF.Copy),
                    reads=[p2], writes=[kt], name="kevac")
        rotary(at_q, at_k, rotA_k, rotB_k, "k")
        sch.add("scalar", lambda e, o=at_sq[:], i=at_k[:]: e.activation(o, i, AF.Square),
                reads=[at_k[:]], writes=[at_sq[:]], name="ksq")
        for sc in range(NSC):
            p4 = pool[4][0:1, 0:SB]
            r = at_sq[:, sc * SB:(sc + 1) * SB]
            sch.add("tensor", lambda e, o=p4, r=r:
                    e.matmul(o, ones_bf[:], r, start=True, stop=True),
                    reads=[r, ones_bf[:]], writes=[p4], name="kn2")
            kd = srow_sb[0:1, 1 + sc:2 + sc]
            sch.add("vector", lambda e, o=kd, i=p4: e.reduce_max(o, i, AX.X),
                    reads=[p4], writes=[kd], name="kmax")
        kall = srow_sb[0:1, 1:1 + NSC]
        km = srow_sb[0:1, 0:1]
        sch.add("vector", lambda e, o=km, i=kall: e.reduce_max(o, i, AX.X),
                reads=[kall], writes=[km], name="kmaxall")
        sch.add("scalar", lambda e, o=km, i=km: e.activation(o, i, AF.Sqrt),
                reads=[km], writes=[km], name="kmaxsqrt")
        sch.add("vector", lambda e, o=km, i=km: e.tensor_scalar(o, i, -1.0, None, ALU.mult),
                reads=[km], writes=[km], name="kmaxneg")
        for g in range((JC + 3) // 4):
            n_in_g = min(4, JC - 4 * g)
            p5 = pool[5]
            for j in range(n_in_g):
                jc = 4 * g + j
                for dc in range(DC):
                    l = xn_ap(dc, s0=jc * 128, sl=128)
                    r = vslab[:, dc * 128:(dc + 1) * 128]
                    o = p5[:, j * 128:(j + 1) * 128]
                    sch.add("tensor", lambda e, o=o, l=l, r=r, st=(dc == 0), sp=(dc == DC - 1):
                            e.matmul(o, l, r, start=st, stop=sp, skip_group_check=True),
                            reads=[l, r], writes=[o], name="vmm")
            vv = at_v[:, g * 512:g * 512 + n_in_g * 128]
            pv = p5[:, 0:n_in_g * 128]
            sch.add("vector", lambda e, o=vv, i=pv: e.tensor_copy(o, i),
                    reads=[pv], writes=[vv], name="vevac")
        for n in range(NH):
            qb = n % 2
            qslab = at_wslab[:, qb * 2048:(qb + 1) * 2048]
            sch.dma("gpsimd", qslab.rearrange("p (c n) -> p c n", n=128),
                    wq_t[n].rearrange("(c p) n -> p c n", p=128), name=f"wq{n}")
            for sc in range(NSC):
                p6 = pool[6][:, 0:SB]
                for dc in range(DC):
                    l = qslab[:, dc * 128:(dc + 1) * 128]
                    r = xn_ap(dc, sc)
                    sch.add("tensor", lambda e, o=p6, l=l, r=r, st=(dc == 0), sp=(dc == DC - 1):
                            e.matmul(o, l, r, start=st, stop=sp),
                            reads=[l, r], writes=[p6], name="qmm")
                qt = at_q[:, sc * SB:(sc + 1) * SB]
                sch.add("scalar", lambda e, o=qt, i=p6: e.activation(o, i, AF.Copy),
                        reads=[p6], writes=[qt], name="qevac")
            rotary(at_q, at_qr, rotA_q, rotB_q, "q")
            sch.add("scalar", lambda e, o=at_sq[:], i=at_qr[:]: e.activation(o, i, AF.Square),
                    reads=[at_qr[:]], writes=[at_sq[:]], name="qsq")
            pairs = [tuple(x for x in (a, a + 1) if x < NSC) for a in range(0, NSC, 2)]
            for pair in pairs:
                ncrows = [rows_bf[0:1, 0:SB], rows_bf[0:1, 512:512 + SB]]
                for pi, ib in enumerate(pair):
                    p4r = pool[4][0:1, 0:SB]
                    r = at_sq[:, ib * SB:(ib + 1) * SB]
                    sch.add("tensor", lambda e, o=p4r, r=r:
                            e.matmul(o, ones_bf[:], r, start=True, stop=True),
                            reads=[r, ones_bf[:]], writes=[p4r], name="qn2")
                    nrow = at_rows2[0:1, 0:SB]
                    sch.add("scalar", lambda e, o=nrow, i=p4r: e.activation(o, i, AF.Sqrt),
                            reads=[p4r], writes=[nrow], name="qn")
                    sch.add("vector", lambda e, o=nrow, i=nrow, s=km:
                            e.tensor_scalar(o, i, s, None, ALU.mult),
                            reads=[nrow, km], writes=[nrow], name="negc_row")
                    sch.add("vector", lambda e, o=ncrows[pi], i=nrow: e.tensor_copy(o, i),
                            reads=[nrow], writes=[ncrows[pi]], name="negc_bfc")
                lrows = [pool[2][0:1, 0:SB], pool[4][0:1, 0:SB]]
                OBs = [pool[3][:, 0:SB], pool[5][:, 0:SB]]
                jmaxs = [(ib + 1) * (SB // 128) for ib in pair]

                SBK = (0, 1, 6, 7)

                def emit_sexp0(pi, jc, slot, pair=pair, ncrows=ncrows):
                    ib = pair[pi]
                    sb_ = slot % 4
                    SBANK = pool[SBK[sb_]][:, 0:SB]
                    ncr = ncrows[pi]
                    sch.add("tensor", lambda e, o=SBANK, r=ncr:
                            e.matmul(o, ones_row_bf[:], r, start=True, stop=False,
                                     skip_group_check=True),
                            reads=[ncr, ones_row_bf[:]], writes=[SBANK], name="sr1")
                    l = at_k[:, jc * 128:(jc + 1) * 128]
                    r = at_qr[:, ib * SB:(ib + 1) * SB]
                    sch.add("tensor", lambda e, o=SBANK, l=l, r=r:
                            e.matmul(o, l, r, start=False, stop=True,
                                     skip_group_check=True),
                            reads=[l, r], writes=[SBANK], name="smm")
                    pt = at_pt[:, (slot % 4) * SB:(slot % 4 + 1) * SB]
                    diag_k = jc - ib * (SB // 128)
                    if diag_k >= 0:
                        sh = at_sh[:, 0:SB]
                        mk = mask_k(diag_k)
                        sch.add("vector", lambda e, o=sh, a=SBANK, b=mk: e.tensor_add(o, a, b),
                                reads=[SBANK, mk], writes=[sh], name="smask")
                        sch.add("scalar", lambda e, o=pt, i=sh: e.activation(o, i, AF.Exp),
                                reads=[sh], writes=[pt], name="sexp")
                    else:
                        sch.add("scalar", lambda e, o=pt, i=SBANK: e.activation(o, i, AF.Exp),
                                reads=[SBANK], writes=[pt], name="sexp")

                def emit_lo0(pi, jc, slot, jmaxs=jmaxs, lrows=lrows, OBs=OBs):
                    pt = at_pt[:, (slot % 4) * SB:(slot % 4 + 1) * SB]
                    lr_ = lrows[pi]
                    sch.add("tensor", lambda e, o=lr_, r=pt, st=(jc == 0),
                            sp=(jc == jmaxs[pi] - 1):
                            e.matmul(o, ones_bf[:], r, start=st, stop=sp,
                                     skip_group_check=True),
                            reads=[pt, ones_bf[:]], writes=[lr_], name="lmm")
                    p3o = OBs[pi]
                    lv = at_v[:, jc * 128:(jc + 1) * 128]
                    sch.add("tensor", lambda e, o=p3o, l=lv, r=pt, st=(jc == 0),
                            sp=(jc == jmaxs[pi] - 1):
                            e.matmul(o, l, r, start=st, stop=sp, skip_group_check=True),
                            reads=[pt, lv], writes=[p3o], name="omm")

                items = []
                mx = max(jmaxs)
                for jc in range(mx):
                    for pi in range(len(pair)):
                        if jc < jmaxs[pi]:
                            items.append((pi, jc))
                for s in range(len(items) + 3):
                    if s >= 3:
                        emit_lo0(items[s - 3][0], items[s - 3][1], s - 3)
                    if s < len(items):
                        emit_sexp0(items[s][0], items[s][1], s)
                for pi, ib in enumerate(pair):
                    rrow = at_rows2[0:1, 0:SB]
                    sch.add("vector", lambda e, o=rrow, i=lrows[pi]:
                            e.tensor_scalar(o, i, 1e-30, None, ALU.add),
                            reads=[lrows[pi]], writes=[rrow], name="laddeps")
                    sch.add("vector", lambda e, o=rrow, i=rrow: e.reciprocal(o, i),
                            reads=[rrow], writes=[rrow], name="lrecip")
                    p6b = pool[6][:, 0:SB]
                    sch.add("tensor", lambda e, o=p6b, r=rrow:
                            e.matmul(o, ones_row[:], r, start=True, stop=True),
                            reads=[rrow, ones_row[:]], writes=[p6b], name="rbc")
                    rbv = at_rb[:, 0:SB]
                    sch.add("vector", lambda e, o=rbv, i=p6b: e.tensor_copy(o, i),
                            reads=[p6b], writes=[rbv], name="rbsb")
                    ho = heads[:, n * S + ib * SB: n * S + (ib + 1) * SB]
                    sch.add("vector", lambda e, o=ho, a=OBs[pi], b=rbv: e.tensor_mul(o, a, b),
                            reads=[OBs[pi], rbv], writes=[ho], name="oscale")

        sT = core.dtmp("sT", [D, S], F32)
        for dc in range(DC):
            wb = dc % 2
            wslab = at_wslab[:, wb * 2048:(wb + 1) * 2048]
            sch.dma("gpsimd", wslab.rearrange("p (c n) -> p c n", n=128),
                    wp_t[dc].rearrange("(c p) n -> p c n", p=128), name=f"wp{dc}")
            for sc in range(NSC):
                bank = pool[5 + sc % 2][:, 0:SB]
                for oc in range(NH):
                    l = wslab[:, oc * 128:(oc + 1) * 128]
                    r = heads[:, oc * S + sc * SB: oc * S + (sc + 1) * SB]
                    sch.add("tensor", lambda e, o=bank, l=l, r=r, st=(oc == 0), sp=(oc == NH - 1):
                            e.matmul(o, l, r, start=st, stop=sp),
                            reads=[l, r], writes=[bank], name="pmm")
                eb = sc % 2
                ev = at_sh[:, eb * SB:(eb + 1) * SB]
                bc = b_col[:, dc:dc + 1]
                sch.add("scalar", lambda e, o=ev, i=bank, b=bc: e.add(o, i, b),
                        reads=[bank, bc], writes=[ev], name="pevac")
                sch.dma("sync", sT[dc * 128:(dc + 1) * 128, sc * SB:(sc + 1) * SB], ev,
                        name=f"sTw{dc}_{sc}")

    def emit_moe0():
        wg = core.din("L0_ffn_W_g", [D, 8], F32)
        sch.dma("gpsimd", wg_sb[:].rearrange("p (c n) -> p c n", n=8),
                wg.rearrange("(c p) n -> p c n", p=128), name="wg")
        gwT = core.dtmp("gw8", [8, S], F32)
        xT = core.dram["xT"]
        sTd = core.dram["sT"]
        x1T = core.dtmp("x1T", [D, S], F32)
        for tc in range(S // 128):
            p0 = pool[0][:, 0:8]
            for dc in range(DC):
                l = xn_ap(dc, s0=tc * 128, sl=128)
                r = wg_sb[:, dc * 8:(dc + 1) * 8]
                sch.add("tensor", lambda e, o=p0, l=l, r=r, st=(dc == 0), sp=(dc == DC - 1):
                        e.matmul(o, l, r, start=st, stop=sp),
                        reads=[l, r], writes=[p0], name="gmm")
            t = [gt_t[:, k * 8:(k + 1) * 8] for k in range(6)]
            ga = t[5]
            r4 = gt_r
            sch.add("vector", lambda e, o=ga, i=p0: e.tensor_copy(o, i),
                    reads=[p0], writes=[ga], name="gcp")
            sch.add("vector", lambda e, o=r4[:, 0:1], i=ga:
                    e.reduce_max(o, i, AX.X, negate=True),
                    reads=[ga], writes=[r4[:, 0:1]], name="gmax1")
            sch.add("scalar", lambda e, o=t[0], i=ga, b=r4[:, 0:1]:
                    e.activation(o, i, AF.Exp, bias=b),
                    reads=[ga, r4[:, 0:1]], writes=[t[0]], name="gexp")
            sch.add("vector", lambda e, o=r4[:, 1:2], i=t[0]:
                    e.tensor_reduce(o, i, AX.X, ALU.add),
                    reads=[t[0]], writes=[r4[:, 1:2]], name="gsum")
            sch.add("vector", lambda e, o=r4[:, 1:2], i=r4[:, 1:2]: e.reciprocal(o, i),
                    reads=[r4[:, 1:2]], writes=[r4[:, 1:2]], name="grec")
            sch.add("vector", lambda e, o=t[1], i=ga: e.tensor_scalar(o, i, -1.0, None, ALU.mult),
                    reads=[ga], writes=[t[1]], name="gneg")
            sch.add("vector", lambda e, o=t[2], i=t[1], s=r4[:, 0:1]:
                    e.tensor_scalar(o, i, s, None, ALU.is_equal),
                    reads=[t[1], r4[:, 0:1]], writes=[t[2]], name="geq1")
            sch.add("vector", lambda e, o=t[3], a=t[2], b=iota8[:]: e.tensor_mul(o, a, b),
                    reads=[t[2], iota8[:]], writes=[t[3]], name="gkey")
            sch.add("vector", lambda e, o=t[3], a=t[2], b=t[3]:
                    e.scalar_tensor_tensor(o, a, -8.0, b, ALU.mult, ALU.add),
                    reads=[t[2], t[3]], writes=[t[3]], name="gkey2")
            sch.add("vector", lambda e, o=t[3], i=t[3]: e.tensor_scalar(o, i, 8.0, None, ALU.add),
                    reads=[t[3]], writes=[t[3]], name="gkey3")
            sch.add("vector", lambda e, o=r4[:, 2:3], i=t[3]:
                    e.tensor_reduce(o, i, AX.X, ALU.min),
                    reads=[t[3]], writes=[r4[:, 2:3]], name="gi1")
            sch.add("vector", lambda e, o=t[2], i=iota8[:], s=r4[:, 2:3]:
                    e.tensor_scalar(o, i, s, None, ALU.is_equal),
                    reads=[iota8[:], r4[:, 2:3]], writes=[t[2]], name="gsel1")
            sch.add("vector", lambda e, o=t[4], a=t[2], b=ga:
                    e.scalar_tensor_tensor(o, a, -1e9, b, ALU.mult, ALU.add),
                    reads=[t[2], ga], writes=[t[4]], name="gmask")
            sch.add("vector", lambda e, o=r4[:, 3:4], i=t[4]:
                    e.reduce_max(o, i, AX.X, negate=True),
                    reads=[t[4]], writes=[r4[:, 3:4]], name="gmax2")
            sch.add("vector", lambda e, o=t[4], i=t[4]:
                    e.tensor_scalar(o, i, -1.0, None, ALU.mult),
                    reads=[t[4]], writes=[t[4]], name="gneg2")
            sch.add("vector", lambda e, o=t[1], i=t[4], s=r4[:, 3:4]:
                    e.tensor_scalar(o, i, s, None, ALU.is_equal),
                    reads=[t[4], r4[:, 3:4]], writes=[t[1]], name="geq2")
            sch.add("vector", lambda e, o=t[4], a=t[1], b=iota8[:]: e.tensor_mul(o, a, b),
                    reads=[t[1], iota8[:]], writes=[t[4]], name="gkey4")
            sch.add("vector", lambda e, o=t[4], a=t[1], b=t[4]:
                    e.scalar_tensor_tensor(o, a, -8.0, b, ALU.mult, ALU.add),
                    reads=[t[1], t[4]], writes=[t[4]], name="gkey5")
            sch.add("vector", lambda e, o=t[4], i=t[4]: e.tensor_scalar(o, i, 8.0, None, ALU.add),
                    reads=[t[4]], writes=[t[4]], name="gkey6")
            sch.add("vector", lambda e, o=r4[:, 2:3], i=t[4]:
                    e.tensor_reduce(o, i, AX.X, ALU.min),
                    reads=[t[4]], writes=[r4[:, 2:3]], name="gi2")
            sch.add("vector", lambda e, o=t[1], i=iota8[:], s=r4[:, 2:3]:
                    e.tensor_scalar(o, i, s, None, ALU.is_equal),
                    reads=[iota8[:], r4[:, 2:3]], writes=[t[1]], name="gsel2")
            sch.add("vector", lambda e, o=t[2], a=t[2], b=t[1]: e.tensor_add(o, a, b),
                    reads=[t[2], t[1]], writes=[t[2]], name="gselb")
            sch.add("vector", lambda e, o=t[0], a=t[0], b=t[2]: e.tensor_mul(o, a, b),
                    reads=[t[0], t[2]], writes=[t[0]], name="gwm")
            sch.add("vector", lambda e, o=gt_w[:], i=t[0], s=r4[:, 1:2]:
                    e.tensor_scalar(o, i, s, None, ALU.mult),
                    reads=[t[0], r4[:, 1:2]], writes=[gt_w[:]], name="gwf")
            p1t = pool[1][0:8, 0:128]
            sch.add("tensor", lambda e, o=p1t, i=gt_w[:]: e.transpose(o, i, ident[:]),
                    reads=[gt_w[:], ident[:]], writes=[pool[1][:, 0:128]], name="gtr")
            wo = wT8[0:8, tc * 128:(tc + 1) * 128]
            sch.add("vector", lambda e, o=wo, i=p1t: e.tensor_copy(o, i),
                    reads=[pool[1][:, 0:128]], writes=[wo], name="gtrev")
        sch.dma("sync", gwT, wT8[:], name="gw8")

        Wsg = core.din("ws_gate_t", [2, D, 1024], F32)
        Wsu = core.din("ws_up_t", [2, D, 1024], F32)
        Wsd = core.din("L0_ffn_Ws_down", [D, D], F32)
        Weg = core.din("L0_ffn_We_gate", [8, D, 1024], F32)
        Weu = core.din("L0_ffn_We_up", [8, D, 1024], F32)
        Wed = core.din("L0_ffn_We_down", [8, 1024, D], F32)
        SHW = min(512, SH)
        NB = SH // SHW
        HSLB = 8192

        def load_gu_half(mat_ap, g, buf, what):
            for q in range(2):
                dst = slabs[:, buf * HSLB + q * 4096: buf * HSLB + (q + 1) * 4096]
                dst = dst.rearrange("p (c n) -> p c n", n=512)
                srcv = mat_ap[q * 1024:(q + 1) * 1024, g * 512:(g + 1) * 512]
                srcv = srcv.rearrange("(c p) n -> p c n", p=128)
                sch.dma("gpsimd", dst, srcv, name=f"m{what}{g}{q}")

        def load_dn_half(mat_ap, g, buf, what):
            for q in range(2):
                dst = slabs[:, buf * HSLB + q * 4096: buf * HSLB + (q + 1) * 4096]
                dst = dst.rearrange("p (c n) -> p c n", n=1024)
                srcv = mat_ap[q * 512:(q + 1) * 512, g * 1024:(g + 1) * 1024]
                srcv = srcv.rearrange("(c p) n -> p c n", p=128)
                sch.dma("gpsimd", dst, srcv, name=f"md{what}{g}{q}")

        def moe_gu_half(kind, g, buf, s0, shared):
            for dc in range(DC):
                for eo_l in range(4):
                    l = slabs[:, buf * HSLB + dc * 512 + eo_l * 128:
                              buf * HSLB + dc * 512 + (eo_l + 1) * 128]
                    for sw in range(NB):
                        o = pool[eo_l * NB + sw][:, 0:SHW]
                        r = xn_ap(dc, s0=s0 + sw * SHW, sl=SHW)
                        sch.add("tensor",
                                lambda e, o=o, l=l, r=r, st=(dc == 0), sp=(dc == DC - 1):
                                e.matmul(o, l, r, start=st, stop=sp, skip_group_check=True),
                                reads=[l, r], writes=[o], name="moe" + kind)
            for eo_l in range(4):
                eo = 4 * g + eo_l
                for sw in range(NB):
                    ghs = gh[:, eo * SH + sw * SHW: eo * SH + (sw + 1) * SHW]
                    pu = pool[eo_l * NB + sw][:, 0:SHW]
                    if kind == "g":
                        sch.add("scalar", lambda e, o=ghs, i=pu: e.activation(o, i, AF.Silu),
                                reads=[pu], writes=[ghs], name="moesilu")
                    elif shared:
                        sch.add("vector", lambda e, o=ghs, a=ghs, b=pu: e.tensor_mul(o, a, b),
                                reads=[pu, ghs], writes=[ghs], name="moeh")
                    else:
                        hfv = hf[:, sw * SHW:(sw + 1) * SHW]
                        sch.add("vector", lambda e, o=hfv, a=ghs, b=pu: e.tensor_mul(o, a, b),
                                reads=[pu, ghs], writes=[hfv], name="moeh1")
                        wv2 = web[:, sw * SHW:(sw + 1) * SHW]
                        sch.add("vector", lambda e, o=ghs, a=hfv, b=wv2: e.tensor_mul(o, a, b),
                                reads=[hfv, wv2], writes=[ghs], name="moeh2")

        def moe_dn_half(g, buf, s0, first):
            for dc in range(8 * g, 8 * g + 8):
                d_l = dc - 8 * g
                bank = (dc % 2) * NB
                for eo in range(8):
                    l = slabs[:, buf * HSLB + eo * 1024 + d_l * 128:
                              buf * HSLB + eo * 1024 + (d_l + 1) * 128]
                    for sw in range(NB):
                        o = pool[bank + sw][:, 0:SHW]
                        r = gh[:, eo * SH + sw * SHW: eo * SH + (sw + 1) * SHW]
                        sch.add("tensor",
                                lambda e, o=o, l=l, r=r, st=(eo == 0), sp=(eo == 7):
                                e.matmul(o, l, r, start=st, stop=sp, skip_group_check=True),
                                reads=[l, r], writes=[o], name="moed")
                for sw in range(NB):
                    yv = y_moe[:, dc * SH + sw * SHW: dc * SH + (sw + 1) * SHW]
                    po = pool[bank + sw][:, 0:SHW]
                    if first:
                        sch.add("vector", lambda e, o=yv, i=po: e.tensor_copy(o, i),
                                reads=[po], writes=[yv], name="ycp")
                    else:
                        sch.add("vector", lambda e, o=yv, a=yv, b=po: e.tensor_add(o, a, b),
                                reads=[po, yv], writes=[yv], name="yadd")

        for half in range(2):
            s0 = half * SH
            tasks = []
            for pe in range(10):
                shared = pe < 2
                if shared:
                    gate_src, up_src = Wsg[pe], Wsu[pe]
                    down_src = Wsd[pe * 1024:(pe + 1) * 1024, :]
                else:
                    gate_src, up_src = Weg[pe - 2], Weu[pe - 2]
                    down_src = Wed[pe - 2]
                for g in range(2):
                    tasks.append(("g", gate_src, g, pe, shared))
                for g in range(2):
                    tasks.append(("u", up_src, g, pe, shared))
                for g in range(2):
                    tasks.append(("d", down_src, g, pe, shared))

            def load_task(i):
                kind, mat, g, pe, shared = tasks[i]
                if kind == "d":
                    load_dn_half(mat, g, i % 2, f"{pe}")
                else:
                    load_gu_half(mat, g, i % 2, f"{kind}{pe}")

            load_task(0)
            for i, (kind, mat, g, pe, shared) in enumerate(tasks):
                if pe >= 2 and kind == "g" and g == 0:
                    er = pe - 2
                    wv = web[:, 0:SH]
                    sch.dma("sync", wv,
                            gwT[er:er + 1, s0:s0 + SH].partition_broadcast(128),
                            name=f"web{er}_{half}")
                if i + 1 < len(tasks):
                    load_task(i + 1)
                if kind == "d":
                    moe_dn_half(g, i % 2, s0, pe == 0)
                else:
                    moe_gu_half(kind, g, i % 2, s0, shared)
            for dc in range(DC):
                for sw in range(NB):
                    eb = sw % 2
                    stile = ep_s[:, eb * 512:eb * 512 + SHW]
                    xtile = ep_x[:, eb * 512:eb * 512 + SHW]
                    gtile = ep_g[:, eb * 512:eb * 512 + SHW]
                    sch.dma("sync", stile,
                            sTd[dc * 128:(dc + 1) * 128, s0 + sw * SHW:s0 + (sw + 1) * SHW],
                            name="eplds")
                    sch.dma("sync", xtile,
                            xT[dc * 128:(dc + 1) * 128, s0 + sw * SHW:s0 + (sw + 1) * SHW],
                            name="epldx")
                    yv = y_moe[:, dc * SH + sw * SHW: dc * SH + (sw + 1) * SHW]
                    sch.add("scalar", lambda e, o=gtile, i=yv: e.activation(o, i, AF.Silu),
                            reads=[yv], writes=[gtile], name="epsilu")
                    sch.add("vector", lambda e, o=gtile, a=gtile, b=stile: e.tensor_mul(o, a, b),
                            reads=[gtile, stile], writes=[gtile], name="epmul")
                    sch.add("vector", lambda e, o=xtile, a=xtile, b=gtile: e.tensor_add(o, a, b),
                            reads=[gtile, xtile], writes=[xtile], name="epadd")
                    sch.dma("sync",
                            x1T[dc * 128:(dc + 1) * 128, s0 + sw * SHW:s0 + (sw + 1) * SHW],
                            xtile, name="epst")

    def emit_l1():
        pool_bf = core.pool_bf
        emit_ln("x1T", "ln1")
        wq_t1 = core.din("wq1_t", [NH, D, 128], F32)
        wk_t1 = core.din("wk1_t", [NH, D, 128], F32)
        wv_t1 = core.din("wv1_t", [8, D, 256], F32)
        wo_t1 = core.din("wo1_t", [DC, D, 128], F32)
        lam_d = core.din("lam_row", [1, 16], F32)
        sTd = core.dram["sT"]
        x1T = core.dram["x1T"]
        outT = core.dout("outT", [D, S], F32)

        lr = lam_sb[0:1, 0:16]
        sch.dma("sync", lr, lam_d, name="lam")
        sch.add("scalar", lambda e, o=lr, i=lr: e.activation(o, i, AF.Sigmoid),
                reads=[lr], writes=[lr], name="lamsig")
        sch.add("vector", lambda e, o=lr, i=lr: e.tensor_scalar(o, i, -1.0, None, ALU.mult),
                reads=[lr], writes=[lr], name="lamneg")
        pl = pool[7][:, 0:16]
        sch.add("tensor", lambda e, o=pl, r=lr: e.matmul(o, ones_row[:], r, start=True, stop=True),
                reads=[lr, ones_row[:]], writes=[pool[7][:, 0:512]], name="lambc")
        nlam = lam_sb[:, 16:32]
        sch.add("vector", lambda e, o=nlam, i=pl: e.tensor_copy(o, i),
                reads=[pool[7][:, 0:512]], writes=[nlam], name="lamsb")

        for pr in range(NH // 2):
            vslab = at_wslab[:, 0:4096]
            sch.dma("gpsimd", vslab.rearrange("p (c n) -> p c n", n=256),
                    wv_t1[pr].rearrange("(c p) n -> p c n", p=128), name=f"wv{pr}")
            for jc in range(JC):
                pv = pool[6 + jc % 2][:, 0:256]
                for dc in range(DC):
                    l = xn_ap(dc, s0=jc * 128, sl=128)
                    r = vslab[:, dc * 256:(dc + 1) * 256]
                    sch.add("tensor", lambda e, o=pv, l=l, r=r, st=(dc == 0), sp=(dc == DC - 1):
                            e.matmul(o, l, r, start=st, stop=sp, skip_group_check=True),
                            reads=[l, r], writes=[pv], name="v1mm")
                vv = l1_v2[:, jc * 256:(jc + 1) * 256]
                sch.add("vector", lambda e, o=vv, i=pv: e.tensor_copy(o, i),
                        reads=[pv], writes=[vv], name="v1evac")
            for hp in range(2):
                n = 2 * pr + hp
                for wi, (wt, dst) in enumerate(((wq_t1, at_qr), (wk_t1, at_k))):
                    slab = at_wslab[:, wi * 2048:(wi + 1) * 2048]
                    sch.dma("gpsimd", slab.rearrange("p (c n) -> p c n", n=128),
                            wt[n].rearrange("(c p) n -> p c n", p=128), name=f"w1_{n}_{wi}")
                    for sc in range(NSC):
                        p6 = pool[6][:, 0:SB]
                        for dc in range(DC):
                            l = slab[:, dc * 128:(dc + 1) * 128]
                            r = xn_ap(dc, sc)
                            sch.add("tensor",
                                    lambda e, o=p6, l=l, r=r, st=(dc == 0), sp=(dc == DC - 1):
                                    e.matmul(o, l, r, start=st, stop=sp),
                                    reads=[l, r], writes=[p6], name="qk1mm")
                        dv = dst[:, sc * SB:(sc + 1) * SB]
                        if wi == 0:
                            sch.add("scalar", lambda e, o=dv, i=p6:
                                    e.activation(o, i, AF.Copy, scale=0.125),
                                    reads=[p6], writes=[dv], name="q1evac")
                        else:
                            sch.add("scalar", lambda e, o=dv, i=p6: e.activation(o, i, AF.Copy),
                                    reads=[p6], writes=[dv], name="k1evac")
                sqc = at_pt[:, 0:512]
                km2 = [srow_sb[0:1, 8 + 2 * h:9 + 2 * h] for h in range(2)]
                tmp = srow_sb[0:1, 12:13]
                for h in range(2):
                    P = slice(0, 64) if h == 0 else slice(64, 128)
                    acc = km2[h]
                    for sc in range(NSC):
                        ksl = at_k[P, sc * SB:(sc + 1) * SB]
                        sqv = sqc[P, 0:SB]
                        sch.add("scalar", lambda e, o=sqv, i=ksl: e.activation(o, i, AF.Square),
                                reads=[ksl], writes=[sqv], name="k1sq")
                        p4 = pool[4][0:1, 0:SB]
                        r = sqv
                        lw = ones_bf[P, 0:1]
                        sch.add("tensor", lambda e, o=p4, l=lw, r=r:
                                e.matmul(o, l, r, start=True, stop=True),
                                reads=[r, lw], writes=[p4], name="kn21")
                        if sc == 0:
                            sch.add("vector", lambda e, o=acc, i=p4: e.reduce_max(o, i, AX.X),
                                    reads=[p4], writes=[acc], name="km1")
                        else:
                            sch.add("vector", lambda e, o=tmp, i=p4: e.reduce_max(o, i, AX.X),
                                    reads=[p4], writes=[tmp], name="km1b")
                            sch.add("vector", lambda e, o=acc, a=acc, b=tmp:
                                    e.tensor_tensor(o, a, b, ALU.max),
                                    reads=[acc, tmp], writes=[acc], name="km1c")
                    sch.add("scalar", lambda e, o=acc, i=acc: e.activation(o, i, AF.Sqrt),
                            reads=[acc], writes=[acc], name="km1sqrt")
                    sch.add("vector", lambda e, o=acc, i=acc:
                            e.tensor_scalar(o, i, -1.0, None, ALU.mult),
                            reads=[acc], writes=[acc], name="km1neg")
                for ib in range(NSC):
                    jmax = (ib + 1) * (SB // 128)
                    combs = []
                    ncrows = [rows_bf[0:1, 0:SB], rows_bf[0:1, 512:512 + SB]]
                    for h in range(2):
                        P = slice(0, 64) if h == 0 else slice(64, 128)
                        qsl = at_qr[P, ib * SB:(ib + 1) * SB]
                        sqv = sqc[P, 0:SB]
                        sch.add("scalar", lambda e, o=sqv, i=qsl: e.activation(o, i, AF.Square),
                                reads=[qsl], writes=[sqv], name="q1sq")
                        p4r = pool[4][0:1, 0:SB]
                        r = sqv
                        lw = ones_bf[P, 0:1]
                        sch.add("tensor", lambda e, o=p4r, l=lw, r=r:
                                e.matmul(o, l, r, start=True, stop=True),
                                reads=[r, lw], writes=[p4r], name="qn21")
                        nrow = at_rows2[0:1, 0:SB]
                        sch.add("scalar", lambda e, o=nrow, i=p4r: e.activation(o, i, AF.Sqrt),
                                reads=[p4r], writes=[nrow], name="qn1")
                        sch.add("vector", lambda e, o=nrow, i=nrow, s=km2[h]:
                                e.tensor_scalar(o, i, s, None, ALU.mult),
                                reads=[nrow, km2[h]], writes=[nrow], name="negc1r")
                        sch.add("vector", lambda e, o=ncrows[h], i=nrow: e.tensor_copy(o, i),
                                reads=[nrow], writes=[ncrows[h]], name="negc1c")
                    lrows = [pool[2][0:1, 0:SB], pool[4][0:1, 0:SB]]
                    OBs = [pool[3][:, 0:SB], pool[5][:, 0:SB]]
                    Ps = [slice(0, 64), slice(64, 128)]

                    SBK = (0, 1, 6, 7)

                    def emit_sexp1(h, jc, slot, ib=ib):
                        sb_ = slot % 4
                        SBANK = pool[SBK[sb_]][:, 0:SB]
                        ncr = ncrows[h]
                        sch.add("tensor", lambda e, o=SBANK, r=ncr:
                                e.matmul(o, ones_row_bf[:], r, start=True, stop=False,
                                         skip_group_check=True),
                                reads=[ncr, ones_row_bf[:]], writes=[SBANK], name="s1r1")
                        l = at_k[Ps[h], jc * 128:(jc + 1) * 128]
                        rq = at_qr[Ps[h], ib * SB:(ib + 1) * SB]
                        sch.add("tensor", lambda e, o=SBANK, l=l, r=rq:
                                e.matmul(o, l, r, start=False, stop=True,
                                         skip_group_check=True),
                                reads=[l, rq], writes=[SBANK], name="s1mm")
                        pt = at_pt[:, (slot % 4) * SB:(slot % 4 + 1) * SB]
                        diag_k = jc - ib * (SB // 128)
                        if diag_k >= 0:
                            sh = at_sh[:, 0:SB]
                            mk = mask_k(diag_k)
                            sch.add("vector", lambda e, o=sh, a=SBANK, b=mk:
                                    e.tensor_add(o, a, b),
                                    reads=[SBANK, mk], writes=[sh], name="s1mask")
                            sch.add("scalar", lambda e, o=pt, i=sh: e.activation(o, i, AF.Exp),
                                    reads=[sh], writes=[pt], name="s1exp")
                        else:
                            sch.add("scalar", lambda e, o=pt, i=SBANK: e.activation(o, i, AF.Exp),
                                    reads=[SBANK], writes=[pt], name="s1exp")

                    def emit_lo1(h, jc, slot, jmax=jmax, hp=hp):
                        pt = at_pt[:, (slot % 4) * SB:(slot % 4 + 1) * SB]
                        lr_ = lrows[h]
                        sch.add("tensor", lambda e, o=lr_, r=pt, st=(jc == 0),
                                sp=(jc == jmax - 1):
                                e.matmul(o, ones_bf[:], r, start=st, stop=sp,
                                         skip_group_check=True),
                                reads=[pt, ones_bf[:]], writes=[lr_], name="l1mm")
                        lv = l1_v2[:, jc * 256 + hp * 128: jc * 256 + hp * 128 + 128]
                        OB = OBs[h]
                        sch.add("tensor", lambda e, o=OB, l=lv, r=pt, st=(jc == 0),
                                sp=(jc == jmax - 1):
                                e.matmul(o, l, r, start=st, stop=sp, skip_group_check=True),
                                reads=[pt, lv], writes=[OB], name="o1mm")

                    items = []
                    for jc in range(jmax):
                        items.append((0, jc))
                        items.append((1, jc))
                    for s in range(len(items) + 3):
                        if s >= 3:
                            emit_lo1(items[s - 3][0], items[s - 3][1], s - 3)
                        if s < len(items):
                            emit_sexp1(items[s][0], items[s][1], s)
                    for h in range(2):
                        rrow = at_rows2[0:1, 0:SB]
                        sch.add("vector", lambda e, o=rrow, i=lrows[h]:
                                e.tensor_scalar(o, i, 1e-30, None, ALU.add),
                                reads=[lrows[h]], writes=[rrow], name="l1eps")
                        sch.add("vector", lambda e, o=rrow, i=rrow: e.reciprocal(o, i),
                                reads=[rrow], writes=[rrow], name="l1rec")
                        p6b = pool[6][:, 0:SB]
                        sch.add("tensor", lambda e, o=p6b, r=rrow:
                                e.matmul(o, ones_row[:], r, start=True, stop=True),
                                reads=[rrow, ones_row[:]], writes=[p6b], name="r1bc")
                        rbv = at_rb[:, 0:SB]
                        sch.add("vector", lambda e, o=rbv, i=p6b: e.tensor_copy(o, i),
                                reads=[p6b], writes=[rbv], name="r1sb")
                        tc = (l1_tc1 if h == 0 else at_negc)[:, 0:SB]
                        sch.add("vector", lambda e, o=tc, a=OBs[h], b=rbv:
                                e.tensor_mul(o, a, b),
                                reads=[OBs[h], rbv], writes=[tc], name="o1s")
                        combs.append(tc)
                    ho = heads[:, n * S + ib * SB: n * S + (ib + 1) * SB]
                    nl = nlam[:, n:n + 1]
                    sch.add("vector", lambda e, o=ho, a=combs[1], s=nl, b=combs[0]:
                            e.scalar_tensor_tensor(o, a, s, b, ALU.mult, ALU.add),
                            reads=[combs[0], combs[1], nl], writes=[ho], name="dcomb")

        for dc in range(DC):
            wb = dc % 2
            wslab = at_wslab[:, wb * 2048:(wb + 1) * 2048]
            sch.dma("gpsimd", wslab.rearrange("p (c n) -> p c n", n=128),
                    wo_t1[dc].rearrange("(c p) n -> p c n", p=128), name=f"wo{dc}")
            for sc in range(NSC):
                bank = pool[5 + sc % 2][:, 0:SB]
                for oc in range(NH):
                    l = wslab[:, oc * 128:(oc + 1) * 128]
                    r = heads[:, oc * S + sc * SB: oc * S + (sc + 1) * SB]
                    sch.add("tensor", lambda e, o=bank, l=l, r=r, st=(oc == 0), sp=(oc == NH - 1):
                            e.matmul(o, l, r, start=st, stop=sp),
                            reads=[l, r], writes=[bank], name="womm")
                eb = sc % 2
                ev = at_sh[:, eb * SB:(eb + 1) * SB]
                sch.add("scalar", lambda e, o=ev, i=bank: e.activation(o, i, AF.Copy),
                        reads=[bank], writes=[ev], name="woevac")
                sch.dma("sync", sTd[dc * 128:(dc + 1) * 128, sc * SB:(sc + 1) * SB], ev,
                        name=f"s1Tw{dc}_{sc}")

        wgu = core.din("L1_ffn_w_gate_up", [16384, D], F32)
        wp1 = core.din("wproj1_t", [16, D, 512], F32)
        h_d = core.dtmp("h_l1", [8192, S], BF16)
        HC = 64
        SW = min(512, S)
        NSW = S // SW

        def quant_prefix(src_wb, g_buf, l_buf, scf, scb, G, tag):
            ng = G // 64
            s32a = scf[:, 0:min(ng, 32)]
            v3 = src_wb.rearrange("p (g k) -> p g k", k=64)
            if ng <= 32:
                sch.add("vector", lambda e, o=s32a, i=v3:
                        e.tensor_reduce(o, i, AX.X, ALU.add, apply_absolute_value=True),
                        reads=[src_wb], writes=[s32a], name=tag + "qs")
                sch.add("vector", lambda e, o=s32a, i=s32a:
                        e.tensor_scalar(o, i, 1.0 / 64, 1e-8, ALU.mult, ALU.max),
                        reads=[s32a], writes=[s32a], name=tag + "qc")
                sbf = scb[:, 0:ng]
                sch.add("vector", lambda e, o=sbf, i=s32a: e.tensor_copy(o, i),
                        reads=[s32a], writes=[sbf], name=tag + "qb")
                th = scf[:, 32:32 + ng]
                sch.add("vector", lambda e, o=th, i=sbf:
                        e.tensor_scalar(o, i, 0.501953125, None, ALU.mult),
                        reads=[sbf], writes=[th], name=tag + "qt")
                nth = scf[:, 64:64 + ng]
                sch.add("vector", lambda e, o=nth, i=th:
                        e.tensor_scalar(o, i, -1.0, None, ALU.mult),
                        reads=[th], writes=[nth], name=tag + "qn")
            else:
                raise NotImplementedError
            g_ = g_buf[:, 0:G]
            l_ = l_buf[:, 0:G]
            thb = th.unsqueeze(2).to_broadcast([128, ng, 64])
            nthb = nth.unsqueeze(2).to_broadcast([128, ng, 64])
            sch.add("vector", lambda e, o=g_.rearrange("p (g k) -> p g k", k=64), a=v3, b=thb:
                    e.tensor_tensor(o, a, b, ALU.is_gt),
                    reads=[src_wb, th], writes=[g_], name=tag + "qg")
            sch.add("vector", lambda e, o=l_.rearrange("p (g k) -> p g k", k=64), a=v3, b=nthb:
                    e.tensor_tensor(o, a, b, ALU.is_lt),
                    reads=[src_wb, nth], writes=[l_], name=tag + "ql")
            sch.add("vector", lambda e, o=g_, a=g_, b=l_: e.tensor_sub(o, a, b),
                    reads=[g_, l_], writes=[g_], name=tag + "qd")

        def quant_final(dst_wh, g_buf, scb, G, tag):
            ng = G // 64
            sbf = scb[:, 0:ng]
            g_ = g_buf[:, 0:G]
            sbb = sbf.unsqueeze(2).to_broadcast([128, ng, 64])
            sch.add("vector", lambda e, o=dst_wh.rearrange("p (g k) -> p g k", k=64),
                    a=g_.rearrange("p (g k) -> p g k", k=64), b=sbb:
                    e.tensor_tensor(o, a, b, ALU.mult),
                    reads=[g_, sbf], writes=[dst_wh], name=tag + "qw")

        def quant(dst_wh, src_wb, g_buf, l_buf, scf, scb, G, tag):
            quant_prefix(src_wb, g_buf, l_buf, scf, scb, G, tag)
            quant_final(dst_wh, g_buf, scb, G, tag)

        def transpose_tiles(src_bf, dst_bf, tiles, tag, evac_view=None,
                            psum_perm=False):
            for gi, i in enumerate(range(0, len(tiles), 8)):
                bk = 5 if gi % 2 == 0 else 7
                grp = tiles[i:i + 8]
                for j, (scol, dcol) in enumerate(grp):
                    o = pool_bf[bk][:, j * 128:(j + 1) * 128]
                    iv = src_bf[:, scol:scol + 128]
                    sch.add("tensor", lambda e, o=o, i=iv: e.transpose(o, i, identb[:]),
                            reads=[iv, identb[:]], writes=[pool[bk][:, 0:512]],
                            name=tag + "tr")
                if evac_view is not None:
                    ov = evac_view(i)
                    pv = pool_bf[bk][:, 0:128 * len(grp)].rearrange(
                        "p (t j n) -> p t j n", t=2, n=128)
                else:
                    ov = dst_bf[:, grp[0][1]: grp[0][1] + 128 * len(grp)]
                    pv = pool_bf[bk][:, 0:128 * len(grp)]
                sch.add("scalar", lambda e, o=ov, i=pv: e.copy(o, i),
                        reads=[pool[bk][:, 0:512]], writes=[ov], name=tag + "te")

        def gu_load(hc):
            wb = (f1_wbA, f1_wbB)[hc % 2]
            sch.dma("gpsimd", wb[:, 0:2048], wgu[hc * 128:(hc + 1) * 128, :],
                    name=f"gw{hc}")
            sch.dma("gpsimd", wb[:, 2048:4096],
                    wgu[8192 + hc * 128: 8192 + (hc + 1) * 128, :], name=f"uw{hc}")

        def gu_q_pre(hc, part):
            wb = (f1_wbA, f1_wbB)[hc % 2]
            off = part * 2048
            quant_prefix(wb[:, off:off + 2048], f1_glb, l_r3b, f1_sc, f1_scb,
                         2048, "gu"[part])

        def gu_q_fin_tr(hc, part):
            off = part * 2048
            tag = "gu"[part]
            quant_final(f1_wh[:, off:off + 2048], f1_glb, f1_scb, 2048, tag)
            transpose_tiles(f1_wh, f1_wh,
                            [(off + i * 128, off + i * 128) for i in range(16)], tag)

        def gu_mm_half(hc, half):
            off = half * 2048
            for dc in range(DC):
                l = f1_wh[:, off + dc * 128: off + (dc + 1) * 128]
                for sw in range(NSW):
                    o = pool[sw][:, 0:SW]
                    r = xn[:, dc * S + sw * SW: dc * S + (sw + 1) * SW]
                    sch.add("tensor",
                            lambda e, o=o, l=l, r=r, st=(dc == 0), sp=(dc == DC - 1):
                            e.matmul(o, l, r, start=st, stop=sp, skip_group_check=True),
                            reads=[l, r], writes=[o], name="gu1mm")
            if half == 0:
                for sw in range(NSW):
                    o = f1_t1[:, sw * SW:(sw + 1) * SW]
                    i = pool[sw][:, 0:SW]
                    sch.add("scalar", lambda e, o=o, i=i: e.activation(o, i, AF.Silu),
                            reads=[i], writes=[o], name="gu1silu")
            else:
                for sw in range(NSW):
                    o = f1_hq[:, sw * SW:(sw + 1) * SW]
                    a = f1_t1[:, sw * SW:(sw + 1) * SW]
                    b = pool[sw][:, 0:SW]
                    sch.add("vector", lambda e, o=o, a=a, b=b: e.tensor_mul(o, a, b),
                            reads=[a, b], writes=[o], name="gu1h")

        def gu_tail(hc):
            sch.dma("sync", h_d[hc * 128:(hc + 1) * 128, :], f1_hq[:, 0:S], name=f"hst{hc}")
            hsq = f1_hq[:, 0:S]
            sch.add("scalar", lambda e, o=hsq, i=hsq: e.activation(o, i, AF.Square),
                    reads=[hsq], writes=[hsq], name="h1sq")
            for sw in range(NSW):
                p6r = pool[6][0:1, 0:SW]
                r = hsq[:, sw * SW:(sw + 1) * SW]
                sch.add("tensor", lambda e, o=p6r, r=r:
                        e.matmul(o, ones_bf[:], r, start=True, stop=True),
                        reads=[r, ones_bf[:]], writes=[p6r], name="st1mm")
                dstr = rms_row[0:1, sw * SW:(sw + 1) * SW]
                if hc == 0:
                    sch.add("vector", lambda e, o=dstr, i=p6r: e.tensor_copy(o, i),
                            reads=[p6r], writes=[dstr], name="st1cp")
                else:
                    sch.add("vector", lambda e, o=dstr, a=dstr, b=p6r: e.tensor_add(o, a, b),
                            reads=[p6r, dstr], writes=[dstr], name="st1ad")

        gu_load(0)
        gu_q_pre(0, 0)
        gu_q_fin_tr(0, 0)
        gu_q_pre(0, 1)
        gu_q_fin_tr(0, 1)
        for hc in range(HC):
            if hc + 1 < HC:
                gu_load(hc + 1)
                gu_q_pre(hc + 1, 0)
            gu_mm_half(hc, 0)
            if hc + 1 < HC:
                gu_q_fin_tr(hc + 1, 0)
            gu_mm_half(hc, 1)
            if hc + 1 < HC:
                gu_q_pre(hc + 1, 1)
                gu_q_fin_tr(hc + 1, 1)
            gu_tail(hc)
        rr = rms_row[0:1, 0:S]
        sch.add("vector", lambda e, o=rr, i=rr:
                e.tensor_scalar(o, i, 1.0 / 8192, None, ALU.mult),
                reads=[rr], writes=[rr], name="rmsm")
        sch.add("scalar", lambda e, o=rr, i=rr, b=ceps[0:1, 1:2]:
                e.activation(o, i, AF.Sqrt, bias=b),
                reads=[rr, ceps[0:1, 1:2]], writes=[rr], name="rmssq")
        sch.add("vector", lambda e, o=rr, i=rr: e.reciprocal(o, i),
                reads=[rr], writes=[rr], name="rmsrec")

        def dn_quant_raw(kg, g2):
            dst = dn_w[:].rearrange("p (c n) -> p c n", n=512)
            srcw = wp1[kg][g2 * 1024:(g2 + 1) * 1024, :].rearrange(
                "(c p) n -> p c n", p=128)
            sch.dma("gpsimd", dst, srcw, name=f"pw{kg}_{g2}")
            for t in range(8):
                dc = g2 * 8 + t
                quant(dn_w[:, t * 512:(t + 1) * 512], dn_w[:, t * 512:(t + 1) * 512],
                      dn_g, l_r3, dn_sc, dn_scb, 512, f"p{dc}")

        def dn_tr(kg, g2):
            tiles = []
            for t in range(8):
                dc = g2 * 8 + t
                for j in range(4):
                    tiles.append((t * 512 + j * 128, j * 2048 + dc * 128))
            pT4 = pT_sb[:].rearrange("p (j dc n) -> p dc j n", j=4, n=128)

            def dn_evac(i, g2=g2, pT4=pT4):
                dc0 = g2 * 8 + i // 4
                return pT4[:, dc0:dc0 + 2, :, :]
            transpose_tiles(dn_w, pT_sb, tiles, "p", evac_view=dn_evac,
                            psum_perm=True)

        dn_quant_raw(0, 0)
        dn_tr(0, 0)
        dn_quant_raw(0, 1)
        dn_tr(0, 1)
        for kg in range(16):
            for c in range(4):
                hcg = kg * 4 + c
                hb = (f1_wbA, f1_wbB)[c % 2][:, (c // 2) * 2048:(c // 2) * 2048 + S]
                sch.dma("sync", hb, h_d[hcg * 128:(hcg + 1) * 128, :], name=f"hld{hcg}")
                for sw in range(NSW):
                    p6b = pool[6][:, 0:SW]
                    irow = rms_row[0:1, sw * SW:(sw + 1) * SW]
                    sch.add("tensor", lambda e, o=p6b, r=irow:
                            e.matmul(o, ones_row[:], r, start=True, stop=True),
                            reads=[irow, ones_row[:]], writes=[p6b], name="invbc")
                    o = hn_sb[:, c * 2048 + sw * SW: c * 2048 + (sw + 1) * SW]
                    a = hb[:, sw * SW:(sw + 1) * SW]
                    sch.add("vector", lambda e, o=o, a=a, b=p6b: e.tensor_mul(o, a, b),
                            reads=[a, p6b], writes=[o], name="hn1")
            if kg + 1 < 16:
                dn_quant_raw(kg + 1, 0)
            for dc in range(DC):
                for swd in range(NSW):
                    bk = (dc * NSW + swd) % 4
                    for c in range(4):
                        l = pT_sb[:, c * 2048 + dc * 128: c * 2048 + (dc + 1) * 128]
                        o = pool[bk][:, 0:SW]
                        r = hn_sb[:, c * 2048 + swd * SW: c * 2048 + (swd + 1) * SW]
                        sch.add("tensor",
                                lambda e, o=o, l=l, r=r, st=(c == 0), sp=(c == 3):
                                e.matmul(o, l, r, start=st, stop=sp, skip_group_check=True),
                                reads=[l, r], writes=[o], name="dn1mm")
                    yv = y_l1[:, dc * S + swd * SW: dc * S + (swd + 1) * SW]
                    po = pool[bk][:, 0:SW]
                    if kg == 0:
                        sch.add("vector", lambda e, o=yv, i=po: e.tensor_copy(o, i),
                                reads=[po], writes=[yv], name="y1cp")
                    else:
                        sch.add("vector", lambda e, o=yv, a=yv, b=po: e.tensor_add(o, a, b),
                                reads=[po, yv], writes=[yv], name="y1ad")
            if kg + 1 < 16:
                dn_tr(kg + 1, 0)
                dn_quant_raw(kg + 1, 1)
                dn_tr(kg + 1, 1)

        for dc in range(DC):
            for sc in range(NSC):
                eb = sc % 2
                stile = ep_s[:, eb * 512:eb * 512 + SB]
                xtile = ep_x[:, eb * 512:eb * 512 + SB]
                gtile = ep_g[:, eb * 512:eb * 512 + SB]
                sch.dma("sync", stile, sTd[dc * 128:(dc + 1) * 128, sc * SB:(sc + 1) * SB],
                        name="ep1s")
                sch.dma("sync", xtile, x1T[dc * 128:(dc + 1) * 128, sc * SB:(sc + 1) * SB],
                        name="ep1x")
                yv = y_l1[:, dc * S + sc * SB: dc * S + (sc + 1) * SB]
                sch.add("scalar", lambda e, o=gtile, i=yv: e.activation(o, i, AF.Silu),
                        reads=[yv], writes=[gtile], name="ep1silu")
                sch.add("vector", lambda e, o=gtile, a=gtile, b=stile: e.tensor_mul(o, a, b),
                        reads=[gtile, stile], writes=[gtile], name="ep1mul")
                sch.add("vector", lambda e, o=xtile, a=xtile, b=gtile: e.tensor_add(o, a, b),
                        reads=[gtile, xtile], writes=[xtile], name="ep1add")
                sch.dma("sync", outT[dc * 128:(dc + 1) * 128, sc * SB:(sc + 1) * SB],
                        xtile, name="ep1st")

    xT = core.din("xT", [D, S], F32)
    if 0 in layers:
        emit_ln("xT", "ln0")
        if "xn" in dbg:
            o = core.dout("dbg_xn", [D, S], BF16)
            for dc in range(DC):
                sch.dma("sync", o[dc * 128:(dc + 1) * 128, :], xn[:, dc * S:(dc + 1) * S],
                        name="dbgxn")
        emit_attn0()
        emit_moe0()
    else:
        core.dram["x1T"] = core.din("x1T_in", [D, S], F32)
        core.dram["xT"] = core.dram["x1T"]

    if 1 in layers:
        emit_l1()
    else:
        x1o = core.dout("x1T_out", [D, S], F32)
        for dc in range(DC):
            sch.dma("sync", x1o[dc * 128:(dc + 1) * 128, :],
                    core.dram["x1T"][dc * 128:(dc + 1) * 128, :], name="x1out")

    if "sT" in dbg:
        o = core.dout("dbg_sT", [D, S], F32)
        for dc in range(DC):
            sch.dma("sync", o[dc * 128:(dc + 1) * 128, :],
                    core.dram["sT"][dc * 128:(dc + 1) * 128, :], name="dbgsT")

    with core.nc.Block() as blk:
        sch.emit(blk)
    return core


def prep_core_inputs(inputs, b, S, layers=(0, 1)):
    c = host_constants(S, min(512, S))
    m = dict(c)
    x = np.asarray(inputs["L0_attn_hidden_states"])[b][:S]
    m["xT"] = np.ascontiguousarray(x.T)
    if 0 in layers:
        wq = np.asarray(inputs["L0_attn_w_q"])
        m["wqT_t"] = np.ascontiguousarray(wq.T.reshape(D, NH, HD).transpose(1, 0, 2))
        wkv = np.asarray(inputs["L0_attn_w_kv"])
        m["wkvT_t"] = np.ascontiguousarray(wkv.T.reshape(D, 2, HD).transpose(1, 0, 2))
        wp = np.asarray(inputs["L0_attn_w_proj"])
        m["wprojT_t"] = np.ascontiguousarray(wp.T.reshape(D, DC, 128).transpose(1, 0, 2))
        m["bproj_col"] = np.ascontiguousarray(
            np.asarray(inputs["L0_attn_b_proj"]).reshape(16, 128).T)
        m["L0_ffn_W_g"] = np.asarray(inputs["L0_ffn_W_g"])
        m["ws_gate_t"] = np.ascontiguousarray(
            np.asarray(inputs["L0_ffn_Ws_gate"]).reshape(D, 2, 1024).transpose(1, 0, 2))
        m["ws_up_t"] = np.ascontiguousarray(
            np.asarray(inputs["L0_ffn_Ws_up"]).reshape(D, 2, 1024).transpose(1, 0, 2))
        m["L0_ffn_Ws_down"] = np.asarray(inputs["L0_ffn_Ws_down"])
        m["L0_ffn_We_gate"] = np.asarray(inputs["L0_ffn_We_gate"])
        m["L0_ffn_We_up"] = np.asarray(inputs["L0_ffn_We_up"])
        m["L0_ffn_We_down"] = np.asarray(inputs["L0_ffn_We_down"])
    if 1 in layers:
        wq = np.asarray(inputs["L1_attn_Wq"])
        m["wq1_t"] = np.ascontiguousarray(wq.reshape(D, NH, HD).transpose(1, 0, 2))
        wk = np.asarray(inputs["L1_attn_Wk"])
        m["wk1_t"] = np.ascontiguousarray(wk.reshape(D, NH, HD).transpose(1, 0, 2))
        wv = np.asarray(inputs["L1_attn_Wv"])
        m["wv1_t"] = np.ascontiguousarray(wv.reshape(D, 8, 256).transpose(1, 0, 2))
        wo = np.asarray(inputs["L1_attn_Wo"])
        m["wo1_t"] = np.ascontiguousarray(wo.reshape(D, DC, 128).transpose(1, 0, 2))
        m["lam_row"] = np.ascontiguousarray(
            np.asarray(inputs["L1_attn_lambda_init"]).reshape(1, 16))
        m["L1_ffn_w_gate_up"] = np.asarray(inputs["L1_ffn_w_gate_up"])
        m["wproj1_t"] = np.ascontiguousarray(
            np.asarray(inputs["L1_ffn_w_proj"]).reshape(D, 16, 512).transpose(1, 0, 2))
    return m


def kernel(**inputs):
    S = 2048
    core = build_program(S=S, layers=(0, 1))
    in_maps = []
    for b in range(8):
        m = prep_core_inputs(inputs, b, S)
        in_maps.append({k: np.ascontiguousarray(v) for k, v in m.items()
                        if k in core.used_inputs})
    missing = [k for k in core.used_inputs if k not in in_maps[0]]
    assert not missing, missing
    res = run_bass_kernel_spmd(core.nc, in_maps, core_ids=list(range(8)))
    outs = [np.ascontiguousarray(np.asarray(r["outT"]).T) for r in res.results]
    return np.stack(outs).astype(np.float32)


# version 12
# speedup vs baseline: 1.0261x; 1.0261x over previous
import math
import numpy as np
import ml_dtypes
import concourse.bass as bass
import concourse.mybir as mybir
from concourse.bass_utils import run_bass_kernel_spmd

F32 = mybir.dt.float32
BF16 = mybir.dt.bfloat16
AF = mybir.ActivationFunctionType
ALU = mybir.AluOpType
AX = mybir.AxisListType

D = 2048
NH = 16
HD = 128
DC = D // 128
EPS_LN = 1e-5
NEG_BIG = -1.0e30
BF = ml_dtypes.bfloat16

ENGINES = ("tensor", "vector", "scalar", "sync", "gpsimd")
DMA_QUEUES = ("sync", "gpsimd")
_ESZ = {F32: 4, BF16: 2, mybir.dt.int32: 4}
SB_BLK = 512
DR_BLK = 65536
_SB_OFF = {}


def ap_keys(ap):
    t = ap.tensor
    name = t.name
    dims = ap.ap
    sp = str(ap.space)
    if "PSUM" in sp:
        return [("ps", name.rsplit("_", 1)[0] if name not in _PS_NAMES else name)]
    esz = _ESZ[ap.dtype]
    if "DRAM" in sp.upper():
        off = ap.offset
        span = 1
        for st, cnt in dims:
            span += (cnt - 1) * st
        b0 = off // DR_BLK
        b1 = (off + span - 1) // DR_BLK
        return [("d", name, b) for b in range(b0, min(b1, b0 + 63) + 1)]
    pstep = dims[0][0] if dims else 1
    off = ap.offset % pstep if pstep > 0 else ap.offset
    span = 1
    for st, cnt in dims[1:]:
        span += (cnt - 1) * st
    base = _SB_OFF.get(name.rsplit("_", 1)[0], _SB_OFF.get(name, 0))
    lo = base + off * esz
    hi = base + (off + span) * esz
    b0 = lo // SB_BLK
    b1 = (hi - 1) // SB_BLK
    return [("s", b) for b in range(b0, b1 + 1)]


_PS_NAMES = set()
NLANES = 8


class Sched:
    def __init__(self, nc):
        self.nc = nc
        self.ops = []
        self.n_local = {e: 0 for e in ENGINES}
        self.n_dma = {q: 0 for q in DMA_QUEUES}
        self.last_writer = {}
        self.readers = {}
        self.clock = {e: {} for e in ENGINES}
        self.opclock = []

    def add(self, eng, fn, reads=(), writes=(), name="", dma=False):
        i = len(self.ops)
        rkeys = []
        for ap in reads:
            rkeys.extend(ap_keys(ap))
        wkeys = []
        for ap in writes:
            wkeys.extend(ap_keys(ap))
        deps = set()
        for k in rkeys:
            j = self.last_writer.get(k)
            if j is not None:
                deps.add(j)
        for k in wkeys:
            j = self.last_writer.get(k)
            if j is not None:
                deps.add(j)
            for j in self.readers.get(k, ()):
                deps.add(j)
        cl = self.clock[eng]
        lane = None
        pre_wait = None
        if dma:
            idx = self.n_dma[eng]
            lane = (eng, idx % NLANES)
            rnd = idx // NLANES + 1
            if rnd > 1:
                pre_wait = (lane, (rnd - 1) * 16)
                if cl.get(lane, -1) < rnd - 1:
                    cl[lane] = rnd - 1
            self.n_dma[eng] += 1
        need = []
        for j in sorted(deps):
            oj = self.ops[j]
            if cl.get(oj["ckey"], -1) >= oj["cpos"]:
                continue
            need.append(j)
        same_drain = False
        waits = []
        for j in need:
            oj = self.ops[j]
            if oj["eng"] == eng and not oj["dma"]:
                if eng == "tensor":
                    continue
                same_drain = True
            else:
                waits.append(j)
                oj["inc"] = True
        if same_drain:
            cl[eng] = self.n_local[eng] - 1
        for j in waits:
            for e2, v in self.opclock[j].items():
                if cl.get(e2, -1) < v:
                    cl[e2] = v
        ckey = lane if dma else eng
        cpos = (self.n_dma[eng] - 1) // NLANES + 1 if dma else self.n_local[eng]
        op = dict(idx=i, eng=eng, fn=fn, local=self.n_local[eng], dma=dma,
                  inc=dma, waits=waits, drain=same_drain, name=name,
                  ckey=ckey, cpos=cpos, pre_wait=pre_wait)
        self.ops.append(op)
        self.n_local[eng] += 1
        oc = dict(cl)
        oc[ckey] = cpos
        if dma:
            oc[eng] = op["local"]
        self.opclock.append(oc)
        for k in rkeys:
            self.readers.setdefault(k, []).append(i)
        for k in wkeys:
            self.last_writer[k] = i
            self.readers[k] = []
        return i

    def dma(self, queue, out, in_, name="d"):
        return self.add(queue,
                        lambda e, o=out, i_=in_: e.dma_start(out=o, in_=i_),
                        reads=[in_], writes=[out], name="dma:" + name, dma=True)

    def emit(self, blk, final_queue="sync"):
        nc = self.nc
        sems = {e: nc.alloc_semaphore(f"sem_{e}") for e in ENGINES}
        for q in DMA_QUEUES:
            for l in range(NLANES):
                sems[(q, l)] = nc.alloc_semaphore(f"dma_{q}_{l}")
        cnt = {e: 0 for e in ENGINES}
        for op in self.ops:
            if op["inc"]:
                if op["dma"]:
                    op["inc_val"] = op["cpos"] * 16
                    op["sem"] = op["ckey"]
                else:
                    cnt[op["eng"]] += 1
                    op["inc_val"] = cnt[op["eng"]]
                    op["sem"] = op["eng"]
        per_engine = {e: [] for e in ENGINES}
        for op in self.ops:
            per_engine[op["eng"]].append(op)

        def mk_body(eng_ops):
            def body(eh):
                for op in eng_ops:
                    if op["drain"]:
                        eh.drain()
                    if op["pre_wait"] is not None:
                        eh.wait_ge(sems[op["pre_wait"][0]], op["pre_wait"][1])
                    wmax = {}
                    for j in op["waits"]:
                        oj = self.ops[j]
                        sk = oj["sem"]
                        wmax[sk] = max(wmax.get(sk, 0), oj["inc_val"])
                    for key, v in wmax.items():
                        eh.wait_ge(sems[key], v)
                    inst = op["fn"](eh)
                    if op["inc"]:
                        assert inst is not None, op["name"]
                        inst.then_inc(sems[op["sem"]], 16 if op["dma"] else 1)
            return body

        for e in ENGINES:
            if per_engine[e]:
                getattr(blk, e)(mk_body(per_engine[e]))

        finals = {}
        for op in self.ops:
            if op.get("inc_val"):
                finals[op["sem"]] = max(finals.get(op["sem"], 0), op["inc_val"])

        def final_body(eh):
            for sk, v in finals.items():
                eh.wait_ge(sems[sk], v)
        getattr(blk, final_queue)(final_body)


def rotary_tables(S):
    dim = HD
    SCALE_BASE = 512
    inv_freq = 1.0 / 10000.0 ** (np.arange(0, dim, 2, dtype=np.float64) / dim)
    t = np.arange(S, dtype=np.float64)
    freqs = np.einsum("i,j->ij", t, inv_freq)
    freqs = np.concatenate((freqs, freqs), axis=-1)
    scale_base = (np.arange(0, dim, 2, dtype=np.float64) + 0.4 * dim) / (1.4 * dim)
    power = (t - S // 2) / SCALE_BASE
    scale = scale_base[None, :] ** power[:, None]
    scale = np.concatenate((scale, scale), axis=-1)
    pos32 = freqs.astype(np.float32).astype(np.float64)
    sc32 = scale.astype(np.float32).astype(np.float64)
    sw = 1.0 / math.sqrt(HD)
    Aq = (np.cos(pos32) * sc32 * sw).astype(np.float32).T.copy()
    Bq = (np.sin(pos32) * sc32 * sw).astype(np.float32).T.copy()
    Ak = (np.cos(pos32) / sc32 * sw).astype(np.float32).T.copy()
    Bk = (np.sin(pos32) / sc32 * sw).astype(np.float32).T.copy()
    return Aq, Bq, Ak, Bk


def rot_half_matrix():
    R = np.zeros((HD, HD), dtype=np.float32)
    for hp in range(64):
        R[hp + 64, hp] = -1.0
    for hp in range(64, HD):
        R[hp - 64, hp] = 1.0
    return np.asarray(R.astype(BF))


def causal_mask1(SBv):
    pj = np.arange(128)[:, None]
    i = np.arange(SBv)[None, :]
    m0 = np.where(pj > i, np.float32(NEG_BIG), np.float32(0.0))
    full = np.full((128, 384), np.float32(NEG_BIG))
    return np.concatenate([full, m0], axis=1).astype(BF)


def host_constants(S, SBv):
    Aq, Bq, Ak, Bk = rotary_tables(S)
    return {
        "c_rotAq": Aq, "c_rotBq": Bq, "c_rotAk": Ak, "c_rotBk": Bk,
        "c_rotR": rot_half_matrix(),
        "c_mask1": np.asarray(causal_mask1(SBv)),
        "c_ones_bf": np.ones((128, 1), dtype=BF),
        "c_ones_row": np.ones((1, 128), dtype=np.float32),
        "c_ones_row_bf": np.ones((1, 128), dtype=BF),
        "c_iota8": np.broadcast_to(np.arange(8, dtype=np.float32), (128, 8)).copy(),
        "c_ident_f32": np.eye(128, dtype=np.float32),
        "c_ident_bf": np.eye(128, dtype=BF),
        "c_eps": np.broadcast_to(np.array([1e-5, 1.1920929e-07], dtype=np.float32),
                                 (128, 2)).copy(),
    }


class Core:
    def __init__(self, S):
        self.S = S
        self.SB = min(512, S)
        self.NSC = S // self.SB
        self.JC = S // 128
        self.nc = bass.Bass("TRN2", target_bir_lowering=False)
        self.sch = Sched(self.nc)
        self.dram = {}
        self.sb = {}
        self.pool = [self.nc.alloc_psum_tensor(f"pp{i}", [128, 512], F32)
                     for i in range(8)]
        for p in self.pool:
            _PS_NAMES.add(p.name)
        self.pool_bf = [p.ap().bitcast(BF16) for p in self.pool]
        self.used_inputs = []
        self.base = self.nc.bump_sbuf(212352, align_bytes=512)[0]
        self._cursor = 0

    def din(self, name, shape, dt=F32):
        if name not in self.dram:
            self.dram[name] = self.nc.dram_tensor(name, list(shape), dt,
                                                  kind="ExternalInput").ap()
            self.used_inputs.append(name)
        return self.dram[name]

    def dout(self, name, shape, dt=F32):
        self.dram[name] = self.nc.dram_tensor(name, list(shape), dt,
                                              kind="ExternalOutput").ap()
        return self.dram[name]

    def dtmp(self, name, shape, dt=F32):
        if name not in self.dram:
            self.dram[name] = self.nc.dram_tensor(name, list(shape), dt,
                                                  kind="Internal").ap()
        return self.dram[name]

    def sb_at(self, name, shape, dt, offset):
        assert offset % 128 == 0, (name, offset)
        off = self.base + offset
        h = self.nc.alloc_sbuf_tensor_at(name, list(shape), dt, offset=off)
        self.sb[name] = h
        _SB_OFF[name] = off
        _SB_OFF[h.name] = off
        return h

    def alloc(self, name, shape, dt):
        n = shape[-1] * _ESZ[dt]
        for s in shape[1:-1]:
            n *= s
        n = (n + 127) // 128 * 128
        h = self.sb_at(name, shape, dt, self._cursor)
        self._cursor += n
        return h


def build_program(S=2048, layers=(0, 1), dbg=()):
    core = Core(S)
    nc, sch = core.nc, core.sch
    SB, NSC, JC = core.SB, core.NSC, core.JC
    pool = core.pool
    dbg = set(dbg)

    xn = core.sb_at("xn", [128, DC * S], BF16, 0)
    off_big = 32 * S
    heads = core.sb_at("heads", [128, DC * S], BF16, off_big)
    SH = S // 2
    y_moe = core.sb_at("y_moe", [128, DC * SH], F32, off_big)
    y_l1 = core.sb_at("y_l1", [128, DC * S], F32, 0)
    R3 = off_big + 32 * S
    rotA_q = core.sb_at("rotA_q", [128, S], F32, R3)
    rotB_q = core.sb_at("rotB_q", [128, S], F32, R3 + 4 * S)
    rotA_k = core.sb_at("rotA_k", [128, S], F32, R3 + 8 * S)
    rotB_k = core.sb_at("rotB_k", [128, S], F32, R3 + 12 * S)
    slabs = core.sb_at("slabs", [128, 16 * 1024], BF16, R3)
    pT_sb = core.sb_at("pT_sb", [128, 8192], BF16, R3)
    rms_row = core.sb_at("rms_row", [1, 2048], F32, R3 + 16384)
    l_r3 = core.sb_at("l_r3", [128, 1024], F32, R3 + 24576)
    l_r3b = core.sb_at("l_r3b", [128, 2048], BF16, R3 + 24576)
    dn_sc = core.sb_at("dn_sc", [128, 96], F32, R3 + 28672)
    dn_scb = core.sb_at("dn_scb", [128, 96], BF16, R3 + 29184)
    W = R3 + 32768
    ln_xa = core.sb_at("ln_xa", [128, 2 * S], F32, W)
    ln_xbf = core.sb_at("ln_xbf", [128, S], BF16, W + 8 * S)
    ln_xsq = core.sb_at("ln_xsq", [128, S], BF16, W + 10 * S)
    ln_rows = core.sb_at("ln_rows", [1, 2 * S], F32, W + 12 * S)
    ln_tile = core.sb_at("ln_tile", [128, 512], F32, W + 20 * S)
    at_k = core.sb_at("at_k", [128, S], BF16, W)
    at_v = core.sb_at("at_v", [128, S], BF16, W + 2 * S)
    l1_v2 = core.sb_at("l1_v2", [128, 2 * S], BF16, W + 2 * S)
    at_q = core.sb_at("at_q", [128, S], BF16, W + 4 * S)
    at_qr = core.sb_at("at_qr", [128, S], BF16, W + 6 * S)
    at_sq = core.sb_at("at_sq", [128, S], BF16, W + 4 * S)
    FX = W + 8 * S
    at_sh = core.sb_at("at_sh", [128, 1024], F32, FX)
    at_pt = core.sb_at("at_pt", [128, 2048], BF16, FX + 4096)
    at_negc = core.sb_at("at_negc", [128, 512], F32, FX + 2048)
    at_rb = core.sb_at("at_rb", [128, 512], F32, FX + 8192)
    at_rows = core.sb_at("at_rows", [1, 512], F32, FX + 10240)
    rows_bf = core.sb_at("rows_bf", [1, 1024], BF16, FX + 10240)
    at_rows2 = core.sb_at("at_rows2", [1, 512], F32, FX + 12288)
    at_wslab = core.sb_at("at_wslab", [128, 4096], BF16, FX + 14336)
    l1_tc1 = core.sb_at("l1_tc1", [128, 512], F32, FX + 22528)
    l1_tc2 = core.sb_at("l1_tc2", [128, 512], F32, FX + 24576)
    gh = core.sb_at("gh", [128, 8 * 1024], BF16, W)
    ep_g = core.sb_at("ep_g", [128, 1024], F32, W + 16384)
    ep_s = core.sb_at("ep_s", [128, 1024], F32, W + 20480)
    ep_x = core.sb_at("ep_x", [128, 1024], F32, W + 24576)
    web = core.sb_at("web", [128, 1024], F32, W + 28672)
    hf = core.sb_at("hf", [128, 1024], F32, W + 32768)
    gt_w = core.sb_at("gt_w", [128, 8], F32, W + 36864)
    gt_t = core.sb_at("gt_t", [128, 48], F32, W + 37376)
    gt_r = core.sb_at("gt_r", [128, 4], F32, W + 37888)
    wT8 = core.sb_at("wT8", [8, S], F32, W + 16384)
    f1_wbA = core.sb_at("f1_wbA", [128, 4096], BF16, W)
    f1_wbB = core.sb_at("f1_wbB", [128, 4096], BF16, W + 8192)
    f1_gl = core.sb_at("f1_gl", [128, 1024], F32, W + 16384)
    f1_glb = core.sb_at("f1_glb", [128, 2048], BF16, W + 16384)
    f1_wh = core.sb_at("f1_wh", [128, 4096], BF16, W + 20480)
    f1_t1 = core.sb_at("f1_t1", [128, 2048], F32, W + 28672)
    f1_hq = core.sb_at("f1_hq", [128, 2048], BF16, W + 36864)
    f1_sc = core.sb_at("f1_sc", [128, 96], F32, W + 40960)
    f1_scb = core.sb_at("f1_scb", [128, 96], BF16, W + 41472)
    dn_w = core.sb_at("dn_w", [128, 4096], BF16, W + 16384)
    hn_sb = core.sb_at("hn_sb", [128, 8192], BF16, W + 24576)
    dn_g = core.sb_at("dn_g", [128, 512], F32, W + 40960)
    CW = W + 43008
    core._cursor = CW
    mask1 = core.alloc("mask1", [128, 384 + SB], BF16)
    rotR = core.alloc("rotR", [128, 128], BF16)
    ident = core.alloc("ident", [128, 128], F32)
    identb = core.alloc("identb", [128, 128], BF16)
    ones_bf = core.alloc("ones_bf", [128, 1], BF16)
    ones_row = core.alloc("ones_row", [1, 128], F32)
    iota8 = core.alloc("iota8", [128, 8], F32)
    b_col = core.alloc("b_col", [128, 16], F32)
    wg_sb = core.alloc("wg_sb", [128, 128], BF16)
    lam_sb = core.alloc("lam_sb", [128, 32], F32)
    srow_sb = core.alloc("srow_sb", [1, 64], F32)
    ones_row_bf = core.alloc("ones_row_bf", [1, 128], BF16)
    ceps = core.alloc("ceps", [128, 2], F32)
    assert core._cursor <= 212352, core._cursor

    CONST_SHAPES = {
        "c_rotR": ([128, 128], BF16, rotR), "c_ident_f32": ([128, 128], F32, ident),
        "c_ident_bf": ([128, 128], BF16, identb), "c_ones_bf": ([128, 1], BF16, ones_bf),
        "c_ones_row": ([1, 128], F32, ones_row), "c_iota8": ([128, 8], F32, iota8),
        "c_eps": ([128, 2], F32, ceps), "c_mask1": ([128, 384 + SB], BF16, mask1),
        "c_ones_row_bf": ([1, 128], BF16, ones_row_bf),
    }
    for nm, (shp, dt, tile) in CONST_SHAPES.items():
        d = core.din(nm, shp, dt)
        sch.dma("sync", tile[:], d, name=nm)

    def mask_k(k):
        return mask1[:, 384 - 128 * k: 384 - 128 * k + SB]

    def xn_ap(dc, sc=None, s0=None, sl=None):
        if sc is not None:
            return xn[:, dc * S + sc * SB: dc * S + (sc + 1) * SB]
        return xn[:, dc * S + s0: dc * S + s0 + sl]

    def emit_ln(src_name, tag):
        src = core.dram[src_name]
        for dc in range(DC):
            buf = dc % 2
            xav = ln_xa[:, buf * S:(buf + 1) * S]
            sch.dma("sync", xav, src[dc * 128:(dc + 1) * 128, :], name=f"{tag}x{dc}")
            sch.add("vector", lambda e, o=ln_xbf[:], i=xav: e.tensor_copy(o, i),
                    reads=[xav], writes=[ln_xbf[:]], name="ln_cast")
            sch.add("scalar", lambda e, o=ln_xsq[:], i=xav: e.activation(o, i, AF.Square),
                    reads=[xav], writes=[ln_xsq[:]], name="ln_sq")
            for sc in range(NSC):
                o1 = pool[sc % 4][0:1, 0:SB]
                r1 = ln_xbf[:, sc * SB:(sc + 1) * SB]
                sch.add("tensor",
                        lambda e, o=o1, r=r1, st=(dc == 0), sp=(dc == DC - 1):
                        e.matmul(o, ones_bf[:], r, start=st, stop=sp, skip_group_check=True),
                        reads=[r1, ones_bf[:]], writes=[o1], name="ln_mm1")
                o2 = pool[4 + sc % 4][0:1, 0:SB]
                r2 = ln_xsq[:, sc * SB:(sc + 1) * SB]
                sch.add("tensor",
                        lambda e, o=o2, r=r2, st=(dc == 0), sp=(dc == DC - 1):
                        e.matmul(o, ones_bf[:], r, start=st, stop=sp, skip_group_check=True),
                        reads=[r2, ones_bf[:]], writes=[o2], name="ln_mm2")
        for sc in range(NSC):
            mrow = ln_rows[0:1, sc * SB:(sc + 1) * SB]
            vrow = ln_rows[0:1, S + sc * SB: S + (sc + 1) * SB]
            p1 = pool[sc % 4][0:1, 0:SB]
            p2 = pool[4 + sc % 4][0:1, 0:SB]
            sch.add("vector", lambda e, o=mrow, i=p1:
                    e.tensor_scalar(o, i, 1.0 / D, None, ALU.mult),
                    reads=[p1], writes=[mrow], name="ln_m")
            sch.add("vector", lambda e, o=vrow, i=p2:
                    e.tensor_scalar(o, i, 1.0 / D, None, ALU.mult),
                    reads=[p2], writes=[vrow], name="ln_v")
            tr = ln_tile[0:1, 0:SB]
            sch.add("vector", lambda e, o=tr, m=mrow: e.tensor_mul(o, m, m),
                    reads=[mrow], writes=[tr], name="ln_m2")
            sch.add("vector", lambda e, o=vrow, v=vrow, t=tr: e.tensor_sub(o, v, t),
                    reads=[vrow, tr], writes=[vrow], name="ln_var")
            sch.add("scalar", lambda e, o=vrow, i=vrow, b=ceps[0:1, 0:1]:
                    e.activation(o, i, AF.Sqrt, bias=b),
                    reads=[vrow, ceps[0:1, 0:1]], writes=[vrow], name="ln_sqrt")
            sch.add("vector", lambda e, o=vrow, i=vrow: e.reciprocal(o, i),
                    reads=[vrow], writes=[vrow], name="ln_recip")
        for sc in range(NSC):
            mrow = ln_rows[0:1, sc * SB:(sc + 1) * SB]
            vrow = ln_rows[0:1, S + sc * SB: S + (sc + 1) * SB]
            pm = pool[0][:, 0:SB]
            pi = pool[1][:, 0:SB]
            sch.add("tensor", lambda e, o=pm, r=mrow:
                    e.matmul(o, ones_row[:], r, start=True, stop=True),
                    reads=[mrow, ones_row[:]], writes=[pm], name="ln_bm")
            sch.add("tensor", lambda e, o=pi, r=vrow:
                    e.matmul(o, ones_row[:], r, start=True, stop=True),
                    reads=[vrow, ones_row[:]], writes=[pi], name="ln_bi")
            for dc in range(DC):
                buf = dc % 2
                xt = ln_xa[:, buf * S + sc * SB: buf * S + (sc + 1) * SB]
                sch.dma("sync", xt, src[dc * 128:(dc + 1) * 128, sc * SB:(sc + 1) * SB],
                        name=f"{tag}x2_{dc}_{sc}")
                tf = ln_tile[:, 0:SB]
                xo = xn_ap(dc, sc)
                sch.add("vector", lambda e, o=tf, a=xt, b=pm: e.tensor_sub(o, a, b),
                        reads=[xt, pm], writes=[tf], name="ln_n1")
                sch.add("vector", lambda e, o=xo, a=tf, b=pi: e.tensor_mul(o, a, b),
                        reads=[tf, pi], writes=[xo], name="ln_n2")

    def emit_attn0():
        for nm, tile in [("c_rotAq", rotA_q), ("c_rotBq", rotB_q),
                         ("c_rotAk", rotA_k), ("c_rotBk", rotB_k)]:
            d = core.din(nm, [128, S], F32)
            sch.dma("sync", tile[:], d, name=nm)
        wkv = core.din("wkvT_t", [2, D, 128], F32)
        wq_t = core.din("wqT_t", [NH, D, 128], F32)
        wp_t = core.din("wprojT_t", [DC, D, 128], F32)
        bpc = core.din("bproj_col", [128, 16], F32)
        sch.dma("sync", b_col[:], bpc, name="bcol")

        kslab = at_wslab[:, 0:2048]
        vslab = at_wslab[:, 2048:4096]
        sch.dma("gpsimd", kslab.rearrange("p (c n) -> p c n", n=128),
                wkv[0].rearrange("(c p) n -> p c n", p=128), name="wkvk")
        sch.dma("gpsimd", vslab.rearrange("p (c n) -> p c n", n=128),
                wkv[1].rearrange("(c p) n -> p c n", p=128), name="wkvv")

        def rotary(src_bf, dst_bf, tabA, tabB, tag):
            for sc in range(NSC):
                srcv = src_bf[:, sc * SB:(sc + 1) * SB]
                p3 = pool[3][:, 0:SB]
                sch.add("tensor", lambda e, o=p3, r=srcv:
                        e.matmul(o, rotR[:], r, start=True, stop=True),
                        reads=[srcv, rotR[:]], writes=[p3], name=tag + "rot")
                sh = at_sh[:, 0:SB]
                tb = tabB[:, sc * SB:(sc + 1) * SB]
                sch.add("vector", lambda e, o=sh, a=tb, b=p3: e.tensor_mul(o, a, b),
                        reads=[p3, tb], writes=[sh], name=tag + "rb")
                sh2 = at_sh[:, SB:2 * SB]
                ta = tabA[:, sc * SB:(sc + 1) * SB]
                sch.add("vector", lambda e, o=sh2, a=ta, b=srcv: e.tensor_mul(o, a, b),
                        reads=[srcv, ta], writes=[sh2], name=tag + "ra")
                dv = dst_bf[:, sc * SB:(sc + 1) * SB]
                sch.add("vector", lambda e, o=dv, a=sh2, b=sh: e.tensor_add(o, a, b),
                        reads=[sh, sh2], writes=[dv], name=tag + "rs")

        for sc in range(NSC):
            p2 = pool[2][:, 0:SB]
            for dc in range(DC):
                l = kslab[:, dc * 128:(dc + 1) * 128]
                r = xn_ap(dc, sc)
                sch.add("tensor", lambda e, o=p2, l=l, r=r, st=(dc == 0), sp=(dc == DC - 1):
                        e.matmul(o, l, r, start=st, stop=sp),
                        reads=[l, r], writes=[p2], name="kmm")
            kt = at_q[:, sc * SB:(sc + 1) * SB]
            sch.add("scalar", lambda e, o=kt, i=p2: e.activation(o, i, AF.Copy),
                    reads=[p2], writes=[kt], name="kevac")
        rotary(at_q, at_k, rotA_k, rotB_k, "k")
        sch.add("scalar", lambda e, o=at_sq[:], i=at_k[:]: e.activation(o, i, AF.Square),
                reads=[at_k[:]], writes=[at_sq[:]], name="ksq")
        for sc in range(NSC):
            p4 = pool[4][0:1, 0:SB]
            r = at_sq[:, sc * SB:(sc + 1) * SB]
            sch.add("tensor", lambda e, o=p4, r=r:
                    e.matmul(o, ones_bf[:], r, start=True, stop=True),
                    reads=[r, ones_bf[:]], writes=[p4], name="kn2")
            kd = srow_sb[0:1, 1 + sc:2 + sc]
            sch.add("vector", lambda e, o=kd, i=p4: e.reduce_max(o, i, AX.X),
                    reads=[p4], writes=[kd], name="kmax")
        kall = srow_sb[0:1, 1:1 + NSC]
        km = srow_sb[0:1, 0:1]
        sch.add("vector", lambda e, o=km, i=kall: e.reduce_max(o, i, AX.X),
                reads=[kall], writes=[km], name="kmaxall")
        sch.add("scalar", lambda e, o=km, i=km: e.activation(o, i, AF.Sqrt),
                reads=[km], writes=[km], name="kmaxsqrt")
        sch.add("vector", lambda e, o=km, i=km: e.tensor_scalar(o, i, -1.0, None, ALU.mult),
                reads=[km], writes=[km], name="kmaxneg")
        for g in range((JC + 3) // 4):
            n_in_g = min(4, JC - 4 * g)
            p5 = pool[5]
            for j in range(n_in_g):
                jc = 4 * g + j
                for dc in range(DC):
                    l = xn_ap(dc, s0=jc * 128, sl=128)
                    r = vslab[:, dc * 128:(dc + 1) * 128]
                    o = p5[:, j * 128:(j + 1) * 128]
                    sch.add("tensor", lambda e, o=o, l=l, r=r, st=(dc == 0), sp=(dc == DC - 1):
                            e.matmul(o, l, r, start=st, stop=sp, skip_group_check=True),
                            reads=[l, r], writes=[o], name="vmm")
            vv = at_v[:, g * 512:g * 512 + n_in_g * 128]
            pv = p5[:, 0:n_in_g * 128]
            sch.add("vector", lambda e, o=vv, i=pv: e.tensor_copy(o, i),
                    reads=[pv], writes=[vv], name="vevac")
        for n in range(NH):
            qb = n % 2
            qslab = at_wslab[:, qb * 2048:(qb + 1) * 2048]
            sch.dma("gpsimd", qslab.rearrange("p (c n) -> p c n", n=128),
                    wq_t[n].rearrange("(c p) n -> p c n", p=128), name=f"wq{n}")
            for sc in range(NSC):
                p6 = pool[6][:, 0:SB]
                for dc in range(DC):
                    l = qslab[:, dc * 128:(dc + 1) * 128]
                    r = xn_ap(dc, sc)
                    sch.add("tensor", lambda e, o=p6, l=l, r=r, st=(dc == 0), sp=(dc == DC - 1):
                            e.matmul(o, l, r, start=st, stop=sp),
                            reads=[l, r], writes=[p6], name="qmm")
                qt = at_q[:, sc * SB:(sc + 1) * SB]
                sch.add("scalar", lambda e, o=qt, i=p6: e.activation(o, i, AF.Copy),
                        reads=[p6], writes=[qt], name="qevac")
            rotary(at_q, at_qr, rotA_q, rotB_q, "q")
            sch.add("scalar", lambda e, o=at_sq[:], i=at_qr[:]: e.activation(o, i, AF.Square),
                    reads=[at_qr[:]], writes=[at_sq[:]], name="qsq")
            pairs = [tuple(x for x in (a, a + 1) if x < NSC) for a in range(0, NSC, 2)]
            for pair in pairs:
                ncrows = [rows_bf[0:1, 0:SB], rows_bf[0:1, 512:512 + SB]]
                for pi, ib in enumerate(pair):
                    p4r = pool[4][0:1, 0:SB]
                    r = at_sq[:, ib * SB:(ib + 1) * SB]
                    sch.add("tensor", lambda e, o=p4r, r=r:
                            e.matmul(o, ones_bf[:], r, start=True, stop=True),
                            reads=[r, ones_bf[:]], writes=[p4r], name="qn2")
                    nrow = at_rows2[0:1, 0:SB]
                    sch.add("scalar", lambda e, o=nrow, i=p4r: e.activation(o, i, AF.Sqrt),
                            reads=[p4r], writes=[nrow], name="qn")
                    sch.add("vector", lambda e, o=nrow, i=nrow, s=km:
                            e.tensor_scalar(o, i, s, None, ALU.mult),
                            reads=[nrow, km], writes=[nrow], name="negc_row")
                    sch.add("vector", lambda e, o=ncrows[pi], i=nrow: e.tensor_copy(o, i),
                            reads=[nrow], writes=[ncrows[pi]], name="negc_bfc")
                lrows = [pool[2][0:1, 0:SB], pool[4][0:1, 0:SB]]
                OBs = [pool[3][:, 0:SB], pool[5][:, 0:SB]]
                jmaxs = [(ib + 1) * (SB // 128) for ib in pair]

                def emit_sexp0(pi, jc, slot, pair=pair, ncrows=ncrows):
                    ib = pair[pi]
                    SBANK = pool[slot % 2][:, 0:SB]
                    ncr = ncrows[pi]
                    sch.add("tensor", lambda e, o=SBANK, r=ncr:
                            e.matmul(o, ones_row_bf[:], r, start=True, stop=False,
                                     skip_group_check=True),
                            reads=[ncr, ones_row_bf[:]], writes=[SBANK], name="sr1")
                    l = at_k[:, jc * 128:(jc + 1) * 128]
                    r = at_qr[:, ib * SB:(ib + 1) * SB]
                    sch.add("tensor", lambda e, o=SBANK, l=l, r=r:
                            e.matmul(o, l, r, start=False, stop=True,
                                     skip_group_check=True),
                            reads=[l, r], writes=[SBANK], name="smm")
                    pt = at_pt[:, (slot % 4) * SB:(slot % 4 + 1) * SB]
                    diag_k = jc - ib * (SB // 128)
                    if diag_k >= 0:
                        sh = at_sh[:, 0:SB]
                        mk = mask_k(diag_k)
                        sch.add("vector", lambda e, o=sh, a=SBANK, b=mk: e.tensor_add(o, a, b),
                                reads=[SBANK, mk], writes=[sh], name="smask")
                        sch.add("scalar", lambda e, o=pt, i=sh: e.activation(o, i, AF.Exp),
                                reads=[sh], writes=[pt], name="sexp")
                    else:
                        sch.add("scalar", lambda e, o=pt, i=SBANK: e.activation(o, i, AF.Exp),
                                reads=[SBANK], writes=[pt], name="sexp")

                def emit_lo0(pi, jc, slot, jmaxs=jmaxs, lrows=lrows, OBs=OBs):
                    pt = at_pt[:, (slot % 4) * SB:(slot % 4 + 1) * SB]
                    lr_ = lrows[pi]
                    sch.add("tensor", lambda e, o=lr_, r=pt, st=(jc == 0),
                            sp=(jc == jmaxs[pi] - 1):
                            e.matmul(o, ones_bf[:], r, start=st, stop=sp,
                                     skip_group_check=True),
                            reads=[pt, ones_bf[:]], writes=[lr_], name="lmm")
                    p3o = OBs[pi]
                    lv = at_v[:, jc * 128:(jc + 1) * 128]
                    sch.add("tensor", lambda e, o=p3o, l=lv, r=pt, st=(jc == 0),
                            sp=(jc == jmaxs[pi] - 1):
                            e.matmul(o, l, r, start=st, stop=sp, skip_group_check=True),
                            reads=[pt, lv], writes=[p3o], name="omm")

                items = []
                mx = max(jmaxs)
                for jc in range(mx):
                    for pi in range(len(pair)):
                        if jc < jmaxs[pi]:
                            items.append((pi, jc))
                for s in range(len(items) + 3):
                    if s >= 3:
                        emit_lo0(items[s - 3][0], items[s - 3][1], s - 3)
                    if s < len(items):
                        emit_sexp0(items[s][0], items[s][1], s)
                for pi, ib in enumerate(pair):
                    rrow = at_rows2[0:1, 0:SB]
                    sch.add("vector", lambda e, o=rrow, i=lrows[pi]:
                            e.tensor_scalar(o, i, 1e-30, None, ALU.add),
                            reads=[lrows[pi]], writes=[rrow], name="laddeps")
                    sch.add("vector", lambda e, o=rrow, i=rrow: e.reciprocal(o, i),
                            reads=[rrow], writes=[rrow], name="lrecip")
                    p6b = pool[6][:, 0:SB]
                    sch.add("tensor", lambda e, o=p6b, r=rrow:
                            e.matmul(o, ones_row[:], r, start=True, stop=True),
                            reads=[rrow, ones_row[:]], writes=[p6b], name="rbc")
                    rbv = at_rb[:, 0:SB]
                    sch.add("vector", lambda e, o=rbv, i=p6b: e.tensor_copy(o, i),
                            reads=[p6b], writes=[rbv], name="rbsb")
                    ho = heads[:, n * S + ib * SB: n * S + (ib + 1) * SB]
                    sch.add("vector", lambda e, o=ho, a=OBs[pi], b=rbv: e.tensor_mul(o, a, b),
                            reads=[OBs[pi], rbv], writes=[ho], name="oscale")

        sT = core.dtmp("sT", [D, S], F32)
        for dc in range(DC):
            wb = dc % 2
            wslab = at_wslab[:, wb * 2048:(wb + 1) * 2048]
            sch.dma("gpsimd", wslab.rearrange("p (c n) -> p c n", n=128),
                    wp_t[dc].rearrange("(c p) n -> p c n", p=128), name=f"wp{dc}")
            for sc in range(NSC):
                bank = pool[5 + sc % 2][:, 0:SB]
                for oc in range(NH):
                    l = wslab[:, oc * 128:(oc + 1) * 128]
                    r = heads[:, oc * S + sc * SB: oc * S + (sc + 1) * SB]
                    sch.add("tensor", lambda e, o=bank, l=l, r=r, st=(oc == 0), sp=(oc == NH - 1):
                            e.matmul(o, l, r, start=st, stop=sp),
                            reads=[l, r], writes=[bank], name="pmm")
                eb = sc % 2
                ev = at_sh[:, eb * SB:(eb + 1) * SB]
                bc = b_col[:, dc:dc + 1]
                sch.add("scalar", lambda e, o=ev, i=bank, b=bc: e.add(o, i, b),
                        reads=[bank, bc], writes=[ev], name="pevac")
                sch.dma("sync", sT[dc * 128:(dc + 1) * 128, sc * SB:(sc + 1) * SB], ev,
                        name=f"sTw{dc}_{sc}")

    def emit_moe0():
        wg = core.din("L0_ffn_W_g", [D, 8], F32)
        sch.dma("gpsimd", wg_sb[:].rearrange("p (c n) -> p c n", n=8),
                wg.rearrange("(c p) n -> p c n", p=128), name="wg")
        gwT = core.dtmp("gw8", [8, S], F32)
        xT = core.dram["xT"]
        sTd = core.dram["sT"]
        x1T = core.dtmp("x1T", [D, S], F32)
        for tc in range(S // 128):
            p0 = pool[0][:, 0:8]
            for dc in range(DC):
                l = xn_ap(dc, s0=tc * 128, sl=128)
                r = wg_sb[:, dc * 8:(dc + 1) * 8]
                sch.add("tensor", lambda e, o=p0, l=l, r=r, st=(dc == 0), sp=(dc == DC - 1):
                        e.matmul(o, l, r, start=st, stop=sp),
                        reads=[l, r], writes=[p0], name="gmm")
            t = [gt_t[:, k * 8:(k + 1) * 8] for k in range(6)]
            ga = t[5]
            r4 = gt_r
            sch.add("vector", lambda e, o=ga, i=p0: e.tensor_copy(o, i),
                    reads=[p0], writes=[ga], name="gcp")
            sch.add("vector", lambda e, o=r4[:, 0:1], i=ga:
                    e.reduce_max(o, i, AX.X, negate=True),
                    reads=[ga], writes=[r4[:, 0:1]], name="gmax1")
            sch.add("scalar", lambda e, o=t[0], i=ga, b=r4[:, 0:1]:
                    e.activation(o, i, AF.Exp, bias=b),
                    reads=[ga, r4[:, 0:1]], writes=[t[0]], name="gexp")
            sch.add("vector", lambda e, o=r4[:, 1:2], i=t[0]:
                    e.tensor_reduce(o, i, AX.X, ALU.add),
                    reads=[t[0]], writes=[r4[:, 1:2]], name="gsum")
            sch.add("vector", lambda e, o=r4[:, 1:2], i=r4[:, 1:2]: e.reciprocal(o, i),
                    reads=[r4[:, 1:2]], writes=[r4[:, 1:2]], name="grec")
            sch.add("vector", lambda e, o=t[1], i=ga: e.tensor_scalar(o, i, -1.0, None, ALU.mult),
                    reads=[ga], writes=[t[1]], name="gneg")
            sch.add("vector", lambda e, o=t[2], i=t[1], s=r4[:, 0:1]:
                    e.tensor_scalar(o, i, s, None, ALU.is_equal),
                    reads=[t[1], r4[:, 0:1]], writes=[t[2]], name="geq1")
            sch.add("vector", lambda e, o=t[3], a=t[2], b=iota8[:]: e.tensor_mul(o, a, b),
                    reads=[t[2], iota8[:]], writes=[t[3]], name="gkey")
            sch.add("vector", lambda e, o=t[3], a=t[2], b=t[3]:
                    e.scalar_tensor_tensor(o, a, -8.0, b, ALU.mult, ALU.add),
                    reads=[t[2], t[3]], writes=[t[3]], name="gkey2")
            sch.add("vector", lambda e, o=t[3], i=t[3]: e.tensor_scalar(o, i, 8.0, None, ALU.add),
                    reads=[t[3]], writes=[t[3]], name="gkey3")
            sch.add("vector", lambda e, o=r4[:, 2:3], i=t[3]:
                    e.tensor_reduce(o, i, AX.X, ALU.min),
                    reads=[t[3]], writes=[r4[:, 2:3]], name="gi1")
            sch.add("vector", lambda e, o=t[2], i=iota8[:], s=r4[:, 2:3]:
                    e.tensor_scalar(o, i, s, None, ALU.is_equal),
                    reads=[iota8[:], r4[:, 2:3]], writes=[t[2]], name="gsel1")
            sch.add("vector", lambda e, o=t[4], a=t[2], b=ga:
                    e.scalar_tensor_tensor(o, a, -1e9, b, ALU.mult, ALU.add),
                    reads=[t[2], ga], writes=[t[4]], name="gmask")
            sch.add("vector", lambda e, o=r4[:, 3:4], i=t[4]:
                    e.reduce_max(o, i, AX.X, negate=True),
                    reads=[t[4]], writes=[r4[:, 3:4]], name="gmax2")
            sch.add("vector", lambda e, o=t[4], i=t[4]:
                    e.tensor_scalar(o, i, -1.0, None, ALU.mult),
                    reads=[t[4]], writes=[t[4]], name="gneg2")
            sch.add("vector", lambda e, o=t[1], i=t[4], s=r4[:, 3:4]:
                    e.tensor_scalar(o, i, s, None, ALU.is_equal),
                    reads=[t[4], r4[:, 3:4]], writes=[t[1]], name="geq2")
            sch.add("vector", lambda e, o=t[4], a=t[1], b=iota8[:]: e.tensor_mul(o, a, b),
                    reads=[t[1], iota8[:]], writes=[t[4]], name="gkey4")
            sch.add("vector", lambda e, o=t[4], a=t[1], b=t[4]:
                    e.scalar_tensor_tensor(o, a, -8.0, b, ALU.mult, ALU.add),
                    reads=[t[1], t[4]], writes=[t[4]], name="gkey5")
            sch.add("vector", lambda e, o=t[4], i=t[4]: e.tensor_scalar(o, i, 8.0, None, ALU.add),
                    reads=[t[4]], writes=[t[4]], name="gkey6")
            sch.add("vector", lambda e, o=r4[:, 2:3], i=t[4]:
                    e.tensor_reduce(o, i, AX.X, ALU.min),
                    reads=[t[4]], writes=[r4[:, 2:3]], name="gi2")
            sch.add("vector", lambda e, o=t[1], i=iota8[:], s=r4[:, 2:3]:
                    e.tensor_scalar(o, i, s, None, ALU.is_equal),
                    reads=[iota8[:], r4[:, 2:3]], writes=[t[1]], name="gsel2")
            sch.add("vector", lambda e, o=t[2], a=t[2], b=t[1]: e.tensor_add(o, a, b),
                    reads=[t[2], t[1]], writes=[t[2]], name="gselb")
            sch.add("vector", lambda e, o=t[0], a=t[0], b=t[2]: e.tensor_mul(o, a, b),
                    reads=[t[0], t[2]], writes=[t[0]], name="gwm")
            sch.add("vector", lambda e, o=gt_w[:], i=t[0], s=r4[:, 1:2]:
                    e.tensor_scalar(o, i, s, None, ALU.mult),
                    reads=[t[0], r4[:, 1:2]], writes=[gt_w[:]], name="gwf")
            p1t = pool[1][0:8, 0:128]
            sch.add("tensor", lambda e, o=p1t, i=gt_w[:]: e.transpose(o, i, ident[:]),
                    reads=[gt_w[:], ident[:]], writes=[pool[1][:, 0:128]], name="gtr")
            wo = wT8[0:8, tc * 128:(tc + 1) * 128]
            sch.add("vector", lambda e, o=wo, i=p1t: e.tensor_copy(o, i),
                    reads=[pool[1][:, 0:128]], writes=[wo], name="gtrev")
        sch.dma("sync", gwT, wT8[:], name="gw8")

        Wsg = core.din("ws_gate_t", [2, D, 1024], F32)
        Wsu = core.din("ws_up_t", [2, D, 1024], F32)
        Wsd = core.din("L0_ffn_Ws_down", [D, D], F32)
        Weg = core.din("L0_ffn_We_gate", [8, D, 1024], F32)
        Weu = core.din("L0_ffn_We_up", [8, D, 1024], F32)
        Wed = core.din("L0_ffn_We_down", [8, 1024, D], F32)
        SHW = min(512, SH)
        NB = SH // SHW
        HSLB = 8192

        def load_gu_half(mat_ap, g, buf, what):
            for q in range(2):
                dst = slabs[:, buf * HSLB + q * 4096: buf * HSLB + (q + 1) * 4096]
                dst = dst.rearrange("p (c n) -> p c n", n=512)
                srcv = mat_ap[q * 1024:(q + 1) * 1024, g * 512:(g + 1) * 512]
                srcv = srcv.rearrange("(c p) n -> p c n", p=128)
                sch.dma("gpsimd", dst, srcv, name=f"m{what}{g}{q}")

        def load_dn_half(mat_ap, g, buf, what):
            for q in range(2):
                dst = slabs[:, buf * HSLB + q * 4096: buf * HSLB + (q + 1) * 4096]
                dst = dst.rearrange("p (c n) -> p c n", n=1024)
                srcv = mat_ap[q * 512:(q + 1) * 512, g * 1024:(g + 1) * 1024]
                srcv = srcv.rearrange("(c p) n -> p c n", p=128)
                sch.dma("gpsimd", dst, srcv, name=f"md{what}{g}{q}")

        def moe_gu_half(kind, g, buf, s0, shared):
            for dc in range(DC):
                for eo_l in range(4):
                    l = slabs[:, buf * HSLB + dc * 512 + eo_l * 128:
                              buf * HSLB + dc * 512 + (eo_l + 1) * 128]
                    for sw in range(NB):
                        o = pool[eo_l * NB + sw][:, 0:SHW]
                        r = xn_ap(dc, s0=s0 + sw * SHW, sl=SHW)
                        sch.add("tensor",
                                lambda e, o=o, l=l, r=r, st=(dc == 0), sp=(dc == DC - 1):
                                e.matmul(o, l, r, start=st, stop=sp, skip_group_check=True),
                                reads=[l, r], writes=[o], name="moe" + kind)
            for eo_l in range(4):
                eo = 4 * g + eo_l
                for sw in range(NB):
                    ghs = gh[:, eo * SH + sw * SHW: eo * SH + (sw + 1) * SHW]
                    pu = pool[eo_l * NB + sw][:, 0:SHW]
                    if kind == "g":
                        sch.add("scalar", lambda e, o=ghs, i=pu: e.activation(o, i, AF.Silu),
                                reads=[pu], writes=[ghs], name="moesilu")
                    elif shared:
                        sch.add("vector", lambda e, o=ghs, a=ghs, b=pu: e.tensor_mul(o, a, b),
                                reads=[pu, ghs], writes=[ghs], name="moeh")
                    else:
                        hfv = hf[:, sw * SHW:(sw + 1) * SHW]
                        sch.add("vector", lambda e, o=hfv, a=ghs, b=pu: e.tensor_mul(o, a, b),
                                reads=[pu, ghs], writes=[hfv], name="moeh1")
                        wv2 = web[:, sw * SHW:(sw + 1) * SHW]
                        sch.add("vector", lambda e, o=ghs, a=hfv, b=wv2: e.tensor_mul(o, a, b),
                                reads=[hfv, wv2], writes=[ghs], name="moeh2")

        def moe_dn_half(g, buf, s0, first):
            for dc in range(8 * g, 8 * g + 8):
                d_l = dc - 8 * g
                bank = (dc % 2) * NB
                for eo in range(8):
                    l = slabs[:, buf * HSLB + eo * 1024 + d_l * 128:
                              buf * HSLB + eo * 1024 + (d_l + 1) * 128]
                    for sw in range(NB):
                        o = pool[bank + sw][:, 0:SHW]
                        r = gh[:, eo * SH + sw * SHW: eo * SH + (sw + 1) * SHW]
                        sch.add("tensor",
                                lambda e, o=o, l=l, r=r, st=(eo == 0), sp=(eo == 7):
                                e.matmul(o, l, r, start=st, stop=sp, skip_group_check=True),
                                reads=[l, r], writes=[o], name="moed")
                for sw in range(NB):
                    yv = y_moe[:, dc * SH + sw * SHW: dc * SH + (sw + 1) * SHW]
                    po = pool[bank + sw][:, 0:SHW]
                    if first:
                        sch.add("vector", lambda e, o=yv, i=po: e.tensor_copy(o, i),
                                reads=[po], writes=[yv], name="ycp")
                    else:
                        sch.add("vector", lambda e, o=yv, a=yv, b=po: e.tensor_add(o, a, b),
                                reads=[po, yv], writes=[yv], name="yadd")

        for half in range(2):
            s0 = half * SH
            tasks = []
            for pe in range(10):
                shared = pe < 2
                if shared:
                    gate_src, up_src = Wsg[pe], Wsu[pe]
                    down_src = Wsd[pe * 1024:(pe + 1) * 1024, :]
                else:
                    gate_src, up_src = Weg[pe - 2], Weu[pe - 2]
                    down_src = Wed[pe - 2]
                for g in range(2):
                    tasks.append(("g", gate_src, g, pe, shared))
                for g in range(2):
                    tasks.append(("u", up_src, g, pe, shared))
                for g in range(2):
                    tasks.append(("d", down_src, g, pe, shared))

            def load_task(i):
                kind, mat, g, pe, shared = tasks[i]
                if kind == "d":
                    load_dn_half(mat, g, i % 2, f"{pe}")
                else:
                    load_gu_half(mat, g, i % 2, f"{kind}{pe}")

            load_task(0)
            for i, (kind, mat, g, pe, shared) in enumerate(tasks):
                if pe >= 2 and kind == "g" and g == 0:
                    er = pe - 2
                    wv = web[:, 0:SH]
                    sch.dma("sync", wv,
                            gwT[er:er + 1, s0:s0 + SH].partition_broadcast(128),
                            name=f"web{er}_{half}")
                if i + 1 < len(tasks):
                    load_task(i + 1)
                if kind == "d":
                    moe_dn_half(g, i % 2, s0, pe == 0)
                else:
                    moe_gu_half(kind, g, i % 2, s0, shared)
            for dc in range(DC):
                for sw in range(NB):
                    eb = sw % 2
                    stile = ep_s[:, eb * 512:eb * 512 + SHW]
                    xtile = ep_x[:, eb * 512:eb * 512 + SHW]
                    gtile = ep_g[:, eb * 512:eb * 512 + SHW]
                    sch.dma("sync", stile,
                            sTd[dc * 128:(dc + 1) * 128, s0 + sw * SHW:s0 + (sw + 1) * SHW],
                            name="eplds")
                    sch.dma("sync", xtile,
                            xT[dc * 128:(dc + 1) * 128, s0 + sw * SHW:s0 + (sw + 1) * SHW],
                            name="epldx")
                    yv = y_moe[:, dc * SH + sw * SHW: dc * SH + (sw + 1) * SHW]
                    sch.add("scalar", lambda e, o=gtile, i=yv: e.activation(o, i, AF.Silu),
                            reads=[yv], writes=[gtile], name="epsilu")
                    sch.add("vector", lambda e, o=gtile, a=gtile, b=stile: e.tensor_mul(o, a, b),
                            reads=[gtile, stile], writes=[gtile], name="epmul")
                    sch.add("vector", lambda e, o=xtile, a=xtile, b=gtile: e.tensor_add(o, a, b),
                            reads=[gtile, xtile], writes=[xtile], name="epadd")
                    sch.dma("sync",
                            x1T[dc * 128:(dc + 1) * 128, s0 + sw * SHW:s0 + (sw + 1) * SHW],
                            xtile, name="epst")

    def emit_l1():
        pool_bf = core.pool_bf
        emit_ln("x1T", "ln1")
        wq_t1 = core.din("wq1_t", [NH, D, 128], F32)
        wk_t1 = core.din("wk1_t", [NH, D, 128], F32)
        wv_t1 = core.din("wv1_t", [8, D, 256], F32)
        wo_t1 = core.din("wo1_t", [DC, D, 128], F32)
        lam_d = core.din("lam_row", [1, 16], F32)
        sTd = core.dram["sT"]
        x1T = core.dram["x1T"]
        outT = core.dout("outT", [D, S], F32)

        lr = lam_sb[0:1, 0:16]
        sch.dma("sync", lr, lam_d, name="lam")
        sch.add("scalar", lambda e, o=lr, i=lr: e.activation(o, i, AF.Sigmoid),
                reads=[lr], writes=[lr], name="lamsig")
        sch.add("vector", lambda e, o=lr, i=lr: e.tensor_scalar(o, i, -1.0, None, ALU.mult),
                reads=[lr], writes=[lr], name="lamneg")
        pl = pool[7][:, 0:16]
        sch.add("tensor", lambda e, o=pl, r=lr: e.matmul(o, ones_row[:], r, start=True, stop=True),
                reads=[lr, ones_row[:]], writes=[pool[7][:, 0:512]], name="lambc")
        nlam = lam_sb[:, 16:32]
        sch.add("vector", lambda e, o=nlam, i=pl: e.tensor_copy(o, i),
                reads=[pool[7][:, 0:512]], writes=[nlam], name="lamsb")

        for pr in range(NH // 2):
            vslab = at_wslab[:, 0:4096]
            sch.dma("gpsimd", vslab.rearrange("p (c n) -> p c n", n=256),
                    wv_t1[pr].rearrange("(c p) n -> p c n", p=128), name=f"wv{pr}")
            for jc in range(JC):
                pv = pool[6 + jc % 2][:, 0:256]
                for dc in range(DC):
                    l = xn_ap(dc, s0=jc * 128, sl=128)
                    r = vslab[:, dc * 256:(dc + 1) * 256]
                    sch.add("tensor", lambda e, o=pv, l=l, r=r, st=(dc == 0), sp=(dc == DC - 1):
                            e.matmul(o, l, r, start=st, stop=sp, skip_group_check=True),
                            reads=[l, r], writes=[pv], name="v1mm")
                vv = l1_v2[:, jc * 256:(jc + 1) * 256]
                sch.add("vector", lambda e, o=vv, i=pv: e.tensor_copy(o, i),
                        reads=[pv], writes=[vv], name="v1evac")
            for hp in range(2):
                n = 2 * pr + hp
                for wi, (wt, dst) in enumerate(((wq_t1, at_qr), (wk_t1, at_k))):
                    slab = at_wslab[:, wi * 2048:(wi + 1) * 2048]
                    sch.dma("gpsimd", slab.rearrange("p (c n) -> p c n", n=128),
                            wt[n].rearrange("(c p) n -> p c n", p=128), name=f"w1_{n}_{wi}")
                    for sc in range(NSC):
                        p6 = pool[6][:, 0:SB]
                        for dc in range(DC):
                            l = slab[:, dc * 128:(dc + 1) * 128]
                            r = xn_ap(dc, sc)
                            sch.add("tensor",
                                    lambda e, o=p6, l=l, r=r, st=(dc == 0), sp=(dc == DC - 1):
                                    e.matmul(o, l, r, start=st, stop=sp),
                                    reads=[l, r], writes=[p6], name="qk1mm")
                        dv = dst[:, sc * SB:(sc + 1) * SB]
                        if wi == 0:
                            sch.add("scalar", lambda e, o=dv, i=p6:
                                    e.activation(o, i, AF.Copy, scale=0.125),
                                    reads=[p6], writes=[dv], name="q1evac")
                        else:
                            sch.add("scalar", lambda e, o=dv, i=p6: e.activation(o, i, AF.Copy),
                                    reads=[p6], writes=[dv], name="k1evac")
                sqc = at_pt[:, 0:512]
                km2 = [srow_sb[0:1, 8 + 2 * h:9 + 2 * h] for h in range(2)]
                tmp = srow_sb[0:1, 12:13]
                for h in range(2):
                    P = slice(0, 64) if h == 0 else slice(64, 128)
                    acc = km2[h]
                    for sc in range(NSC):
                        ksl = at_k[P, sc * SB:(sc + 1) * SB]
                        sqv = sqc[P, 0:SB]
                        sch.add("scalar", lambda e, o=sqv, i=ksl: e.activation(o, i, AF.Square),
                                reads=[ksl], writes=[sqv], name="k1sq")
                        p4 = pool[4][0:1, 0:SB]
                        r = sqv
                        lw = ones_bf[P, 0:1]
                        sch.add("tensor", lambda e, o=p4, l=lw, r=r:
                                e.matmul(o, l, r, start=True, stop=True),
                                reads=[r, lw], writes=[p4], name="kn21")
                        if sc == 0:
                            sch.add("vector", lambda e, o=acc, i=p4: e.reduce_max(o, i, AX.X),
                                    reads=[p4], writes=[acc], name="km1")
                        else:
                            sch.add("vector", lambda e, o=tmp, i=p4: e.reduce_max(o, i, AX.X),
                                    reads=[p4], writes=[tmp], name="km1b")
                            sch.add("vector", lambda e, o=acc, a=acc, b=tmp:
                                    e.tensor_tensor(o, a, b, ALU.max),
                                    reads=[acc, tmp], writes=[acc], name="km1c")
                    sch.add("scalar", lambda e, o=acc, i=acc: e.activation(o, i, AF.Sqrt),
                            reads=[acc], writes=[acc], name="km1sqrt")
                    sch.add("vector", lambda e, o=acc, i=acc:
                            e.tensor_scalar(o, i, -1.0, None, ALU.mult),
                            reads=[acc], writes=[acc], name="km1neg")
                for ib in range(NSC):
                    jmax = (ib + 1) * (SB // 128)
                    combs = []
                    ncrows = [rows_bf[0:1, 0:SB], rows_bf[0:1, 512:512 + SB]]
                    for h in range(2):
                        P = slice(0, 64) if h == 0 else slice(64, 128)
                        qsl = at_qr[P, ib * SB:(ib + 1) * SB]
                        sqv = sqc[P, 0:SB]
                        sch.add("scalar", lambda e, o=sqv, i=qsl: e.activation(o, i, AF.Square),
                                reads=[qsl], writes=[sqv], name="q1sq")
                        p4r = pool[4][0:1, 0:SB]
                        r = sqv
                        lw = ones_bf[P, 0:1]
                        sch.add("tensor", lambda e, o=p4r, l=lw, r=r:
                                e.matmul(o, l, r, start=True, stop=True),
                                reads=[r, lw], writes=[p4r], name="qn21")
                        nrow = at_rows2[0:1, 0:SB]
                        sch.add("scalar", lambda e, o=nrow, i=p4r: e.activation(o, i, AF.Sqrt),
                                reads=[p4r], writes=[nrow], name="qn1")
                        sch.add("vector", lambda e, o=nrow, i=nrow, s=km2[h]:
                                e.tensor_scalar(o, i, s, None, ALU.mult),
                                reads=[nrow, km2[h]], writes=[nrow], name="negc1r")
                        sch.add("vector", lambda e, o=ncrows[h], i=nrow: e.tensor_copy(o, i),
                                reads=[nrow], writes=[ncrows[h]], name="negc1c")
                    lrows = [pool[2][0:1, 0:SB], pool[4][0:1, 0:SB]]
                    OBs = [pool[3][:, 0:SB], pool[5][:, 0:SB]]
                    Ps = [slice(0, 64), slice(64, 128)]

                    def emit_sexp1(h, jc, slot, ib=ib):
                        SBANK = pool[slot % 2][:, 0:SB]
                        ncr = ncrows[h]
                        sch.add("tensor", lambda e, o=SBANK, r=ncr:
                                e.matmul(o, ones_row_bf[:], r, start=True, stop=False,
                                         skip_group_check=True),
                                reads=[ncr, ones_row_bf[:]], writes=[SBANK], name="s1r1")
                        l = at_k[Ps[h], jc * 128:(jc + 1) * 128]
                        rq = at_qr[Ps[h], ib * SB:(ib + 1) * SB]
                        sch.add("tensor", lambda e, o=SBANK, l=l, r=rq:
                                e.matmul(o, l, r, start=False, stop=True,
                                         skip_group_check=True),
                                reads=[l, rq], writes=[SBANK], name="s1mm")
                        pt = at_pt[:, (slot % 4) * SB:(slot % 4 + 1) * SB]
                        diag_k = jc - ib * (SB // 128)
                        if diag_k >= 0:
                            sh = at_sh[:, 0:SB]
                            mk = mask_k(diag_k)
                            sch.add("vector", lambda e, o=sh, a=SBANK, b=mk:
                                    e.tensor_add(o, a, b),
                                    reads=[SBANK, mk], writes=[sh], name="s1mask")
                            sch.add("scalar", lambda e, o=pt, i=sh: e.activation(o, i, AF.Exp),
                                    reads=[sh], writes=[pt], name="s1exp")
                        else:
                            sch.add("scalar", lambda e, o=pt, i=SBANK: e.activation(o, i, AF.Exp),
                                    reads=[SBANK], writes=[pt], name="s1exp")

                    def emit_lo1(h, jc, slot, jmax=jmax, hp=hp):
                        pt = at_pt[:, (slot % 4) * SB:(slot % 4 + 1) * SB]
                        lr_ = lrows[h]
                        sch.add("tensor", lambda e, o=lr_, r=pt, st=(jc == 0),
                                sp=(jc == jmax - 1):
                                e.matmul(o, ones_bf[:], r, start=st, stop=sp,
                                         skip_group_check=True),
                                reads=[pt, ones_bf[:]], writes=[lr_], name="l1mm")
                        lv = l1_v2[:, jc * 256 + hp * 128: jc * 256 + hp * 128 + 128]
                        OB = OBs[h]
                        sch.add("tensor", lambda e, o=OB, l=lv, r=pt, st=(jc == 0),
                                sp=(jc == jmax - 1):
                                e.matmul(o, l, r, start=st, stop=sp, skip_group_check=True),
                                reads=[pt, lv], writes=[OB], name="o1mm")

                    items = []
                    for jc in range(jmax):
                        items.append((0, jc))
                        items.append((1, jc))
                    for s in range(len(items) + 3):
                        if s >= 3:
                            emit_lo1(items[s - 3][0], items[s - 3][1], s - 3)
                        if s < len(items):
                            emit_sexp1(items[s][0], items[s][1], s)
                    for h in range(2):
                        rrow = at_rows2[0:1, 0:SB]
                        sch.add("vector", lambda e, o=rrow, i=lrows[h]:
                                e.tensor_scalar(o, i, 1e-30, None, ALU.add),
                                reads=[lrows[h]], writes=[rrow], name="l1eps")
                        sch.add("vector", lambda e, o=rrow, i=rrow: e.reciprocal(o, i),
                                reads=[rrow], writes=[rrow], name="l1rec")
                        p6b = pool[6][:, 0:SB]
                        sch.add("tensor", lambda e, o=p6b, r=rrow:
                                e.matmul(o, ones_row[:], r, start=True, stop=True),
                                reads=[rrow, ones_row[:]], writes=[p6b], name="r1bc")
                        rbv = at_rb[:, 0:SB]
                        sch.add("vector", lambda e, o=rbv, i=p6b: e.tensor_copy(o, i),
                                reads=[p6b], writes=[rbv], name="r1sb")
                        tc = (l1_tc1 if h == 0 else at_negc)[:, 0:SB]
                        sch.add("vector", lambda e, o=tc, a=OBs[h], b=rbv:
                                e.tensor_mul(o, a, b),
                                reads=[OBs[h], rbv], writes=[tc], name="o1s")
                        combs.append(tc)
                    ho = heads[:, n * S + ib * SB: n * S + (ib + 1) * SB]
                    nl = nlam[:, n:n + 1]
                    sch.add("vector", lambda e, o=ho, a=combs[1], s=nl, b=combs[0]:
                            e.scalar_tensor_tensor(o, a, s, b, ALU.mult, ALU.add),
                            reads=[combs[0], combs[1], nl], writes=[ho], name="dcomb")

        for dc in range(DC):
            wb = dc % 2
            wslab = at_wslab[:, wb * 2048:(wb + 1) * 2048]
            sch.dma("gpsimd", wslab.rearrange("p (c n) -> p c n", n=128),
                    wo_t1[dc].rearrange("(c p) n -> p c n", p=128), name=f"wo{dc}")
            for sc in range(NSC):
                bank = pool[5 + sc % 2][:, 0:SB]
                for oc in range(NH):
                    l = wslab[:, oc * 128:(oc + 1) * 128]
                    r = heads[:, oc * S + sc * SB: oc * S + (sc + 1) * SB]
                    sch.add("tensor", lambda e, o=bank, l=l, r=r, st=(oc == 0), sp=(oc == NH - 1):
                            e.matmul(o, l, r, start=st, stop=sp),
                            reads=[l, r], writes=[bank], name="womm")
                eb = sc % 2
                ev = at_sh[:, eb * SB:(eb + 1) * SB]
                sch.add("scalar", lambda e, o=ev, i=bank: e.activation(o, i, AF.Copy),
                        reads=[bank], writes=[ev], name="woevac")
                sch.dma("sync", sTd[dc * 128:(dc + 1) * 128, sc * SB:(sc + 1) * SB], ev,
                        name=f"s1Tw{dc}_{sc}")

        wgu = core.din("L1_ffn_w_gate_up", [16384, D], F32)
        wp1 = core.din("wproj1_t", [16, D, 512], F32)
        h_d = core.dtmp("h_l1", [8192, S], BF16)
        HC = 64
        SW = min(512, S)
        NSW = S // SW

        def quant_prefix(src_wb, g_buf, l_buf, scf, scb, G, tag):
            ng = G // 64
            s32a = scf[:, 0:min(ng, 32)]
            v3 = src_wb.rearrange("p (g k) -> p g k", k=64)
            if ng <= 32:
                sch.add("vector", lambda e, o=s32a, i=v3:
                        e.tensor_reduce(o, i, AX.X, ALU.add, apply_absolute_value=True),
                        reads=[src_wb], writes=[s32a], name=tag + "qs")
                sch.add("vector", lambda e, o=s32a, i=s32a:
                        e.tensor_scalar(o, i, 1.0 / 64, 1e-8, ALU.mult, ALU.max),
                        reads=[s32a], writes=[s32a], name=tag + "qc")
                sbf = scb[:, 0:ng]
                sch.add("vector", lambda e, o=sbf, i=s32a: e.tensor_copy(o, i),
                        reads=[s32a], writes=[sbf], name=tag + "qb")
                th = scf[:, 32:32 + ng]
                sch.add("vector", lambda e, o=th, i=sbf:
                        e.tensor_scalar(o, i, 0.501953125, None, ALU.mult),
                        reads=[sbf], writes=[th], name=tag + "qt")
                nth = scf[:, 64:64 + ng]
                sch.add("vector", lambda e, o=nth, i=th:
                        e.tensor_scalar(o, i, -1.0, None, ALU.mult),
                        reads=[th], writes=[nth], name=tag + "qn")
            else:
                raise NotImplementedError
            g_ = g_buf[:, 0:G]
            l_ = l_buf[:, 0:G]
            thb = th.unsqueeze(2).to_broadcast([128, ng, 64])
            nthb = nth.unsqueeze(2).to_broadcast([128, ng, 64])
            sch.add("vector", lambda e, o=g_.rearrange("p (g k) -> p g k", k=64), a=v3, b=thb:
                    e.tensor_tensor(o, a, b, ALU.is_gt),
                    reads=[src_wb, th], writes=[g_], name=tag + "qg")
            sch.add("vector", lambda e, o=l_.rearrange("p (g k) -> p g k", k=64), a=v3, b=nthb:
                    e.tensor_tensor(o, a, b, ALU.is_lt),
                    reads=[src_wb, nth], writes=[l_], name=tag + "ql")
            sch.add("vector", lambda e, o=g_, a=g_, b=l_: e.tensor_sub(o, a, b),
                    reads=[g_, l_], writes=[g_], name=tag + "qd")

        def quant_final(dst_wh, g_buf, scb, G, tag):
            ng = G // 64
            sbf = scb[:, 0:ng]
            g_ = g_buf[:, 0:G]
            sbb = sbf.unsqueeze(2).to_broadcast([128, ng, 64])
            sch.add("vector", lambda e, o=dst_wh.rearrange("p (g k) -> p g k", k=64),
                    a=g_.rearrange("p (g k) -> p g k", k=64), b=sbb:
                    e.tensor_tensor(o, a, b, ALU.mult),
                    reads=[g_, sbf], writes=[dst_wh], name=tag + "qw")

        def quant(dst_wh, src_wb, g_buf, l_buf, scf, scb, G, tag):
            quant_prefix(src_wb, g_buf, l_buf, scf, scb, G, tag)
            quant_final(dst_wh, g_buf, scb, G, tag)

        def transpose_tiles(src_bf, dst_bf, tiles, tag, evac_view=None,
                            psum_perm=False):
            for gi, i in enumerate(range(0, len(tiles), 8)):
                bk = 5 if gi % 2 == 0 else 7
                grp = tiles[i:i + 8]
                for j, (scol, dcol) in enumerate(grp):
                    o = pool_bf[bk][:, j * 128:(j + 1) * 128]
                    iv = src_bf[:, scol:scol + 128]
                    sch.add("tensor", lambda e, o=o, i=iv: e.transpose(o, i, identb[:]),
                            reads=[iv, identb[:]], writes=[pool[bk][:, 0:512]],
                            name=tag + "tr")
                if evac_view is not None:
                    ov = evac_view(i)
                    pv = pool_bf[bk][:, 0:128 * len(grp)].rearrange(
                        "p (t j n) -> p t j n", t=2, n=128)
                else:
                    ov = dst_bf[:, grp[0][1]: grp[0][1] + 128 * len(grp)]
                    pv = pool_bf[bk][:, 0:128 * len(grp)]
                sch.add("scalar", lambda e, o=ov, i=pv: e.copy(o, i),
                        reads=[pool[bk][:, 0:512]], writes=[ov], name=tag + "te")

        def gu_load(hc):
            wb = (f1_wbA, f1_wbB)[hc % 2]
            sch.dma("gpsimd", wb[:, 0:2048], wgu[hc * 128:(hc + 1) * 128, :],
                    name=f"gw{hc}")
            sch.dma("gpsimd", wb[:, 2048:4096],
                    wgu[8192 + hc * 128: 8192 + (hc + 1) * 128, :], name=f"uw{hc}")

        def gu_q_pre(hc, part):
            wb = (f1_wbA, f1_wbB)[hc % 2]
            off = part * 2048
            quant_prefix(wb[:, off:off + 2048], f1_glb, l_r3b, f1_sc, f1_scb,
                         2048, "gu"[part])

        def gu_q_fin_tr(hc, part):
            off = part * 2048
            tag = "gu"[part]
            quant_final(f1_wh[:, off:off + 2048], f1_glb, f1_scb, 2048, tag)
            transpose_tiles(f1_wh, f1_wh,
                            [(off + i * 128, off + i * 128) for i in range(16)], tag)

        def gu_mm_half(hc, half):
            off = half * 2048
            for dc in range(DC):
                l = f1_wh[:, off + dc * 128: off + (dc + 1) * 128]
                for sw in range(NSW):
                    o = pool[sw][:, 0:SW]
                    r = xn[:, dc * S + sw * SW: dc * S + (sw + 1) * SW]
                    sch.add("tensor",
                            lambda e, o=o, l=l, r=r, st=(dc == 0), sp=(dc == DC - 1):
                            e.matmul(o, l, r, start=st, stop=sp, skip_group_check=True),
                            reads=[l, r], writes=[o], name="gu1mm")
            if half == 0:
                for sw in range(NSW):
                    o = f1_t1[:, sw * SW:(sw + 1) * SW]
                    i = pool[sw][:, 0:SW]
                    sch.add("scalar", lambda e, o=o, i=i: e.activation(o, i, AF.Silu),
                            reads=[i], writes=[o], name="gu1silu")
            else:
                for sw in range(NSW):
                    o = f1_hq[:, sw * SW:(sw + 1) * SW]
                    a = f1_t1[:, sw * SW:(sw + 1) * SW]
                    b = pool[sw][:, 0:SW]
                    sch.add("vector", lambda e, o=o, a=a, b=b: e.tensor_mul(o, a, b),
                            reads=[a, b], writes=[o], name="gu1h")

        def gu_tail(hc):
            sch.dma("sync", h_d[hc * 128:(hc + 1) * 128, :], f1_hq[:, 0:S], name=f"hst{hc}")
            hsq = f1_hq[:, 0:S]
            sch.add("scalar", lambda e, o=hsq, i=hsq: e.activation(o, i, AF.Square),
                    reads=[hsq], writes=[hsq], name="h1sq")
            for sw in range(NSW):
                p6r = pool[6][0:1, 0:SW]
                r = hsq[:, sw * SW:(sw + 1) * SW]
                sch.add("tensor", lambda e, o=p6r, r=r:
                        e.matmul(o, ones_bf[:], r, start=True, stop=True),
                        reads=[r, ones_bf[:]], writes=[p6r], name="st1mm")
                dstr = rms_row[0:1, sw * SW:(sw + 1) * SW]
                if hc == 0:
                    sch.add("vector", lambda e, o=dstr, i=p6r: e.tensor_copy(o, i),
                            reads=[p6r], writes=[dstr], name="st1cp")
                else:
                    sch.add("vector", lambda e, o=dstr, a=dstr, b=p6r: e.tensor_add(o, a, b),
                            reads=[p6r, dstr], writes=[dstr], name="st1ad")

        gu_load(0)
        gu_q_pre(0, 0)
        gu_q_fin_tr(0, 0)
        gu_q_pre(0, 1)
        gu_q_fin_tr(0, 1)
        for hc in range(HC):
            if hc + 1 < HC:
                gu_load(hc + 1)
                gu_q_pre(hc + 1, 0)
            gu_mm_half(hc, 0)
            if hc + 1 < HC:
                gu_q_fin_tr(hc + 1, 0)
            gu_mm_half(hc, 1)
            if hc + 1 < HC:
                gu_q_pre(hc + 1, 1)
                gu_q_fin_tr(hc + 1, 1)
            gu_tail(hc)
        rr = rms_row[0:1, 0:S]
        sch.add("vector", lambda e, o=rr, i=rr:
                e.tensor_scalar(o, i, 1.0 / 8192, None, ALU.mult),
                reads=[rr], writes=[rr], name="rmsm")
        sch.add("scalar", lambda e, o=rr, i=rr, b=ceps[0:1, 1:2]:
                e.activation(o, i, AF.Sqrt, bias=b),
                reads=[rr, ceps[0:1, 1:2]], writes=[rr], name="rmssq")
        sch.add("vector", lambda e, o=rr, i=rr: e.reciprocal(o, i),
                reads=[rr], writes=[rr], name="rmsrec")

        def dn_quant_raw(kg, g2):
            dst = dn_w[:].rearrange("p (c n) -> p c n", n=512)
            srcw = wp1[kg][g2 * 1024:(g2 + 1) * 1024, :].rearrange(
                "(c p) n -> p c n", p=128)
            sch.dma("gpsimd", dst, srcw, name=f"pw{kg}_{g2}")
            for t in range(8):
                dc = g2 * 8 + t
                quant(dn_w[:, t * 512:(t + 1) * 512], dn_w[:, t * 512:(t + 1) * 512],
                      dn_g, l_r3, dn_sc, dn_scb, 512, f"p{dc}")

        def dn_tr(kg, g2):
            tiles = []
            for t in range(8):
                dc = g2 * 8 + t
                for j in range(4):
                    tiles.append((t * 512 + j * 128, j * 2048 + dc * 128))
            pT4 = pT_sb[:].rearrange("p (j dc n) -> p dc j n", j=4, n=128)

            def dn_evac(i, g2=g2, pT4=pT4):
                dc0 = g2 * 8 + i // 4
                return pT4[:, dc0:dc0 + 2, :, :]
            transpose_tiles(dn_w, pT_sb, tiles, "p", evac_view=dn_evac,
                            psum_perm=True)

        dn_quant_raw(0, 0)
        dn_tr(0, 0)
        dn_quant_raw(0, 1)
        dn_tr(0, 1)
        for kg in range(16):
            for c in range(4):
                hcg = kg * 4 + c
                hb = (f1_wbA, f1_wbB)[c % 2][:, (c // 2) * 2048:(c // 2) * 2048 + S]
                sch.dma("sync", hb, h_d[hcg * 128:(hcg + 1) * 128, :], name=f"hld{hcg}")
                for sw in range(NSW):
                    p6b = pool[6][:, 0:SW]
                    irow = rms_row[0:1, sw * SW:(sw + 1) * SW]
                    sch.add("tensor", lambda e, o=p6b, r=irow:
                            e.matmul(o, ones_row[:], r, start=True, stop=True),
                            reads=[irow, ones_row[:]], writes=[p6b], name="invbc")
                    o = hn_sb[:, c * 2048 + sw * SW: c * 2048 + (sw + 1) * SW]
                    a = hb[:, sw * SW:(sw + 1) * SW]
                    sch.add("vector", lambda e, o=o, a=a, b=p6b: e.tensor_mul(o, a, b),
                            reads=[a, p6b], writes=[o], name="hn1")
            if kg + 1 < 16:
                dn_quant_raw(kg + 1, 0)
            for dc in range(DC):
                for swd in range(NSW):
                    bk = (dc * NSW + swd) % 4
                    for c in range(4):
                        l = pT_sb[:, c * 2048 + dc * 128: c * 2048 + (dc + 1) * 128]
                        o = pool[bk][:, 0:SW]
                        r = hn_sb[:, c * 2048 + swd * SW: c * 2048 + (swd + 1) * SW]
                        sch.add("tensor",
                                lambda e, o=o, l=l, r=r, st=(c == 0), sp=(c == 3):
                                e.matmul(o, l, r, start=st, stop=sp, skip_group_check=True),
                                reads=[l, r], writes=[o], name="dn1mm")
                    yv = y_l1[:, dc * S + swd * SW: dc * S + (swd + 1) * SW]
                    po = pool[bk][:, 0:SW]
                    if kg == 0:
                        sch.add("vector", lambda e, o=yv, i=po: e.tensor_copy(o, i),
                                reads=[po], writes=[yv], name="y1cp")
                    else:
                        sch.add("vector", lambda e, o=yv, a=yv, b=po: e.tensor_add(o, a, b),
                                reads=[po, yv], writes=[yv], name="y1ad")
            if kg + 1 < 16:
                dn_tr(kg + 1, 0)
                dn_quant_raw(kg + 1, 1)
                dn_tr(kg + 1, 1)

        for dc in range(DC):
            for sc in range(NSC):
                eb = sc % 2
                stile = ep_s[:, eb * 512:eb * 512 + SB]
                xtile = ep_x[:, eb * 512:eb * 512 + SB]
                gtile = ep_g[:, eb * 512:eb * 512 + SB]
                sch.dma("sync", stile, sTd[dc * 128:(dc + 1) * 128, sc * SB:(sc + 1) * SB],
                        name="ep1s")
                sch.dma("sync", xtile, x1T[dc * 128:(dc + 1) * 128, sc * SB:(sc + 1) * SB],
                        name="ep1x")
                yv = y_l1[:, dc * S + sc * SB: dc * S + (sc + 1) * SB]
                sch.add("scalar", lambda e, o=gtile, i=yv: e.activation(o, i, AF.Silu),
                        reads=[yv], writes=[gtile], name="ep1silu")
                sch.add("vector", lambda e, o=gtile, a=gtile, b=stile: e.tensor_mul(o, a, b),
                        reads=[gtile, stile], writes=[gtile], name="ep1mul")
                sch.add("vector", lambda e, o=xtile, a=xtile, b=gtile: e.tensor_add(o, a, b),
                        reads=[gtile, xtile], writes=[xtile], name="ep1add")
                sch.dma("sync", outT[dc * 128:(dc + 1) * 128, sc * SB:(sc + 1) * SB],
                        xtile, name="ep1st")

    xT = core.din("xT", [D, S], F32)
    if 0 in layers:
        emit_ln("xT", "ln0")
        if "xn" in dbg:
            o = core.dout("dbg_xn", [D, S], BF16)
            for dc in range(DC):
                sch.dma("sync", o[dc * 128:(dc + 1) * 128, :], xn[:, dc * S:(dc + 1) * S],
                        name="dbgxn")
        emit_attn0()
        emit_moe0()
    else:
        core.dram["x1T"] = core.din("x1T_in", [D, S], F32)
        core.dram["xT"] = core.dram["x1T"]

    if 1 in layers:
        emit_l1()
    else:
        x1o = core.dout("x1T_out", [D, S], F32)
        for dc in range(DC):
            sch.dma("sync", x1o[dc * 128:(dc + 1) * 128, :],
                    core.dram["x1T"][dc * 128:(dc + 1) * 128, :], name="x1out")

    if "sT" in dbg:
        o = core.dout("dbg_sT", [D, S], F32)
        for dc in range(DC):
            sch.dma("sync", o[dc * 128:(dc + 1) * 128, :],
                    core.dram["sT"][dc * 128:(dc + 1) * 128, :], name="dbgsT")

    with core.nc.Block() as blk:
        sch.emit(blk)
    return core


def prep_core_inputs(inputs, b, S, layers=(0, 1)):
    c = host_constants(S, min(512, S))
    m = dict(c)
    x = np.asarray(inputs["L0_attn_hidden_states"])[b][:S]
    m["xT"] = np.ascontiguousarray(x.T)
    if 0 in layers:
        wq = np.asarray(inputs["L0_attn_w_q"])
        m["wqT_t"] = np.ascontiguousarray(wq.T.reshape(D, NH, HD).transpose(1, 0, 2))
        wkv = np.asarray(inputs["L0_attn_w_kv"])
        m["wkvT_t"] = np.ascontiguousarray(wkv.T.reshape(D, 2, HD).transpose(1, 0, 2))
        wp = np.asarray(inputs["L0_attn_w_proj"])
        m["wprojT_t"] = np.ascontiguousarray(wp.T.reshape(D, DC, 128).transpose(1, 0, 2))
        m["bproj_col"] = np.ascontiguousarray(
            np.asarray(inputs["L0_attn_b_proj"]).reshape(16, 128).T)
        m["L0_ffn_W_g"] = np.asarray(inputs["L0_ffn_W_g"])
        m["ws_gate_t"] = np.ascontiguousarray(
            np.asarray(inputs["L0_ffn_Ws_gate"]).reshape(D, 2, 1024).transpose(1, 0, 2))
        m["ws_up_t"] = np.ascontiguousarray(
            np.asarray(inputs["L0_ffn_Ws_up"]).reshape(D, 2, 1024).transpose(1, 0, 2))
        m["L0_ffn_Ws_down"] = np.asarray(inputs["L0_ffn_Ws_down"])
        m["L0_ffn_We_gate"] = np.asarray(inputs["L0_ffn_We_gate"])
        m["L0_ffn_We_up"] = np.asarray(inputs["L0_ffn_We_up"])
        m["L0_ffn_We_down"] = np.asarray(inputs["L0_ffn_We_down"])
    if 1 in layers:
        wq = np.asarray(inputs["L1_attn_Wq"])
        m["wq1_t"] = np.ascontiguousarray(wq.reshape(D, NH, HD).transpose(1, 0, 2))
        wk = np.asarray(inputs["L1_attn_Wk"])
        m["wk1_t"] = np.ascontiguousarray(wk.reshape(D, NH, HD).transpose(1, 0, 2))
        wv = np.asarray(inputs["L1_attn_Wv"])
        m["wv1_t"] = np.ascontiguousarray(wv.reshape(D, 8, 256).transpose(1, 0, 2))
        wo = np.asarray(inputs["L1_attn_Wo"])
        m["wo1_t"] = np.ascontiguousarray(wo.reshape(D, DC, 128).transpose(1, 0, 2))
        m["lam_row"] = np.ascontiguousarray(
            np.asarray(inputs["L1_attn_lambda_init"]).reshape(1, 16))
        m["L1_ffn_w_gate_up"] = np.asarray(inputs["L1_ffn_w_gate_up"])
        m["wproj1_t"] = np.ascontiguousarray(
            np.asarray(inputs["L1_ffn_w_proj"]).reshape(D, 16, 512).transpose(1, 0, 2))
    return m


def kernel(**inputs):
    S = 2048
    core = build_program(S=S, layers=(0, 1))
    in_maps = []
    for b in range(8):
        m = prep_core_inputs(inputs, b, S)
        in_maps.append({k: np.ascontiguousarray(v) for k, v in m.items()
                        if k in core.used_inputs})
    missing = [k for k in core.used_inputs if k not in in_maps[0]]
    assert not missing, missing
    res = run_bass_kernel_spmd(core.nc, in_maps, core_ids=list(range(8)))
    outs = [np.ascontiguousarray(np.asarray(r["outT"]).T) for r in res.results]
    return np.stack(outs).astype(np.float32)


# version 14
# speedup vs baseline: 1.0426x; 1.0161x over previous
import math
import numpy as np
import ml_dtypes
import concourse.bass as bass
import concourse.mybir as mybir
from concourse.bass_utils import run_bass_kernel_spmd

F32 = mybir.dt.float32
BF16 = mybir.dt.bfloat16
AF = mybir.ActivationFunctionType
ALU = mybir.AluOpType
AX = mybir.AxisListType

D = 2048
NH = 16
HD = 128
DC = D // 128
EPS_LN = 1e-5
NEG_BIG = -1.0e30
BF = ml_dtypes.bfloat16

ENGINES = ("tensor", "vector", "scalar", "sync", "gpsimd")
DMA_QUEUES = ("sync", "gpsimd")
_ESZ = {F32: 4, BF16: 2, mybir.dt.int32: 4}
SB_BLK = 512
DR_BLK = 65536
_SB_OFF = {}


def ap_keys(ap):
    t = ap.tensor
    name = t.name
    dims = ap.ap
    sp = str(ap.space)
    if "PSUM" in sp:
        return [("ps", name.rsplit("_", 1)[0] if name not in _PS_NAMES else name)]
    esz = _ESZ[ap.dtype]
    if "DRAM" in sp.upper():
        off = ap.offset
        span = 1
        for st, cnt in dims:
            span += (cnt - 1) * st
        b0 = off // DR_BLK
        b1 = (off + span - 1) // DR_BLK
        return [("d", name, b) for b in range(b0, min(b1, b0 + 63) + 1)]
    pstep = dims[0][0] if dims else 1
    off = ap.offset % pstep if pstep > 0 else ap.offset
    span = 1
    for st, cnt in dims[1:]:
        span += (cnt - 1) * st
    base = _SB_OFF.get(name.rsplit("_", 1)[0], _SB_OFF.get(name, 0))
    lo = base + off * esz
    hi = base + (off + span) * esz
    b0 = lo // SB_BLK
    b1 = (hi - 1) // SB_BLK
    return [("s", b) for b in range(b0, b1 + 1)]


_PS_NAMES = set()
NLANES = 8


class Sched:
    def __init__(self, nc):
        self.nc = nc
        self.ops = []
        self.n_local = {e: 0 for e in ENGINES}
        self.n_dma = {q: 0 for q in DMA_QUEUES}
        self.last_writer = {}
        self.readers = {}
        self.clock = {e: {} for e in ENGINES}
        self.opclock = []

    def add(self, eng, fn, reads=(), writes=(), name="", dma=False):
        i = len(self.ops)
        rkeys = []
        for ap in reads:
            rkeys.extend(ap_keys(ap))
        wkeys = []
        for ap in writes:
            wkeys.extend(ap_keys(ap))
        deps = set()
        for k in rkeys:
            j = self.last_writer.get(k)
            if j is not None:
                deps.add(j)
        for k in wkeys:
            j = self.last_writer.get(k)
            if j is not None:
                deps.add(j)
            for j in self.readers.get(k, ()):
                deps.add(j)
        cl = self.clock[eng]
        lane = None
        pre_wait = None
        if dma:
            idx = self.n_dma[eng]
            lane = (eng, idx % NLANES)
            rnd = idx // NLANES + 1
            if rnd > 1:
                pre_wait = (lane, (rnd - 1) * 16)
                if cl.get(lane, -1) < rnd - 1:
                    cl[lane] = rnd - 1
            self.n_dma[eng] += 1
        need = []
        for j in sorted(deps):
            oj = self.ops[j]
            if cl.get(oj["ckey"], -1) >= oj["cpos"]:
                continue
            need.append(j)
        same_drain = False
        waits = []
        for j in need:
            oj = self.ops[j]
            if oj["eng"] == eng and not oj["dma"]:
                if eng == "tensor":
                    continue
                same_drain = True
            else:
                waits.append(j)
                oj["inc"] = True
        if same_drain:
            cl[eng] = self.n_local[eng] - 1
        for j in waits:
            for e2, v in self.opclock[j].items():
                if cl.get(e2, -1) < v:
                    cl[e2] = v
        ckey = lane if dma else eng
        cpos = (self.n_dma[eng] - 1) // NLANES + 1 if dma else self.n_local[eng]
        op = dict(idx=i, eng=eng, fn=fn, local=self.n_local[eng], dma=dma,
                  inc=dma, waits=waits, drain=same_drain, name=name,
                  ckey=ckey, cpos=cpos, pre_wait=pre_wait)
        self.ops.append(op)
        self.n_local[eng] += 1
        oc = dict(cl)
        oc[ckey] = cpos
        if dma:
            oc[eng] = op["local"]
        self.opclock.append(oc)
        for k in rkeys:
            self.readers.setdefault(k, []).append(i)
        for k in wkeys:
            self.last_writer[k] = i
            self.readers[k] = []
        return i

    def dma(self, queue, out, in_, name="d"):
        return self.add(queue,
                        lambda e, o=out, i_=in_: e.dma_start(out=o, in_=i_),
                        reads=[in_], writes=[out], name="dma:" + name, dma=True)

    def emit(self, blk, final_queue="sync"):
        nc = self.nc
        sems = {e: nc.alloc_semaphore(f"sem_{e}") for e in ENGINES}
        for q in DMA_QUEUES:
            for l in range(NLANES):
                sems[(q, l)] = nc.alloc_semaphore(f"dma_{q}_{l}")
        cnt = {e: 0 for e in ENGINES}
        for op in self.ops:
            if op["inc"]:
                if op["dma"]:
                    op["inc_val"] = op["cpos"] * 16
                    op["sem"] = op["ckey"]
                else:
                    cnt[op["eng"]] += 1
                    op["inc_val"] = cnt[op["eng"]]
                    op["sem"] = op["eng"]
        per_engine = {e: [] for e in ENGINES}
        for op in self.ops:
            per_engine[op["eng"]].append(op)

        def mk_body(eng_ops):
            def body(eh):
                for op in eng_ops:
                    if op["drain"]:
                        eh.drain()
                    if op["pre_wait"] is not None:
                        eh.wait_ge(sems[op["pre_wait"][0]], op["pre_wait"][1])
                    wmax = {}
                    for j in op["waits"]:
                        oj = self.ops[j]
                        sk = oj["sem"]
                        wmax[sk] = max(wmax.get(sk, 0), oj["inc_val"])
                    for key, v in wmax.items():
                        eh.wait_ge(sems[key], v)
                    inst = op["fn"](eh)
                    if op["inc"]:
                        assert inst is not None, op["name"]
                        inst.then_inc(sems[op["sem"]], 16 if op["dma"] else 1)
            return body

        for e in ENGINES:
            if per_engine[e]:
                getattr(blk, e)(mk_body(per_engine[e]))

        finals = {}
        for op in self.ops:
            if op.get("inc_val"):
                finals[op["sem"]] = max(finals.get(op["sem"], 0), op["inc_val"])

        def final_body(eh):
            for sk, v in finals.items():
                eh.wait_ge(sems[sk], v)
        getattr(blk, final_queue)(final_body)


def rotary_tables(S):
    dim = HD
    SCALE_BASE = 512
    inv_freq = 1.0 / 10000.0 ** (np.arange(0, dim, 2, dtype=np.float64) / dim)
    t = np.arange(S, dtype=np.float64)
    freqs = np.einsum("i,j->ij", t, inv_freq)
    freqs = np.concatenate((freqs, freqs), axis=-1)
    scale_base = (np.arange(0, dim, 2, dtype=np.float64) + 0.4 * dim) / (1.4 * dim)
    power = (t - S // 2) / SCALE_BASE
    scale = scale_base[None, :] ** power[:, None]
    scale = np.concatenate((scale, scale), axis=-1)
    pos32 = freqs.astype(np.float32).astype(np.float64)
    sc32 = scale.astype(np.float32).astype(np.float64)
    sw = 1.0 / math.sqrt(HD)
    Aq = (np.cos(pos32) * sc32 * sw).astype(np.float32).T.copy()
    Bq = (np.sin(pos32) * sc32 * sw).astype(np.float32).T.copy()
    Ak = (np.cos(pos32) / sc32 * sw).astype(np.float32).T.copy()
    Bk = (np.sin(pos32) / sc32 * sw).astype(np.float32).T.copy()
    return Aq, Bq, Ak, Bk


def rot_half_matrix():
    R = np.zeros((HD, HD), dtype=np.float32)
    for hp in range(64):
        R[hp + 64, hp] = -1.0
    for hp in range(64, HD):
        R[hp - 64, hp] = 1.0
    return np.asarray(R.astype(BF))


def causal_mask1(SBv):
    pj = np.arange(128)[:, None]
    i = np.arange(SBv)[None, :]
    m0 = np.where(pj > i, np.float32(NEG_BIG), np.float32(0.0))
    full = np.full((128, 384), np.float32(NEG_BIG))
    return np.concatenate([full, m0], axis=1).astype(BF)


def host_constants(S, SBv):
    Aq, Bq, Ak, Bk = rotary_tables(S)
    return {
        "c_rotAq": Aq, "c_rotBq": Bq, "c_rotAk": Ak, "c_rotBk": Bk,
        "c_rotR": rot_half_matrix(),
        "c_mask1": np.asarray(causal_mask1(SBv)),
        "c_ones_bf": np.ones((128, 1), dtype=BF),
        "c_ones_row": np.ones((1, 128), dtype=np.float32),
        "c_ones_row_bf": np.ones((1, 128), dtype=BF),
        "c_iota8": np.broadcast_to(np.arange(8, dtype=np.float32), (128, 8)).copy(),
        "c_ident_f32": np.eye(128, dtype=np.float32),
        "c_ident_bf": np.eye(128, dtype=BF),
        "c_eps": np.broadcast_to(np.array([1e-5, 1.1920929e-07], dtype=np.float32),
                                 (128, 2)).copy(),
    }


class Core:
    def __init__(self, S):
        self.S = S
        self.SB = min(512, S)
        self.NSC = S // self.SB
        self.JC = S // 128
        self.nc = bass.Bass("TRN2", target_bir_lowering=False)
        self.sch = Sched(self.nc)
        self.dram = {}
        self.sb = {}
        self.pool = [self.nc.alloc_psum_tensor(f"pp{i}", [128, 512], F32)
                     for i in range(8)]
        for p in self.pool:
            _PS_NAMES.add(p.name)
        self.pool_bf = [p.ap().bitcast(BF16) for p in self.pool]
        self.used_inputs = []
        self.base = self.nc.bump_sbuf(212352, align_bytes=512)[0]
        self._cursor = 0

    def din(self, name, shape, dt=F32):
        if name not in self.dram:
            self.dram[name] = self.nc.dram_tensor(name, list(shape), dt,
                                                  kind="ExternalInput").ap()
            self.used_inputs.append(name)
        return self.dram[name]

    def dout(self, name, shape, dt=F32):
        self.dram[name] = self.nc.dram_tensor(name, list(shape), dt,
                                              kind="ExternalOutput").ap()
        return self.dram[name]

    def dtmp(self, name, shape, dt=F32):
        if name not in self.dram:
            self.dram[name] = self.nc.dram_tensor(name, list(shape), dt,
                                                  kind="Internal").ap()
        return self.dram[name]

    def sb_at(self, name, shape, dt, offset):
        assert offset % 128 == 0, (name, offset)
        off = self.base + offset
        h = self.nc.alloc_sbuf_tensor_at(name, list(shape), dt, offset=off)
        self.sb[name] = h
        _SB_OFF[name] = off
        _SB_OFF[h.name] = off
        return h

    def alloc(self, name, shape, dt):
        n = shape[-1] * _ESZ[dt]
        for s in shape[1:-1]:
            n *= s
        n = (n + 127) // 128 * 128
        h = self.sb_at(name, shape, dt, self._cursor)
        self._cursor += n
        return h


def build_program(S=2048, layers=(0, 1), dbg=()):
    core = Core(S)
    nc, sch = core.nc, core.sch
    SB, NSC, JC = core.SB, core.NSC, core.JC
    pool = core.pool
    dbg = set(dbg)

    xn = core.sb_at("xn", [128, DC * S], BF16, 0)
    off_big = 32 * S
    heads = core.sb_at("heads", [128, DC * S], BF16, off_big)
    SH = S // 2
    y_moe = core.sb_at("y_moe", [128, DC * SH], F32, off_big)
    y_l1 = core.sb_at("y_l1", [128, DC * S], F32, 0)
    R3 = off_big + 32 * S
    rotA_q = core.sb_at("rotA_q", [128, S], F32, R3)
    rotB_q = core.sb_at("rotB_q", [128, S], F32, R3 + 4 * S)
    rotA_k = core.sb_at("rotA_k", [128, S], F32, R3 + 8 * S)
    rotB_k = core.sb_at("rotB_k", [128, S], F32, R3 + 12 * S)
    slabs = core.sb_at("slabs", [128, 16 * 1024], BF16, R3)
    pT_sb = core.sb_at("pT_sb", [128, 8192], BF16, R3)
    rms_row = core.sb_at("rms_row", [1, 2048], F32, R3 + 16384)
    l_r3 = core.sb_at("l_r3", [128, 1024], F32, R3 + 24576)
    l_r3b = core.sb_at("l_r3b", [128, 2048], BF16, R3 + 24576)
    dn_sc = core.sb_at("dn_sc", [128, 96], F32, R3 + 28672)
    dn_scb = core.sb_at("dn_scb", [128, 96], BF16, R3 + 29184)
    W = R3 + 32768
    ln_xa = core.sb_at("ln_xa", [128, 2 * S], F32, W)
    ln_xbf = core.sb_at("ln_xbf", [128, S], BF16, W + 8 * S)
    ln_xsq = core.sb_at("ln_xsq", [128, S], BF16, W + 10 * S)
    ln_rows = core.sb_at("ln_rows", [1, 2 * S], F32, W + 12 * S)
    ln_tile = core.sb_at("ln_tile", [128, 512], F32, W + 20 * S)
    at_k = core.sb_at("at_k", [128, S], BF16, W)
    at_v = core.sb_at("at_v", [128, S], BF16, W + 2 * S)
    l1_v2 = core.sb_at("l1_v2", [128, 2 * S], BF16, W + 2 * S)
    at_q = core.sb_at("at_q", [128, S], BF16, W + 4 * S)
    at_qr = core.sb_at("at_qr", [128, S], BF16, W + 6 * S)
    at_sq = core.sb_at("at_sq", [128, S], BF16, W + 4 * S)
    FX = W + 8 * S
    at_sh = core.sb_at("at_sh", [128, 1024], F32, FX)
    at_pt = core.sb_at("at_pt", [128, 1024], BF16, FX + 4096)
    at_negc = core.sb_at("at_negc", [128, 512], F32, FX + 6144)
    at_rb = core.sb_at("at_rb", [128, 512], F32, FX + 8192)
    at_rows = core.sb_at("at_rows", [1, 512], F32, FX + 10240)
    rows_bf = core.sb_at("rows_bf", [1, 1024], BF16, FX + 10240)
    at_rows2 = core.sb_at("at_rows2", [1, 512], F32, FX + 12288)
    at_wslab = core.sb_at("at_wslab", [128, 4096], BF16, FX + 14336)
    l1_tc1 = core.sb_at("l1_tc1", [128, 512], F32, FX + 22528)
    l1_tc2 = core.sb_at("l1_tc2", [128, 512], F32, FX + 24576)
    gh = core.sb_at("gh", [128, 8 * 1024], BF16, W)
    ep_g = core.sb_at("ep_g", [128, 1024], F32, W + 16384)
    ep_s = core.sb_at("ep_s", [128, 1024], F32, W + 20480)
    ep_x = core.sb_at("ep_x", [128, 1024], F32, W + 24576)
    web = core.sb_at("web", [128, 1024], F32, W + 28672)
    hf = core.sb_at("hf", [128, 1024], F32, W + 32768)
    gt_w = core.sb_at("gt_w", [128, 8], F32, W + 36864)
    gt_t = core.sb_at("gt_t", [128, 48], F32, W + 37376)
    gt_r = core.sb_at("gt_r", [128, 4], F32, W + 37888)
    wT8 = core.sb_at("wT8", [8, S], F32, W + 16384)
    f1_wbA = core.sb_at("f1_wbA", [128, 4096], BF16, W)
    f1_wbB = core.sb_at("f1_wbB", [128, 4096], BF16, W + 8192)
    f1_gl = core.sb_at("f1_gl", [128, 1024], F32, W + 16384)
    f1_glb = core.sb_at("f1_glb", [128, 2048], BF16, W + 16384)
    f1_wh = core.sb_at("f1_wh", [128, 4096], BF16, W + 20480)
    f1_t1 = core.sb_at("f1_t1", [128, 2048], F32, W + 28672)
    f1_hq = core.sb_at("f1_hq", [128, 2048], BF16, W + 36864)
    f1_sc = core.sb_at("f1_sc", [128, 96], F32, W + 40960)
    f1_scb = core.sb_at("f1_scb", [128, 96], BF16, W + 41472)
    dn_w = core.sb_at("dn_w", [128, 4096], BF16, W + 16384)
    hn_sb = core.sb_at("hn_sb", [128, 8192], BF16, W + 24576)
    dn_g = core.sb_at("dn_g", [128, 512], F32, W + 40960)
    CW = W + 43008
    core._cursor = CW
    mask1 = core.alloc("mask1", [128, 384 + SB], BF16)
    rotR = core.alloc("rotR", [128, 128], BF16)
    ident = core.alloc("ident", [128, 128], F32)
    identb = core.alloc("identb", [128, 128], BF16)
    ones_bf = core.alloc("ones_bf", [128, 1], BF16)
    ones_row = core.alloc("ones_row", [1, 128], F32)
    iota8 = core.alloc("iota8", [128, 8], F32)
    b_col = core.alloc("b_col", [128, 16], F32)
    wg_sb = core.alloc("wg_sb", [128, 128], BF16)
    lam_sb = core.alloc("lam_sb", [128, 32], F32)
    srow_sb = core.alloc("srow_sb", [1, 64], F32)
    ones_row_bf = core.alloc("ones_row_bf", [1, 128], BF16)
    ceps = core.alloc("ceps", [128, 2], F32)
    assert core._cursor <= 212352, core._cursor

    CONST_SHAPES = {
        "c_rotR": ([128, 128], BF16, rotR), "c_ident_f32": ([128, 128], F32, ident),
        "c_ident_bf": ([128, 128], BF16, identb), "c_ones_bf": ([128, 1], BF16, ones_bf),
        "c_ones_row": ([1, 128], F32, ones_row), "c_iota8": ([128, 8], F32, iota8),
        "c_eps": ([128, 2], F32, ceps), "c_mask1": ([128, 384 + SB], BF16, mask1),
        "c_ones_row_bf": ([1, 128], BF16, ones_row_bf),
    }
    for nm, (shp, dt, tile) in CONST_SHAPES.items():
        d = core.din(nm, shp, dt)
        sch.dma("sync", tile[:], d, name=nm)

    def mask_k(k):
        return mask1[:, 384 - 128 * k: 384 - 128 * k + SB]

    def xn_ap(dc, sc=None, s0=None, sl=None):
        if sc is not None:
            return xn[:, dc * S + sc * SB: dc * S + (sc + 1) * SB]
        return xn[:, dc * S + s0: dc * S + s0 + sl]

    def emit_ln(src_name, tag):
        src = core.dram[src_name]
        for dc in range(DC):
            buf = dc % 2
            xav = ln_xa[:, buf * S:(buf + 1) * S]
            sch.dma("sync", xav, src[dc * 128:(dc + 1) * 128, :], name=f"{tag}x{dc}")
            sch.add("vector", lambda e, o=ln_xbf[:], i=xav: e.tensor_copy(o, i),
                    reads=[xav], writes=[ln_xbf[:]], name="ln_cast")
            sch.add("scalar", lambda e, o=ln_xsq[:], i=xav: e.activation(o, i, AF.Square),
                    reads=[xav], writes=[ln_xsq[:]], name="ln_sq")
            for sc in range(NSC):
                o1 = pool[sc % 4][0:1, 0:SB]
                r1 = ln_xbf[:, sc * SB:(sc + 1) * SB]
                sch.add("tensor",
                        lambda e, o=o1, r=r1, st=(dc == 0), sp=(dc == DC - 1):
                        e.matmul(o, ones_bf[:], r, start=st, stop=sp, skip_group_check=True),
                        reads=[r1, ones_bf[:]], writes=[o1], name="ln_mm1")
                o2 = pool[4 + sc % 4][0:1, 0:SB]
                r2 = ln_xsq[:, sc * SB:(sc + 1) * SB]
                sch.add("tensor",
                        lambda e, o=o2, r=r2, st=(dc == 0), sp=(dc == DC - 1):
                        e.matmul(o, ones_bf[:], r, start=st, stop=sp, skip_group_check=True),
                        reads=[r2, ones_bf[:]], writes=[o2], name="ln_mm2")
        for sc in range(NSC):
            mrow = ln_rows[0:1, sc * SB:(sc + 1) * SB]
            vrow = ln_rows[0:1, S + sc * SB: S + (sc + 1) * SB]
            p1 = pool[sc % 4][0:1, 0:SB]
            p2 = pool[4 + sc % 4][0:1, 0:SB]
            sch.add("vector", lambda e, o=mrow, i=p1:
                    e.tensor_scalar(o, i, 1.0 / D, None, ALU.mult),
                    reads=[p1], writes=[mrow], name="ln_m")
            sch.add("vector", lambda e, o=vrow, i=p2:
                    e.tensor_scalar(o, i, 1.0 / D, None, ALU.mult),
                    reads=[p2], writes=[vrow], name="ln_v")
            tr = ln_tile[0:1, 0:SB]
            sch.add("vector", lambda e, o=tr, m=mrow: e.tensor_mul(o, m, m),
                    reads=[mrow], writes=[tr], name="ln_m2")
            sch.add("vector", lambda e, o=vrow, v=vrow, t=tr: e.tensor_sub(o, v, t),
                    reads=[vrow, tr], writes=[vrow], name="ln_var")
            sch.add("scalar", lambda e, o=vrow, i=vrow, b=ceps[0:1, 0:1]:
                    e.activation(o, i, AF.Sqrt, bias=b),
                    reads=[vrow, ceps[0:1, 0:1]], writes=[vrow], name="ln_sqrt")
            sch.add("vector", lambda e, o=vrow, i=vrow: e.reciprocal(o, i),
                    reads=[vrow], writes=[vrow], name="ln_recip")
        for sc in range(NSC):
            mrow = ln_rows[0:1, sc * SB:(sc + 1) * SB]
            vrow = ln_rows[0:1, S + sc * SB: S + (sc + 1) * SB]
            pm = pool[0][:, 0:SB]
            pi = pool[1][:, 0:SB]
            sch.add("tensor", lambda e, o=pm, r=mrow:
                    e.matmul(o, ones_row[:], r, start=True, stop=True),
                    reads=[mrow, ones_row[:]], writes=[pm], name="ln_bm")
            sch.add("tensor", lambda e, o=pi, r=vrow:
                    e.matmul(o, ones_row[:], r, start=True, stop=True),
                    reads=[vrow, ones_row[:]], writes=[pi], name="ln_bi")
            for dc in range(DC):
                buf = dc % 2
                xt = ln_xa[:, buf * S + sc * SB: buf * S + (sc + 1) * SB]
                sch.dma("sync", xt, src[dc * 128:(dc + 1) * 128, sc * SB:(sc + 1) * SB],
                        name=f"{tag}x2_{dc}_{sc}")
                tf = ln_tile[:, 0:SB]
                xo = xn_ap(dc, sc)
                sch.add("vector", lambda e, o=tf, a=xt, b=pm: e.tensor_sub(o, a, b),
                        reads=[xt, pm], writes=[tf], name="ln_n1")
                sch.add("vector", lambda e, o=xo, a=tf, b=pi: e.tensor_mul(o, a, b),
                        reads=[tf, pi], writes=[xo], name="ln_n2")

    def emit_attn0():
        for nm, tile in [("c_rotAq", rotA_q), ("c_rotBq", rotB_q),
                         ("c_rotAk", rotA_k), ("c_rotBk", rotB_k)]:
            d = core.din(nm, [128, S], F32)
            sch.dma("sync", tile[:], d, name=nm)
        wkv = core.din("wkvT_t", [2, D, 128], F32)
        wq_t = core.din("wqT_t", [NH, D, 128], F32)
        wp_t = core.din("wprojT_t", [DC, D, 128], F32)
        bpc = core.din("bproj_col", [128, 16], F32)
        sch.dma("sync", b_col[:], bpc, name="bcol")

        kslab = at_wslab[:, 0:2048]
        vslab = at_wslab[:, 2048:4096]
        sch.dma("gpsimd", kslab.rearrange("p (c n) -> p c n", n=128),
                wkv[0].rearrange("(c p) n -> p c n", p=128), name="wkvk")
        sch.dma("gpsimd", vslab.rearrange("p (c n) -> p c n", n=128),
                wkv[1].rearrange("(c p) n -> p c n", p=128), name="wkvv")

        def rotary(src_bf, dst_bf, tabA, tabB, tag):
            for sc in range(NSC):
                srcv = src_bf[:, sc * SB:(sc + 1) * SB]
                p3 = pool[3][:, 0:SB]
                sch.add("tensor", lambda e, o=p3, r=srcv:
                        e.matmul(o, rotR[:], r, start=True, stop=True),
                        reads=[srcv, rotR[:]], writes=[p3], name=tag + "rot")
                sh = at_sh[:, 0:SB]
                tb = tabB[:, sc * SB:(sc + 1) * SB]
                sch.add("vector", lambda e, o=sh, a=tb, b=p3: e.tensor_mul(o, a, b),
                        reads=[p3, tb], writes=[sh], name=tag + "rb")
                sh2 = at_sh[:, SB:2 * SB]
                ta = tabA[:, sc * SB:(sc + 1) * SB]
                sch.add("vector", lambda e, o=sh2, a=ta, b=srcv: e.tensor_mul(o, a, b),
                        reads=[srcv, ta], writes=[sh2], name=tag + "ra")
                dv = dst_bf[:, sc * SB:(sc + 1) * SB]
                sch.add("vector", lambda e, o=dv, a=sh2, b=sh: e.tensor_add(o, a, b),
                        reads=[sh, sh2], writes=[dv], name=tag + "rs")

        for sc in range(NSC):
            p2 = pool[2][:, 0:SB]
            for dc in range(DC):
                l = kslab[:, dc * 128:(dc + 1) * 128]
                r = xn_ap(dc, sc)
                sch.add("tensor", lambda e, o=p2, l=l, r=r, st=(dc == 0), sp=(dc == DC - 1):
                        e.matmul(o, l, r, start=st, stop=sp),
                        reads=[l, r], writes=[p2], name="kmm")
            kt = at_q[:, sc * SB:(sc + 1) * SB]
            sch.add("scalar", lambda e, o=kt, i=p2: e.activation(o, i, AF.Copy),
                    reads=[p2], writes=[kt], name="kevac")
        rotary(at_q, at_k, rotA_k, rotB_k, "k")
        sch.add("scalar", lambda e, o=at_sq[:], i=at_k[:]: e.activation(o, i, AF.Square),
                reads=[at_k[:]], writes=[at_sq[:]], name="ksq")
        for sc in range(NSC):
            p4 = pool[4][0:1, 0:SB]
            r = at_sq[:, sc * SB:(sc + 1) * SB]
            sch.add("tensor", lambda e, o=p4, r=r:
                    e.matmul(o, ones_bf[:], r, start=True, stop=True),
                    reads=[r, ones_bf[:]], writes=[p4], name="kn2")
            kd = srow_sb[0:1, 1 + sc:2 + sc]
            sch.add("vector", lambda e, o=kd, i=p4: e.reduce_max(o, i, AX.X),
                    reads=[p4], writes=[kd], name="kmax")
        kall = srow_sb[0:1, 1:1 + NSC]
        km = srow_sb[0:1, 0:1]
        sch.add("vector", lambda e, o=km, i=kall: e.reduce_max(o, i, AX.X),
                reads=[kall], writes=[km], name="kmaxall")
        sch.add("scalar", lambda e, o=km, i=km: e.activation(o, i, AF.Sqrt),
                reads=[km], writes=[km], name="kmaxsqrt")
        sch.add("vector", lambda e, o=km, i=km: e.tensor_scalar(o, i, -1.0, None, ALU.mult),
                reads=[km], writes=[km], name="kmaxneg")
        for g in range((JC + 3) // 4):
            n_in_g = min(4, JC - 4 * g)
            p5 = pool[5]
            for j in range(n_in_g):
                jc = 4 * g + j
                for dc in range(DC):
                    l = xn_ap(dc, s0=jc * 128, sl=128)
                    r = vslab[:, dc * 128:(dc + 1) * 128]
                    o = p5[:, j * 128:(j + 1) * 128]
                    sch.add("tensor", lambda e, o=o, l=l, r=r, st=(dc == 0), sp=(dc == DC - 1):
                            e.matmul(o, l, r, start=st, stop=sp, skip_group_check=True),
                            reads=[l, r], writes=[o], name="vmm")
            vv = at_v[:, g * 512:g * 512 + n_in_g * 128]
            pv = p5[:, 0:n_in_g * 128]
            sch.add("vector", lambda e, o=vv, i=pv: e.tensor_copy(o, i),
                    reads=[pv], writes=[vv], name="vevac")
        for n in range(NH):
            qb = n % 2
            qslab = at_wslab[:, qb * 2048:(qb + 1) * 2048]
            sch.dma("gpsimd", qslab.rearrange("p (c n) -> p c n", n=128),
                    wq_t[n].rearrange("(c p) n -> p c n", p=128), name=f"wq{n}")
            for sc in range(NSC):
                p6 = pool[6][:, 0:SB]
                for dc in range(DC):
                    l = qslab[:, dc * 128:(dc + 1) * 128]
                    r = xn_ap(dc, sc)
                    sch.add("tensor", lambda e, o=p6, l=l, r=r, st=(dc == 0), sp=(dc == DC - 1):
                            e.matmul(o, l, r, start=st, stop=sp),
                            reads=[l, r], writes=[p6], name="qmm")
                qt = at_q[:, sc * SB:(sc + 1) * SB]
                sch.add("scalar", lambda e, o=qt, i=p6: e.activation(o, i, AF.Copy),
                        reads=[p6], writes=[qt], name="qevac")
            rotary(at_q, at_qr, rotA_q, rotB_q, "q")
            sch.add("scalar", lambda e, o=at_sq[:], i=at_qr[:]: e.activation(o, i, AF.Square),
                    reads=[at_qr[:]], writes=[at_sq[:]], name="qsq")
            pairs = [tuple(x for x in (a, a + 1) if x < NSC) for a in range(0, NSC, 2)]
            for pair in pairs:
                ncrows = [rows_bf[0:1, 0:SB], rows_bf[0:1, 512:512 + SB]]
                for pi, ib in enumerate(pair):
                    p4r = pool[4][0:1, 0:SB]
                    r = at_sq[:, ib * SB:(ib + 1) * SB]
                    sch.add("tensor", lambda e, o=p4r, r=r:
                            e.matmul(o, ones_bf[:], r, start=True, stop=True),
                            reads=[r, ones_bf[:]], writes=[p4r], name="qn2")
                    nrow = at_rows2[0:1, 0:SB]
                    sch.add("scalar", lambda e, o=nrow, i=p4r: e.activation(o, i, AF.Sqrt),
                            reads=[p4r], writes=[nrow], name="qn")
                    sch.add("vector", lambda e, o=nrow, i=nrow, s=km:
                            e.tensor_scalar(o, i, s, None, ALU.mult),
                            reads=[nrow, km], writes=[nrow], name="negc_row")
                    sch.add("vector", lambda e, o=ncrows[pi], i=nrow: e.tensor_copy(o, i),
                            reads=[nrow], writes=[ncrows[pi]], name="negc_bfc")
                lrows = [pool[2][0:1, 0:SB], pool[4][0:1, 0:SB]]
                OBs = [pool[3][:, 0:SB], pool[5][:, 0:SB]]
                jmaxs = [(ib + 1) * (SB // 128) for ib in pair]

                def emit_sexp0(pi, jc, slot, pair=pair, ncrows=ncrows):
                    ib = pair[pi]
                    sb_ = slot % 2
                    SBANK = pool[sb_][:, 0:SB]
                    ncr = ncrows[pi]
                    sch.add("tensor", lambda e, o=SBANK, r=ncr:
                            e.matmul(o, ones_row_bf[:], r, start=True, stop=False,
                                     skip_group_check=True),
                            reads=[ncr, ones_row_bf[:]], writes=[SBANK], name="sr1")
                    l = at_k[:, jc * 128:(jc + 1) * 128]
                    r = at_qr[:, ib * SB:(ib + 1) * SB]
                    sch.add("tensor", lambda e, o=SBANK, l=l, r=r:
                            e.matmul(o, l, r, start=False, stop=True,
                                     skip_group_check=True),
                            reads=[l, r], writes=[SBANK], name="smm")
                    pt = at_pt[:, sb_ * SB:(sb_ + 1) * SB]
                    diag_k = jc - ib * (SB // 128)
                    if diag_k >= 0:
                        sh = at_sh[:, sb_ * SB:(sb_ + 1) * SB]
                        mk = mask_k(diag_k)
                        sch.add("vector", lambda e, o=sh, a=SBANK, b=mk: e.tensor_add(o, a, b),
                                reads=[SBANK, mk], writes=[sh], name="smask")
                        sch.add("scalar", lambda e, o=pt, i=sh: e.activation(o, i, AF.Exp),
                                reads=[sh], writes=[pt], name="sexp")
                    else:
                        sch.add("scalar", lambda e, o=pt, i=SBANK: e.activation(o, i, AF.Exp),
                                reads=[SBANK], writes=[pt], name="sexp")

                def emit_lo0(pi, jc, slot, jmaxs=jmaxs, lrows=lrows, OBs=OBs):
                    sb_ = slot % 2
                    pt = at_pt[:, sb_ * SB:(sb_ + 1) * SB]
                    lr_ = lrows[pi]
                    sch.add("tensor", lambda e, o=lr_, r=pt, st=(jc == 0),
                            sp=(jc == jmaxs[pi] - 1):
                            e.matmul(o, ones_bf[:], r, start=st, stop=sp,
                                     skip_group_check=True),
                            reads=[pt, ones_bf[:]], writes=[lr_], name="lmm")
                    p3o = OBs[pi]
                    lv = at_v[:, jc * 128:(jc + 1) * 128]
                    sch.add("tensor", lambda e, o=p3o, l=lv, r=pt, st=(jc == 0),
                            sp=(jc == jmaxs[pi] - 1):
                            e.matmul(o, l, r, start=st, stop=sp, skip_group_check=True),
                            reads=[pt, lv], writes=[p3o], name="omm")

                items = []
                mx = max(jmaxs)
                for jc in range(mx):
                    for pi in range(len(pair)):
                        if jc < jmaxs[pi]:
                            items.append((pi, jc))
                for s in range(len(items) + 2):
                    if s >= 2:
                        emit_lo0(items[s - 2][0], items[s - 2][1], s - 2)
                    if s < len(items):
                        emit_sexp0(items[s][0], items[s][1], s)
                for pi, ib in enumerate(pair):
                    rrow = at_rows2[0:1, 0:SB]
                    sch.add("vector", lambda e, o=rrow, i=lrows[pi]:
                            e.tensor_scalar(o, i, 1e-30, None, ALU.add),
                            reads=[lrows[pi]], writes=[rrow], name="laddeps")
                    sch.add("vector", lambda e, o=rrow, i=rrow: e.reciprocal(o, i),
                            reads=[rrow], writes=[rrow], name="lrecip")
                    p6b = pool[6][:, 0:SB]
                    sch.add("tensor", lambda e, o=p6b, r=rrow:
                            e.matmul(o, ones_row[:], r, start=True, stop=True),
                            reads=[rrow, ones_row[:]], writes=[p6b], name="rbc")
                    rbv = at_rb[:, 0:SB]
                    sch.add("vector", lambda e, o=rbv, i=p6b: e.tensor_copy(o, i),
                            reads=[p6b], writes=[rbv], name="rbsb")
                    ho = heads[:, n * S + ib * SB: n * S + (ib + 1) * SB]
                    sch.add("vector", lambda e, o=ho, a=OBs[pi], b=rbv: e.tensor_mul(o, a, b),
                            reads=[OBs[pi], rbv], writes=[ho], name="oscale")

        sT = core.dtmp("sT", [D, S], F32)
        for dc in range(DC):
            wb = dc % 2
            wslab = at_wslab[:, wb * 2048:(wb + 1) * 2048]
            sch.dma("gpsimd", wslab.rearrange("p (c n) -> p c n", n=128),
                    wp_t[dc].rearrange("(c p) n -> p c n", p=128), name=f"wp{dc}")
            for sc in range(NSC):
                bank = pool[5 + sc % 2][:, 0:SB]
                for oc in range(NH):
                    l = wslab[:, oc * 128:(oc + 1) * 128]
                    r = heads[:, oc * S + sc * SB: oc * S + (sc + 1) * SB]
                    sch.add("tensor", lambda e, o=bank, l=l, r=r, st=(oc == 0), sp=(oc == NH - 1):
                            e.matmul(o, l, r, start=st, stop=sp),
                            reads=[l, r], writes=[bank], name="pmm")
                eb = sc % 2
                ev = at_sh[:, eb * SB:(eb + 1) * SB]
                bc = b_col[:, dc:dc + 1]
                sch.add("scalar", lambda e, o=ev, i=bank, b=bc: e.add(o, i, b),
                        reads=[bank, bc], writes=[ev], name="pevac")
                sch.dma("sync", sT[dc * 128:(dc + 1) * 128, sc * SB:(sc + 1) * SB], ev,
                        name=f"sTw{dc}_{sc}")

    def emit_moe0():
        wg = core.din("L0_ffn_W_g", [D, 8], F32)
        sch.dma("gpsimd", wg_sb[:].rearrange("p (c n) -> p c n", n=8),
                wg.rearrange("(c p) n -> p c n", p=128), name="wg")
        gwT = core.dtmp("gw8", [8, S], F32)
        xT = core.dram["xT"]
        sTd = core.dram["sT"]
        x1T = core.dtmp("x1T", [D, S], F32)
        for tc in range(S // 128):
            p0 = pool[0][:, 0:8]
            for dc in range(DC):
                l = xn_ap(dc, s0=tc * 128, sl=128)
                r = wg_sb[:, dc * 8:(dc + 1) * 8]
                sch.add("tensor", lambda e, o=p0, l=l, r=r, st=(dc == 0), sp=(dc == DC - 1):
                        e.matmul(o, l, r, start=st, stop=sp),
                        reads=[l, r], writes=[p0], name="gmm")
            t = [gt_t[:, k * 8:(k + 1) * 8] for k in range(6)]
            ga = t[5]
            r4 = gt_r
            sch.add("vector", lambda e, o=ga, i=p0: e.tensor_copy(o, i),
                    reads=[p0], writes=[ga], name="gcp")
            sch.add("vector", lambda e, o=r4[:, 0:1], i=ga:
                    e.reduce_max(o, i, AX.X, negate=True),
                    reads=[ga], writes=[r4[:, 0:1]], name="gmax1")
            sch.add("scalar", lambda e, o=t[0], i=ga, b=r4[:, 0:1]:
                    e.activation(o, i, AF.Exp, bias=b),
                    reads=[ga, r4[:, 0:1]], writes=[t[0]], name="gexp")
            sch.add("vector", lambda e, o=r4[:, 1:2], i=t[0]:
                    e.tensor_reduce(o, i, AX.X, ALU.add),
                    reads=[t[0]], writes=[r4[:, 1:2]], name="gsum")
            sch.add("vector", lambda e, o=r4[:, 1:2], i=r4[:, 1:2]: e.reciprocal(o, i),
                    reads=[r4[:, 1:2]], writes=[r4[:, 1:2]], name="grec")
            sch.add("vector", lambda e, o=t[1], i=ga: e.tensor_scalar(o, i, -1.0, None, ALU.mult),
                    reads=[ga], writes=[t[1]], name="gneg")
            sch.add("vector", lambda e, o=t[2], i=t[1], s=r4[:, 0:1]:
                    e.tensor_scalar(o, i, s, None, ALU.is_equal),
                    reads=[t[1], r4[:, 0:1]], writes=[t[2]], name="geq1")
            sch.add("vector", lambda e, o=t[3], a=t[2], b=iota8[:]: e.tensor_mul(o, a, b),
                    reads=[t[2], iota8[:]], writes=[t[3]], name="gkey")
            sch.add("vector", lambda e, o=t[3], a=t[2], b=t[3]:
                    e.scalar_tensor_tensor(o, a, -8.0, b, ALU.mult, ALU.add),
                    reads=[t[2], t[3]], writes=[t[3]], name="gkey2")
            sch.add("vector", lambda e, o=t[3], i=t[3]: e.tensor_scalar(o, i, 8.0, None, ALU.add),
                    reads=[t[3]], writes=[t[3]], name="gkey3")
            sch.add("vector", lambda e, o=r4[:, 2:3], i=t[3]:
                    e.tensor_reduce(o, i, AX.X, ALU.min),
                    reads=[t[3]], writes=[r4[:, 2:3]], name="gi1")
            sch.add("vector", lambda e, o=t[2], i=iota8[:], s=r4[:, 2:3]:
                    e.tensor_scalar(o, i, s, None, ALU.is_equal),
                    reads=[iota8[:], r4[:, 2:3]], writes=[t[2]], name="gsel1")
            sch.add("vector", lambda e, o=t[4], a=t[2], b=ga:
                    e.scalar_tensor_tensor(o, a, -1e9, b, ALU.mult, ALU.add),
                    reads=[t[2], ga], writes=[t[4]], name="gmask")
            sch.add("vector", lambda e, o=r4[:, 3:4], i=t[4]:
                    e.reduce_max(o, i, AX.X, negate=True),
                    reads=[t[4]], writes=[r4[:, 3:4]], name="gmax2")
            sch.add("vector", lambda e, o=t[4], i=t[4]:
                    e.tensor_scalar(o, i, -1.0, None, ALU.mult),
                    reads=[t[4]], writes=[t[4]], name="gneg2")
            sch.add("vector", lambda e, o=t[1], i=t[4], s=r4[:, 3:4]:
                    e.tensor_scalar(o, i, s, None, ALU.is_equal),
                    reads=[t[4], r4[:, 3:4]], writes=[t[1]], name="geq2")
            sch.add("vector", lambda e, o=t[4], a=t[1], b=iota8[:]: e.tensor_mul(o, a, b),
                    reads=[t[1], iota8[:]], writes=[t[4]], name="gkey4")
            sch.add("vector", lambda e, o=t[4], a=t[1], b=t[4]:
                    e.scalar_tensor_tensor(o, a, -8.0, b, ALU.mult, ALU.add),
                    reads=[t[1], t[4]], writes=[t[4]], name="gkey5")
            sch.add("vector", lambda e, o=t[4], i=t[4]: e.tensor_scalar(o, i, 8.0, None, ALU.add),
                    reads=[t[4]], writes=[t[4]], name="gkey6")
            sch.add("vector", lambda e, o=r4[:, 2:3], i=t[4]:
                    e.tensor_reduce(o, i, AX.X, ALU.min),
                    reads=[t[4]], writes=[r4[:, 2:3]], name="gi2")
            sch.add("vector", lambda e, o=t[1], i=iota8[:], s=r4[:, 2:3]:
                    e.tensor_scalar(o, i, s, None, ALU.is_equal),
                    reads=[iota8[:], r4[:, 2:3]], writes=[t[1]], name="gsel2")
            sch.add("vector", lambda e, o=t[2], a=t[2], b=t[1]: e.tensor_add(o, a, b),
                    reads=[t[2], t[1]], writes=[t[2]], name="gselb")
            sch.add("vector", lambda e, o=t[0], a=t[0], b=t[2]: e.tensor_mul(o, a, b),
                    reads=[t[0], t[2]], writes=[t[0]], name="gwm")
            sch.add("vector", lambda e, o=gt_w[:], i=t[0], s=r4[:, 1:2]:
                    e.tensor_scalar(o, i, s, None, ALU.mult),
                    reads=[t[0], r4[:, 1:2]], writes=[gt_w[:]], name="gwf")
            p1t = pool[1][0:8, 0:128]
            sch.add("tensor", lambda e, o=p1t, i=gt_w[:]: e.transpose(o, i, ident[:]),
                    reads=[gt_w[:], ident[:]], writes=[pool[1][:, 0:128]], name="gtr")
            wo = wT8[0:8, tc * 128:(tc + 1) * 128]
            sch.add("vector", lambda e, o=wo, i=p1t: e.tensor_copy(o, i),
                    reads=[pool[1][:, 0:128]], writes=[wo], name="gtrev")
        sch.dma("sync", gwT, wT8[:], name="gw8")

        Wsg = core.din("ws_gate_t", [2, D, 1024], F32)
        Wsu = core.din("ws_up_t", [2, D, 1024], F32)
        Wsd = core.din("L0_ffn_Ws_down", [D, D], F32)
        Weg = core.din("L0_ffn_We_gate", [8, D, 1024], F32)
        Weu = core.din("L0_ffn_We_up", [8, D, 1024], F32)
        Wed = core.din("L0_ffn_We_down", [8, 1024, D], F32)
        SHW = min(512, SH)
        NB = SH // SHW
        HSLB = 8192

        def load_gu_half(mat_ap, g, buf, what):
            for q in range(2):
                dst = slabs[:, buf * HSLB + q * 4096: buf * HSLB + (q + 1) * 4096]
                dst = dst.rearrange("p (c n) -> p c n", n=512)
                srcv = mat_ap[q * 1024:(q + 1) * 1024, g * 512:(g + 1) * 512]
                srcv = srcv.rearrange("(c p) n -> p c n", p=128)
                sch.dma("gpsimd", dst, srcv, name=f"m{what}{g}{q}")

        def load_dn_half(mat_ap, g, buf, what):
            for q in range(2):
                dst = slabs[:, buf * HSLB + q * 4096: buf * HSLB + (q + 1) * 4096]
                dst = dst.rearrange("p (c n) -> p c n", n=1024)
                srcv = mat_ap[q * 512:(q + 1) * 512, g * 1024:(g + 1) * 1024]
                srcv = srcv.rearrange("(c p) n -> p c n", p=128)
                sch.dma("gpsimd", dst, srcv, name=f"md{what}{g}{q}")

        def moe_gu_half(kind, g, buf, s0, shared):
            for dc in range(DC):
                for eo_l in range(4):
                    l = slabs[:, buf * HSLB + dc * 512 + eo_l * 128:
                              buf * HSLB + dc * 512 + (eo_l + 1) * 128]
                    for sw in range(NB):
                        o = pool[eo_l * NB + sw][:, 0:SHW]
                        r = xn_ap(dc, s0=s0 + sw * SHW, sl=SHW)
                        sch.add("tensor",
                                lambda e, o=o, l=l, r=r, st=(dc == 0), sp=(dc == DC - 1):
                                e.matmul(o, l, r, start=st, stop=sp, skip_group_check=True),
                                reads=[l, r], writes=[o], name="moe" + kind)
            for eo_l in range(4):
                eo = 4 * g + eo_l
                for sw in range(NB):
                    ghs = gh[:, eo * SH + sw * SHW: eo * SH + (sw + 1) * SHW]
                    pu = pool[eo_l * NB + sw][:, 0:SHW]
                    if kind == "g":
                        sch.add("scalar", lambda e, o=ghs, i=pu: e.activation(o, i, AF.Silu),
                                reads=[pu], writes=[ghs], name="moesilu")
                    elif shared:
                        sch.add("vector", lambda e, o=ghs, a=ghs, b=pu: e.tensor_mul(o, a, b),
                                reads=[pu, ghs], writes=[ghs], name="moeh")
                    else:
                        hfv = hf[:, sw * SHW:(sw + 1) * SHW]
                        sch.add("vector", lambda e, o=hfv, a=ghs, b=pu: e.tensor_mul(o, a, b),
                                reads=[pu, ghs], writes=[hfv], name="moeh1")
                        wv2 = web[:, sw * SHW:(sw + 1) * SHW]
                        sch.add("vector", lambda e, o=ghs, a=hfv, b=wv2: e.tensor_mul(o, a, b),
                                reads=[hfv, wv2], writes=[ghs], name="moeh2")

        def moe_dn_half(g, buf, s0, first):
            for dc in range(8 * g, 8 * g + 8):
                d_l = dc - 8 * g
                bank = (dc % 2) * NB
                for eo in range(8):
                    l = slabs[:, buf * HSLB + eo * 1024 + d_l * 128:
                              buf * HSLB + eo * 1024 + (d_l + 1) * 128]
                    for sw in range(NB):
                        o = pool[bank + sw][:, 0:SHW]
                        r = gh[:, eo * SH + sw * SHW: eo * SH + (sw + 1) * SHW]
                        sch.add("tensor",
                                lambda e, o=o, l=l, r=r, st=(eo == 0), sp=(eo == 7):
                                e.matmul(o, l, r, start=st, stop=sp, skip_group_check=True),
                                reads=[l, r], writes=[o], name="moed")
                for sw in range(NB):
                    yv = y_moe[:, dc * SH + sw * SHW: dc * SH + (sw + 1) * SHW]
                    po = pool[bank + sw][:, 0:SHW]
                    if first:
                        sch.add("vector", lambda e, o=yv, i=po: e.tensor_copy(o, i),
                                reads=[po], writes=[yv], name="ycp")
                    else:
                        sch.add("vector", lambda e, o=yv, a=yv, b=po: e.tensor_add(o, a, b),
                                reads=[po, yv], writes=[yv], name="yadd")

        for half in range(2):
            s0 = half * SH
            tasks = []
            for pe in range(10):
                shared = pe < 2
                if shared:
                    gate_src, up_src = Wsg[pe], Wsu[pe]
                    down_src = Wsd[pe * 1024:(pe + 1) * 1024, :]
                else:
                    gate_src, up_src = Weg[pe - 2], Weu[pe - 2]
                    down_src = Wed[pe - 2]
                for g in range(2):
                    tasks.append(("g", gate_src, g, pe, shared))
                for g in range(2):
                    tasks.append(("u", up_src, g, pe, shared))
                for g in range(2):
                    tasks.append(("d", down_src, g, pe, shared))

            def load_task(i):
                kind, mat, g, pe, shared = tasks[i]
                if kind == "d":
                    load_dn_half(mat, g, i % 2, f"{pe}")
                else:
                    load_gu_half(mat, g, i % 2, f"{kind}{pe}")

            load_task(0)
            for i, (kind, mat, g, pe, shared) in enumerate(tasks):
                if pe >= 2 and kind == "g" and g == 0:
                    er = pe - 2
                    wv = web[:, 0:SH]
                    sch.dma("sync", wv,
                            gwT[er:er + 1, s0:s0 + SH].partition_broadcast(128),
                            name=f"web{er}_{half}")
                if i + 1 < len(tasks):
                    load_task(i + 1)
                if kind == "d":
                    moe_dn_half(g, i % 2, s0, pe == 0)
                else:
                    moe_gu_half(kind, g, i % 2, s0, shared)
            for dc in range(DC):
                for sw in range(NB):
                    eb = sw % 2
                    stile = ep_s[:, eb * 512:eb * 512 + SHW]
                    xtile = ep_x[:, eb * 512:eb * 512 + SHW]
                    gtile = ep_g[:, eb * 512:eb * 512 + SHW]
                    sch.dma("sync", stile,
                            sTd[dc * 128:(dc + 1) * 128, s0 + sw * SHW:s0 + (sw + 1) * SHW],
                            name="eplds")
                    sch.dma("sync", xtile,
                            xT[dc * 128:(dc + 1) * 128, s0 + sw * SHW:s0 + (sw + 1) * SHW],
                            name="epldx")
                    yv = y_moe[:, dc * SH + sw * SHW: dc * SH + (sw + 1) * SHW]
                    sch.add("scalar", lambda e, o=gtile, i=yv: e.activation(o, i, AF.Silu),
                            reads=[yv], writes=[gtile], name="epsilu")
                    sch.add("vector", lambda e, o=gtile, a=gtile, b=stile: e.tensor_mul(o, a, b),
                            reads=[gtile, stile], writes=[gtile], name="epmul")
                    sch.add("vector", lambda e, o=xtile, a=xtile, b=gtile: e.tensor_add(o, a, b),
                            reads=[gtile, xtile], writes=[xtile], name="epadd")
                    sch.dma("sync",
                            x1T[dc * 128:(dc + 1) * 128, s0 + sw * SHW:s0 + (sw + 1) * SHW],
                            xtile, name="epst")

    def emit_l1():
        pool_bf = core.pool_bf
        emit_ln("x1T", "ln1")
        wq_t1 = core.din("wq1_t", [NH, D, 128], F32)
        wk_t1 = core.din("wk1_t", [NH, D, 128], F32)
        wv_t1 = core.din("wv1_t", [8, D, 256], F32)
        wo_t1 = core.din("wo1_t", [DC, D, 128], F32)
        lam_d = core.din("lam_row", [1, 16], F32)
        sTd = core.dram["sT"]
        x1T = core.dram["x1T"]
        outT = core.dout("outT", [D, S], F32)

        lr = lam_sb[0:1, 0:16]
        sch.dma("sync", lr, lam_d, name="lam")
        sch.add("scalar", lambda e, o=lr, i=lr: e.activation(o, i, AF.Sigmoid),
                reads=[lr], writes=[lr], name="lamsig")
        sch.add("vector", lambda e, o=lr, i=lr: e.tensor_scalar(o, i, -1.0, None, ALU.mult),
                reads=[lr], writes=[lr], name="lamneg")
        pl = pool[7][:, 0:16]
        sch.add("tensor", lambda e, o=pl, r=lr: e.matmul(o, ones_row[:], r, start=True, stop=True),
                reads=[lr, ones_row[:]], writes=[pool[7][:, 0:512]], name="lambc")
        nlam = lam_sb[:, 16:32]
        sch.add("vector", lambda e, o=nlam, i=pl: e.tensor_copy(o, i),
                reads=[pool[7][:, 0:512]], writes=[nlam], name="lamsb")

        for pr in range(NH // 2):
            vslab = at_wslab[:, 0:4096]
            sch.dma("gpsimd", vslab.rearrange("p (c n) -> p c n", n=256),
                    wv_t1[pr].rearrange("(c p) n -> p c n", p=128), name=f"wv{pr}")
            for jc in range(JC):
                pv = pool[6 + jc % 2][:, 0:256]
                for dc in range(DC):
                    l = xn_ap(dc, s0=jc * 128, sl=128)
                    r = vslab[:, dc * 256:(dc + 1) * 256]
                    sch.add("tensor", lambda e, o=pv, l=l, r=r, st=(dc == 0), sp=(dc == DC - 1):
                            e.matmul(o, l, r, start=st, stop=sp, skip_group_check=True),
                            reads=[l, r], writes=[pv], name="v1mm")
                vv = l1_v2[:, jc * 256:(jc + 1) * 256]
                sch.add("vector", lambda e, o=vv, i=pv: e.tensor_copy(o, i),
                        reads=[pv], writes=[vv], name="v1evac")
            for hp in range(2):
                n = 2 * pr + hp
                for wi, (wt, dst) in enumerate(((wq_t1, at_qr), (wk_t1, at_k))):
                    slab = at_wslab[:, wi * 2048:(wi + 1) * 2048]
                    sch.dma("gpsimd", slab.rearrange("p (c n) -> p c n", n=128),
                            wt[n].rearrange("(c p) n -> p c n", p=128), name=f"w1_{n}_{wi}")
                    for sc in range(NSC):
                        p6 = pool[6][:, 0:SB]
                        for dc in range(DC):
                            l = slab[:, dc * 128:(dc + 1) * 128]
                            r = xn_ap(dc, sc)
                            sch.add("tensor",
                                    lambda e, o=p6, l=l, r=r, st=(dc == 0), sp=(dc == DC - 1):
                                    e.matmul(o, l, r, start=st, stop=sp),
                                    reads=[l, r], writes=[p6], name="qk1mm")
                        dv = dst[:, sc * SB:(sc + 1) * SB]
                        if wi == 0:
                            sch.add("scalar", lambda e, o=dv, i=p6:
                                    e.activation(o, i, AF.Copy, scale=0.125),
                                    reads=[p6], writes=[dv], name="q1evac")
                        else:
                            sch.add("scalar", lambda e, o=dv, i=p6: e.activation(o, i, AF.Copy),
                                    reads=[p6], writes=[dv], name="k1evac")
                sqc = at_pt[:, 0:512]
                km2 = [srow_sb[0:1, 8 + 2 * h:9 + 2 * h] for h in range(2)]
                tmp = srow_sb[0:1, 12:13]
                for h in range(2):
                    P = slice(0, 64) if h == 0 else slice(64, 128)
                    acc = km2[h]
                    for sc in range(NSC):
                        ksl = at_k[P, sc * SB:(sc + 1) * SB]
                        sqv = sqc[P, 0:SB]
                        sch.add("scalar", lambda e, o=sqv, i=ksl: e.activation(o, i, AF.Square),
                                reads=[ksl], writes=[sqv], name="k1sq")
                        p4 = pool[4][0:1, 0:SB]
                        r = sqv
                        lw = ones_bf[P, 0:1]
                        sch.add("tensor", lambda e, o=p4, l=lw, r=r:
                                e.matmul(o, l, r, start=True, stop=True),
                                reads=[r, lw], writes=[p4], name="kn21")
                        if sc == 0:
                            sch.add("vector", lambda e, o=acc, i=p4: e.reduce_max(o, i, AX.X),
                                    reads=[p4], writes=[acc], name="km1")
                        else:
                            sch.add("vector", lambda e, o=tmp, i=p4: e.reduce_max(o, i, AX.X),
                                    reads=[p4], writes=[tmp], name="km1b")
                            sch.add("vector", lambda e, o=acc, a=acc, b=tmp:
                                    e.tensor_tensor(o, a, b, ALU.max),
                                    reads=[acc, tmp], writes=[acc], name="km1c")
                    sch.add("scalar", lambda e, o=acc, i=acc: e.activation(o, i, AF.Sqrt),
                            reads=[acc], writes=[acc], name="km1sqrt")
                    sch.add("vector", lambda e, o=acc, i=acc:
                            e.tensor_scalar(o, i, -1.0, None, ALU.mult),
                            reads=[acc], writes=[acc], name="km1neg")
                for ib in range(NSC):
                    jmax = (ib + 1) * (SB // 128)
                    combs = []
                    ncrows = [rows_bf[0:1, 0:SB], rows_bf[0:1, 512:512 + SB]]
                    for h in range(2):
                        P = slice(0, 64) if h == 0 else slice(64, 128)
                        qsl = at_qr[P, ib * SB:(ib + 1) * SB]
                        sqv = sqc[P, 0:SB]
                        sch.add("scalar", lambda e, o=sqv, i=qsl: e.activation(o, i, AF.Square),
                                reads=[qsl], writes=[sqv], name="q1sq")
                        p4r = pool[4][0:1, 0:SB]
                        r = sqv
                        lw = ones_bf[P, 0:1]
                        sch.add("tensor", lambda e, o=p4r, l=lw, r=r:
                                e.matmul(o, l, r, start=True, stop=True),
                                reads=[r, lw], writes=[p4r], name="qn21")
                        nrow = at_rows2[0:1, 0:SB]
                        sch.add("scalar", lambda e, o=nrow, i=p4r: e.activation(o, i, AF.Sqrt),
                                reads=[p4r], writes=[nrow], name="qn1")
                        sch.add("vector", lambda e, o=nrow, i=nrow, s=km2[h]:
                                e.tensor_scalar(o, i, s, None, ALU.mult),
                                reads=[nrow, km2[h]], writes=[nrow], name="negc1r")
                        sch.add("vector", lambda e, o=ncrows[h], i=nrow: e.tensor_copy(o, i),
                                reads=[nrow], writes=[ncrows[h]], name="negc1c")
                    lrows = [pool[2][0:1, 0:SB], pool[4][0:1, 0:SB]]
                    OBs = [pool[3][:, 0:SB], pool[5][:, 0:SB]]
                    Ps = [slice(0, 64), slice(64, 128)]

                    def emit_sexp1(h, jc, slot, ib=ib):
                        sb_ = slot % 2
                        SBANK = pool[sb_][:, 0:SB]
                        ncr = ncrows[h]
                        sch.add("tensor", lambda e, o=SBANK, r=ncr:
                                e.matmul(o, ones_row_bf[:], r, start=True, stop=False,
                                         skip_group_check=True),
                                reads=[ncr, ones_row_bf[:]], writes=[SBANK], name="s1r1")
                        l = at_k[Ps[h], jc * 128:(jc + 1) * 128]
                        rq = at_qr[Ps[h], ib * SB:(ib + 1) * SB]
                        sch.add("tensor", lambda e, o=SBANK, l=l, r=rq:
                                e.matmul(o, l, r, start=False, stop=True,
                                         skip_group_check=True),
                                reads=[l, rq], writes=[SBANK], name="s1mm")
                        pt = at_pt[:, sb_ * SB:(sb_ + 1) * SB]
                        diag_k = jc - ib * (SB // 128)
                        if diag_k >= 0:
                            sh = at_sh[:, sb_ * SB:(sb_ + 1) * SB]
                            mk = mask_k(diag_k)
                            sch.add("vector", lambda e, o=sh, a=SBANK, b=mk:
                                    e.tensor_add(o, a, b),
                                    reads=[SBANK, mk], writes=[sh], name="s1mask")
                            sch.add("scalar", lambda e, o=pt, i=sh: e.activation(o, i, AF.Exp),
                                    reads=[sh], writes=[pt], name="s1exp")
                        else:
                            sch.add("scalar", lambda e, o=pt, i=SBANK: e.activation(o, i, AF.Exp),
                                    reads=[SBANK], writes=[pt], name="s1exp")

                    def emit_lo1(h, jc, slot, jmax=jmax, hp=hp):
                        sb_ = slot % 2
                        pt = at_pt[:, sb_ * SB:(sb_ + 1) * SB]
                        lr_ = lrows[h]
                        sch.add("tensor", lambda e, o=lr_, r=pt, st=(jc == 0),
                                sp=(jc == jmax - 1):
                                e.matmul(o, ones_bf[:], r, start=st, stop=sp,
                                         skip_group_check=True),
                                reads=[pt, ones_bf[:]], writes=[lr_], name="l1mm")
                        lv = l1_v2[:, jc * 256 + hp * 128: jc * 256 + hp * 128 + 128]
                        OB = OBs[h]
                        sch.add("tensor", lambda e, o=OB, l=lv, r=pt, st=(jc == 0),
                                sp=(jc == jmax - 1):
                                e.matmul(o, l, r, start=st, stop=sp, skip_group_check=True),
                                reads=[pt, lv], writes=[OB], name="o1mm")

                    items = []
                    for jc in range(jmax):
                        items.append((0, jc))
                        items.append((1, jc))
                    for s in range(len(items) + 2):
                        if s >= 2:
                            emit_lo1(items[s - 2][0], items[s - 2][1], s - 2)
                        if s < len(items):
                            emit_sexp1(items[s][0], items[s][1], s)
                    for h in range(2):
                        rrow = at_rows2[0:1, 0:SB]
                        sch.add("vector", lambda e, o=rrow, i=lrows[h]:
                                e.tensor_scalar(o, i, 1e-30, None, ALU.add),
                                reads=[lrows[h]], writes=[rrow], name="l1eps")
                        sch.add("vector", lambda e, o=rrow, i=rrow: e.reciprocal(o, i),
                                reads=[rrow], writes=[rrow], name="l1rec")
                        p6b = pool[6][:, 0:SB]
                        sch.add("tensor", lambda e, o=p6b, r=rrow:
                                e.matmul(o, ones_row[:], r, start=True, stop=True),
                                reads=[rrow, ones_row[:]], writes=[p6b], name="r1bc")
                        rbv = at_rb[:, 0:SB]
                        sch.add("vector", lambda e, o=rbv, i=p6b: e.tensor_copy(o, i),
                                reads=[p6b], writes=[rbv], name="r1sb")
                        tc = (l1_tc1 if h == 0 else at_negc)[:, 0:SB]
                        sch.add("vector", lambda e, o=tc, a=OBs[h], b=rbv:
                                e.tensor_mul(o, a, b),
                                reads=[OBs[h], rbv], writes=[tc], name="o1s")
                        combs.append(tc)
                    ho = heads[:, n * S + ib * SB: n * S + (ib + 1) * SB]
                    nl = nlam[:, n:n + 1]
                    sch.add("vector", lambda e, o=ho, a=combs[1], s=nl, b=combs[0]:
                            e.scalar_tensor_tensor(o, a, s, b, ALU.mult, ALU.add),
                            reads=[combs[0], combs[1], nl], writes=[ho], name="dcomb")

        for dc in range(DC):
            wb = dc % 2
            wslab = at_wslab[:, wb * 2048:(wb + 1) * 2048]
            sch.dma("gpsimd", wslab.rearrange("p (c n) -> p c n", n=128),
                    wo_t1[dc].rearrange("(c p) n -> p c n", p=128), name=f"wo{dc}")
            for sc in range(NSC):
                bank = pool[5 + sc % 2][:, 0:SB]
                for oc in range(NH):
                    l = wslab[:, oc * 128:(oc + 1) * 128]
                    r = heads[:, oc * S + sc * SB: oc * S + (sc + 1) * SB]
                    sch.add("tensor", lambda e, o=bank, l=l, r=r, st=(oc == 0), sp=(oc == NH - 1):
                            e.matmul(o, l, r, start=st, stop=sp),
                            reads=[l, r], writes=[bank], name="womm")
                eb = sc % 2
                ev = at_sh[:, eb * SB:(eb + 1) * SB]
                sch.add("scalar", lambda e, o=ev, i=bank: e.activation(o, i, AF.Copy),
                        reads=[bank], writes=[ev], name="woevac")
                sch.dma("sync", sTd[dc * 128:(dc + 1) * 128, sc * SB:(sc + 1) * SB], ev,
                        name=f"s1Tw{dc}_{sc}")

        wgu = core.din("L1_ffn_w_gate_up", [16384, D], F32)
        wp1 = core.din("wproj1_t", [16, D, 512], F32)
        h_d = core.dtmp("h_l1", [8192, S], BF16)
        HC = 64
        SW = min(512, S)
        NSW = S // SW

        def quant_prefix(src_wb, g_buf, l_buf, scf, scb, G, tag):
            ng = G // 64
            s32a = scf[:, 0:min(ng, 32)]
            v3 = src_wb.rearrange("p (g k) -> p g k", k=64)
            if ng <= 32:
                sch.add("vector", lambda e, o=s32a, i=v3:
                        e.tensor_reduce(o, i, AX.X, ALU.add, apply_absolute_value=True),
                        reads=[src_wb], writes=[s32a], name=tag + "qs")
                sch.add("vector", lambda e, o=s32a, i=s32a:
                        e.tensor_scalar(o, i, 1.0 / 64, 1e-8, ALU.mult, ALU.max),
                        reads=[s32a], writes=[s32a], name=tag + "qc")
                sbf = scb[:, 0:ng]
                sch.add("vector", lambda e, o=sbf, i=s32a: e.tensor_copy(o, i),
                        reads=[s32a], writes=[sbf], name=tag + "qb")
                th = scf[:, 32:32 + ng]
                sch.add("vector", lambda e, o=th, i=sbf:
                        e.tensor_scalar(o, i, 0.501953125, None, ALU.mult),
                        reads=[sbf], writes=[th], name=tag + "qt")
                nth = scf[:, 64:64 + ng]
                sch.add("vector", lambda e, o=nth, i=th:
                        e.tensor_scalar(o, i, -1.0, None, ALU.mult),
                        reads=[th], writes=[nth], name=tag + "qn")
            else:
                raise NotImplementedError
            g_ = g_buf[:, 0:G]
            l_ = l_buf[:, 0:G]
            thb = th.unsqueeze(2).to_broadcast([128, ng, 64])
            nthb = nth.unsqueeze(2).to_broadcast([128, ng, 64])
            sch.add("vector", lambda e, o=g_.rearrange("p (g k) -> p g k", k=64), a=v3, b=thb:
                    e.tensor_tensor(o, a, b, ALU.is_gt),
                    reads=[src_wb, th], writes=[g_], name=tag + "qg")
            sch.add("vector", lambda e, o=l_.rearrange("p (g k) -> p g k", k=64), a=v3, b=nthb:
                    e.tensor_tensor(o, a, b, ALU.is_lt),
                    reads=[src_wb, nth], writes=[l_], name=tag + "ql")
            sch.add("vector", lambda e, o=g_, a=g_, b=l_: e.tensor_sub(o, a, b),
                    reads=[g_, l_], writes=[g_], name=tag + "qd")

        def quant_final(dst_wh, g_buf, scb, G, tag):
            ng = G // 64
            sbf = scb[:, 0:ng]
            g_ = g_buf[:, 0:G]
            sbb = sbf.unsqueeze(2).to_broadcast([128, ng, 64])
            sch.add("vector", lambda e, o=dst_wh.rearrange("p (g k) -> p g k", k=64),
                    a=g_.rearrange("p (g k) -> p g k", k=64), b=sbb:
                    e.tensor_tensor(o, a, b, ALU.mult),
                    reads=[g_, sbf], writes=[dst_wh], name=tag + "qw")

        def quant(dst_wh, src_wb, g_buf, l_buf, scf, scb, G, tag):
            quant_prefix(src_wb, g_buf, l_buf, scf, scb, G, tag)
            quant_final(dst_wh, g_buf, scb, G, tag)

        def transpose_tiles(src_bf, dst_bf, tiles, tag, evac_view=None,
                            psum_perm=False):
            for gi, i in enumerate(range(0, len(tiles), 8)):
                bk = 5 if gi % 2 == 0 else 7
                grp = tiles[i:i + 8]
                for j, (scol, dcol) in enumerate(grp):
                    o = pool_bf[bk][:, j * 128:(j + 1) * 128]
                    iv = src_bf[:, scol:scol + 128]
                    sch.add("tensor", lambda e, o=o, i=iv: e.transpose(o, i, identb[:]),
                            reads=[iv, identb[:]], writes=[pool[bk][:, 0:512]],
                            name=tag + "tr")
                if evac_view is not None:
                    ov = evac_view(i)
                    pv = pool_bf[bk][:, 0:128 * len(grp)].rearrange(
                        "p (t j n) -> p t j n", t=2, n=128)
                else:
                    ov = dst_bf[:, grp[0][1]: grp[0][1] + 128 * len(grp)]
                    pv = pool_bf[bk][:, 0:128 * len(grp)]
                sch.add("scalar", lambda e, o=ov, i=pv: e.copy(o, i),
                        reads=[pool[bk][:, 0:512]], writes=[ov], name=tag + "te")

        def gu_load(hc):
            wb = (f1_wbA, f1_wbB)[hc % 2]
            sch.dma("gpsimd", wb[:, 0:2048], wgu[hc * 128:(hc + 1) * 128, :],
                    name=f"gw{hc}")
            sch.dma("gpsimd", wb[:, 2048:4096],
                    wgu[8192 + hc * 128: 8192 + (hc + 1) * 128, :], name=f"uw{hc}")

        def gu_q_pre(hc, part):
            wb = (f1_wbA, f1_wbB)[hc % 2]
            off = part * 2048
            quant_prefix(wb[:, off:off + 2048], f1_glb, l_r3b, f1_sc, f1_scb,
                         2048, "gu"[part])

        def gu_q_fin_tr(hc, part):
            off = part * 2048
            tag = "gu"[part]
            quant_final(f1_wh[:, off:off + 2048], f1_glb, f1_scb, 2048, tag)
            transpose_tiles(f1_wh, f1_wh,
                            [(off + i * 128, off + i * 128) for i in range(16)], tag)

        def gu_mm_half(hc, half):
            off = half * 2048
            for dc in range(DC):
                l = f1_wh[:, off + dc * 128: off + (dc + 1) * 128]
                for sw in range(NSW):
                    o = pool[sw][:, 0:SW]
                    r = xn[:, dc * S + sw * SW: dc * S + (sw + 1) * SW]
                    sch.add("tensor",
                            lambda e, o=o, l=l, r=r, st=(dc == 0), sp=(dc == DC - 1):
                            e.matmul(o, l, r, start=st, stop=sp, skip_group_check=True),
                            reads=[l, r], writes=[o], name="gu1mm")
            if half == 0:
                for sw in range(NSW):
                    o = f1_t1[:, sw * SW:(sw + 1) * SW]
                    i = pool[sw][:, 0:SW]
                    sch.add("scalar", lambda e, o=o, i=i: e.activation(o, i, AF.Silu),
                            reads=[i], writes=[o], name="gu1silu")
            else:
                for sw in range(NSW):
                    o = f1_hq[:, sw * SW:(sw + 1) * SW]
                    a = f1_t1[:, sw * SW:(sw + 1) * SW]
                    b = pool[sw][:, 0:SW]
                    sch.add("vector", lambda e, o=o, a=a, b=b: e.tensor_mul(o, a, b),
                            reads=[a, b], writes=[o], name="gu1h")

        def gu_tail(hc):
            sch.dma("sync", h_d[hc * 128:(hc + 1) * 128, :], f1_hq[:, 0:S], name=f"hst{hc}")
            hsq = f1_hq[:, 0:S]
            sch.add("scalar", lambda e, o=hsq, i=hsq: e.activation(o, i, AF.Square),
                    reads=[hsq], writes=[hsq], name="h1sq")
            for sw in range(NSW):
                p6r = pool[6][0:1, 0:SW]
                r = hsq[:, sw * SW:(sw + 1) * SW]
                sch.add("tensor", lambda e, o=p6r, r=r:
                        e.matmul(o, ones_bf[:], r, start=True, stop=True),
                        reads=[r, ones_bf[:]], writes=[p6r], name="st1mm")
                dstr = rms_row[0:1, sw * SW:(sw + 1) * SW]
                if hc == 0:
                    sch.add("vector", lambda e, o=dstr, i=p6r: e.tensor_copy(o, i),
                            reads=[p6r], writes=[dstr], name="st1cp")
                else:
                    sch.add("vector", lambda e, o=dstr, a=dstr, b=p6r: e.tensor_add(o, a, b),
                            reads=[p6r, dstr], writes=[dstr], name="st1ad")

        gu_load(0)
        gu_q_pre(0, 0)
        gu_q_fin_tr(0, 0)
        gu_q_pre(0, 1)
        gu_q_fin_tr(0, 1)
        for hc in range(HC):
            if hc + 1 < HC:
                gu_load(hc + 1)
                gu_q_pre(hc + 1, 0)
            gu_mm_half(hc, 0)
            if hc + 1 < HC:
                gu_q_fin_tr(hc + 1, 0)
                gu_q_pre(hc + 1, 1)
            gu_mm_half(hc, 1)
            if hc + 1 < HC:
                gu_q_fin_tr(hc + 1, 1)
            gu_tail(hc)
        rr = rms_row[0:1, 0:S]
        sch.add("vector", lambda e, o=rr, i=rr:
                e.tensor_scalar(o, i, 1.0 / 8192, None, ALU.mult),
                reads=[rr], writes=[rr], name="rmsm")
        sch.add("scalar", lambda e, o=rr, i=rr, b=ceps[0:1, 1:2]:
                e.activation(o, i, AF.Sqrt, bias=b),
                reads=[rr, ceps[0:1, 1:2]], writes=[rr], name="rmssq")
        sch.add("vector", lambda e, o=rr, i=rr: e.reciprocal(o, i),
                reads=[rr], writes=[rr], name="rmsrec")

        def dn_quant_raw(kg, g2):
            dst = dn_w[:].rearrange("p (c n) -> p c n", n=512)
            srcw = wp1[kg][g2 * 1024:(g2 + 1) * 1024, :].rearrange(
                "(c p) n -> p c n", p=128)
            sch.dma("gpsimd", dst, srcw, name=f"pw{kg}_{g2}")
            for t in range(8):
                dc = g2 * 8 + t
                quant(dn_w[:, t * 512:(t + 1) * 512], dn_w[:, t * 512:(t + 1) * 512],
                      dn_g, l_r3, dn_sc, dn_scb, 512, f"p{dc}")

        def dn_tr(kg, g2):
            tiles = []
            for t in range(8):
                dc = g2 * 8 + t
                for j in range(4):
                    tiles.append((t * 512 + j * 128, j * 2048 + dc * 128))
            pT4 = pT_sb[:].rearrange("p (j dc n) -> p dc j n", j=4, n=128)

            def dn_evac(i, g2=g2, pT4=pT4):
                dc0 = g2 * 8 + i // 4
                return pT4[:, dc0:dc0 + 2, :, :]
            transpose_tiles(dn_w, pT_sb, tiles, "p", evac_view=dn_evac,
                            psum_perm=True)

        dn_quant_raw(0, 0)
        dn_tr(0, 0)
        dn_quant_raw(0, 1)
        dn_tr(0, 1)
        for kg in range(16):
            for c in range(4):
                hcg = kg * 4 + c
                hb = (f1_wbA, f1_wbB)[c % 2][:, (c // 2) * 2048:(c // 2) * 2048 + S]
                sch.dma("sync", hb, h_d[hcg * 128:(hcg + 1) * 128, :], name=f"hld{hcg}")
                for sw in range(NSW):
                    p6b = pool[6][:, 0:SW]
                    irow = rms_row[0:1, sw * SW:(sw + 1) * SW]
                    sch.add("tensor", lambda e, o=p6b, r=irow:
                            e.matmul(o, ones_row[:], r, start=True, stop=True),
                            reads=[irow, ones_row[:]], writes=[p6b], name="invbc")
                    o = hn_sb[:, c * 2048 + sw * SW: c * 2048 + (sw + 1) * SW]
                    a = hb[:, sw * SW:(sw + 1) * SW]
                    sch.add("vector", lambda e, o=o, a=a, b=p6b: e.tensor_mul(o, a, b),
                            reads=[a, p6b], writes=[o], name="hn1")
            if kg + 1 < 16:
                dn_quant_raw(kg + 1, 0)
            for dc in range(DC):
                for swd in range(NSW):
                    bk = (dc * NSW + swd) % 4
                    for c in range(4):
                        l = pT_sb[:, c * 2048 + dc * 128: c * 2048 + (dc + 1) * 128]
                        o = pool[bk][:, 0:SW]
                        r = hn_sb[:, c * 2048 + swd * SW: c * 2048 + (swd + 1) * SW]
                        sch.add("tensor",
                                lambda e, o=o, l=l, r=r, st=(c == 0), sp=(c == 3):
                                e.matmul(o, l, r, start=st, stop=sp, skip_group_check=True),
                                reads=[l, r], writes=[o], name="dn1mm")
                    yv = y_l1[:, dc * S + swd * SW: dc * S + (swd + 1) * SW]
                    po = pool[bk][:, 0:SW]
                    if kg == 0:
                        sch.add("vector", lambda e, o=yv, i=po: e.tensor_copy(o, i),
                                reads=[po], writes=[yv], name="y1cp")
                    else:
                        sch.add("vector", lambda e, o=yv, a=yv, b=po: e.tensor_add(o, a, b),
                                reads=[po, yv], writes=[yv], name="y1ad")
            if kg + 1 < 16:
                dn_tr(kg + 1, 0)
                dn_quant_raw(kg + 1, 1)
                dn_tr(kg + 1, 1)

        for dc in range(DC):
            for sc in range(NSC):
                eb = sc % 2
                stile = ep_s[:, eb * 512:eb * 512 + SB]
                xtile = ep_x[:, eb * 512:eb * 512 + SB]
                gtile = ep_g[:, eb * 512:eb * 512 + SB]
                sch.dma("sync", stile, sTd[dc * 128:(dc + 1) * 128, sc * SB:(sc + 1) * SB],
                        name="ep1s")
                sch.dma("sync", xtile, x1T[dc * 128:(dc + 1) * 128, sc * SB:(sc + 1) * SB],
                        name="ep1x")
                yv = y_l1[:, dc * S + sc * SB: dc * S + (sc + 1) * SB]
                sch.add("scalar", lambda e, o=gtile, i=yv: e.activation(o, i, AF.Silu),
                        reads=[yv], writes=[gtile], name="ep1silu")
                sch.add("vector", lambda e, o=gtile, a=gtile, b=stile: e.tensor_mul(o, a, b),
                        reads=[gtile, stile], writes=[gtile], name="ep1mul")
                sch.add("vector", lambda e, o=xtile, a=xtile, b=gtile: e.tensor_add(o, a, b),
                        reads=[gtile, xtile], writes=[xtile], name="ep1add")
                sch.dma("sync", outT[dc * 128:(dc + 1) * 128, sc * SB:(sc + 1) * SB],
                        xtile, name="ep1st")

    xT = core.din("xT", [D, S], F32)
    if 0 in layers:
        emit_ln("xT", "ln0")
        if "xn" in dbg:
            o = core.dout("dbg_xn", [D, S], BF16)
            for dc in range(DC):
                sch.dma("sync", o[dc * 128:(dc + 1) * 128, :], xn[:, dc * S:(dc + 1) * S],
                        name="dbgxn")
        emit_attn0()
        emit_moe0()
    else:
        core.dram["x1T"] = core.din("x1T_in", [D, S], F32)
        core.dram["xT"] = core.dram["x1T"]

    if 1 in layers:
        emit_l1()
    else:
        x1o = core.dout("x1T_out", [D, S], F32)
        for dc in range(DC):
            sch.dma("sync", x1o[dc * 128:(dc + 1) * 128, :],
                    core.dram["x1T"][dc * 128:(dc + 1) * 128, :], name="x1out")

    if "sT" in dbg:
        o = core.dout("dbg_sT", [D, S], F32)
        for dc in range(DC):
            sch.dma("sync", o[dc * 128:(dc + 1) * 128, :],
                    core.dram["sT"][dc * 128:(dc + 1) * 128, :], name="dbgsT")

    with core.nc.Block() as blk:
        sch.emit(blk)
    return core


def prep_core_inputs(inputs, b, S, layers=(0, 1)):
    c = host_constants(S, min(512, S))
    m = dict(c)
    x = np.asarray(inputs["L0_attn_hidden_states"])[b][:S]
    m["xT"] = np.ascontiguousarray(x.T)
    if 0 in layers:
        wq = np.asarray(inputs["L0_attn_w_q"])
        m["wqT_t"] = np.ascontiguousarray(wq.T.reshape(D, NH, HD).transpose(1, 0, 2))
        wkv = np.asarray(inputs["L0_attn_w_kv"])
        m["wkvT_t"] = np.ascontiguousarray(wkv.T.reshape(D, 2, HD).transpose(1, 0, 2))
        wp = np.asarray(inputs["L0_attn_w_proj"])
        m["wprojT_t"] = np.ascontiguousarray(wp.T.reshape(D, DC, 128).transpose(1, 0, 2))
        m["bproj_col"] = np.ascontiguousarray(
            np.asarray(inputs["L0_attn_b_proj"]).reshape(16, 128).T)
        m["L0_ffn_W_g"] = np.asarray(inputs["L0_ffn_W_g"])
        m["ws_gate_t"] = np.ascontiguousarray(
            np.asarray(inputs["L0_ffn_Ws_gate"]).reshape(D, 2, 1024).transpose(1, 0, 2))
        m["ws_up_t"] = np.ascontiguousarray(
            np.asarray(inputs["L0_ffn_Ws_up"]).reshape(D, 2, 1024).transpose(1, 0, 2))
        m["L0_ffn_Ws_down"] = np.asarray(inputs["L0_ffn_Ws_down"])
        m["L0_ffn_We_gate"] = np.asarray(inputs["L0_ffn_We_gate"])
        m["L0_ffn_We_up"] = np.asarray(inputs["L0_ffn_We_up"])
        m["L0_ffn_We_down"] = np.asarray(inputs["L0_ffn_We_down"])
    if 1 in layers:
        wq = np.asarray(inputs["L1_attn_Wq"])
        m["wq1_t"] = np.ascontiguousarray(wq.reshape(D, NH, HD).transpose(1, 0, 2))
        wk = np.asarray(inputs["L1_attn_Wk"])
        m["wk1_t"] = np.ascontiguousarray(wk.reshape(D, NH, HD).transpose(1, 0, 2))
        wv = np.asarray(inputs["L1_attn_Wv"])
        m["wv1_t"] = np.ascontiguousarray(wv.reshape(D, 8, 256).transpose(1, 0, 2))
        wo = np.asarray(inputs["L1_attn_Wo"])
        m["wo1_t"] = np.ascontiguousarray(wo.reshape(D, DC, 128).transpose(1, 0, 2))
        m["lam_row"] = np.ascontiguousarray(
            np.asarray(inputs["L1_attn_lambda_init"]).reshape(1, 16))
        m["L1_ffn_w_gate_up"] = np.asarray(inputs["L1_ffn_w_gate_up"])
        m["wproj1_t"] = np.ascontiguousarray(
            np.asarray(inputs["L1_ffn_w_proj"]).reshape(D, 16, 512).transpose(1, 0, 2))
    return m


def kernel(**inputs):
    S = 2048
    core = build_program(S=S, layers=(0, 1))
    in_maps = []
    for b in range(8):
        m = prep_core_inputs(inputs, b, S)
        in_maps.append({k: np.ascontiguousarray(v) for k, v in m.items()
                        if k in core.used_inputs})
    missing = [k for k in core.used_inputs if k not in in_maps[0]]
    assert not missing, missing
    res = run_bass_kernel_spmd(core.nc, in_maps, core_ids=list(range(8)))
    outs = [np.ascontiguousarray(np.asarray(r["outT"]).T) for r in res.results]
    return np.stack(outs).astype(np.float32)


# version 16 (final)
# speedup vs baseline: 1.0771x; 1.0330x over previous
import math
import numpy as np
import ml_dtypes
import concourse.bass as bass
import concourse.mybir as mybir
from concourse.bass_utils import run_bass_kernel_spmd

F32 = mybir.dt.float32
BF16 = mybir.dt.bfloat16
AF = mybir.ActivationFunctionType
ALU = mybir.AluOpType
AX = mybir.AxisListType

D = 2048
NH = 16
HD = 128
DC = D // 128
EPS_LN = 1e-5
NEG_BIG = -1.0e30
BF = ml_dtypes.bfloat16

ENGINES = ("tensor", "vector", "scalar", "sync", "gpsimd")
DMA_QUEUES = ("sync", "gpsimd")
_ESZ = {F32: 4, BF16: 2, mybir.dt.int32: 4}
SB_BLK = 512
DR_BLK = 65536
_SB_OFF = {}


def ap_keys(ap):
    t = ap.tensor
    name = t.name
    dims = ap.ap
    sp = str(ap.space)
    if "PSUM" in sp:
        return [("ps", name.rsplit("_", 1)[0] if name not in _PS_NAMES else name)]
    esz = _ESZ[ap.dtype]
    if "DRAM" in sp.upper():
        off = ap.offset
        span = 1
        for st, cnt in dims:
            span += (cnt - 1) * st
        b0 = off // DR_BLK
        b1 = (off + span - 1) // DR_BLK
        return [("d", name, b) for b in range(b0, min(b1, b0 + 63) + 1)]
    pstep = dims[0][0] if dims else 1
    off = ap.offset % pstep if pstep > 0 else ap.offset
    span = 1
    for st, cnt in dims[1:]:
        span += (cnt - 1) * st
    base = _SB_OFF.get(name.rsplit("_", 1)[0], _SB_OFF.get(name, 0))
    lo = base + off * esz
    hi = base + (off + span) * esz
    b0 = lo // SB_BLK
    b1 = (hi - 1) // SB_BLK
    return [("s", b) for b in range(b0, b1 + 1)]


_PS_NAMES = set()
NLANES = 8


class Sched:
    def __init__(self, nc):
        self.nc = nc
        self.ops = []
        self.n_local = {e: 0 for e in ENGINES}
        self.n_dma = {q: 0 for q in DMA_QUEUES}
        self.last_writer = {}
        self.readers = {}
        self.clock = {e: {} for e in ENGINES}
        self.opclock = []

    def add(self, eng, fn, reads=(), writes=(), name="", dma=False):
        i = len(self.ops)
        rkeys = []
        for ap in reads:
            rkeys.extend(ap_keys(ap))
        wkeys = []
        for ap in writes:
            wkeys.extend(ap_keys(ap))
        deps = set()
        for k in rkeys:
            j = self.last_writer.get(k)
            if j is not None:
                deps.add(j)
        for k in wkeys:
            j = self.last_writer.get(k)
            if j is not None:
                deps.add(j)
            for j in self.readers.get(k, ()):
                deps.add(j)
        cl = self.clock[eng]
        lane = None
        pre_wait = None
        if dma:
            idx = self.n_dma[eng]
            lane = (eng, idx % NLANES)
            rnd = idx // NLANES + 1
            if rnd > 1:
                pre_wait = (lane, (rnd - 1) * 16)
                if cl.get(lane, -1) < rnd - 1:
                    cl[lane] = rnd - 1
            self.n_dma[eng] += 1
        need = []
        for j in sorted(deps):
            oj = self.ops[j]
            if cl.get(oj["ckey"], -1) >= oj["cpos"]:
                continue
            need.append(j)
        same_drain = False
        waits = []
        for j in need:
            oj = self.ops[j]
            if oj["eng"] == eng and not oj["dma"]:
                if eng == "tensor":
                    continue
                same_drain = True
            else:
                waits.append(j)
                oj["inc"] = True
        if same_drain:
            cl[eng] = self.n_local[eng] - 1
        for j in waits:
            for e2, v in self.opclock[j].items():
                if cl.get(e2, -1) < v:
                    cl[e2] = v
        ckey = lane if dma else eng
        cpos = (self.n_dma[eng] - 1) // NLANES + 1 if dma else self.n_local[eng]
        op = dict(idx=i, eng=eng, fn=fn, local=self.n_local[eng], dma=dma,
                  inc=dma, waits=waits, drain=same_drain, name=name,
                  ckey=ckey, cpos=cpos, pre_wait=pre_wait)
        self.ops.append(op)
        self.n_local[eng] += 1
        oc = dict(cl)
        oc[ckey] = cpos
        if dma:
            oc[eng] = op["local"]
        self.opclock.append(oc)
        for k in rkeys:
            self.readers.setdefault(k, []).append(i)
        for k in wkeys:
            self.last_writer[k] = i
            self.readers[k] = []
        return i

    def dma(self, queue, out, in_, name="d"):
        return self.add(queue,
                        lambda e, o=out, i_=in_: e.dma_start(out=o, in_=i_),
                        reads=[in_], writes=[out], name="dma:" + name, dma=True)

    def emit(self, blk, final_queue="sync"):
        nc = self.nc
        sems = {e: nc.alloc_semaphore(f"sem_{e}") for e in ENGINES}
        for q in DMA_QUEUES:
            for l in range(NLANES):
                sems[(q, l)] = nc.alloc_semaphore(f"dma_{q}_{l}")
        cnt = {e: 0 for e in ENGINES}
        for op in self.ops:
            if op["inc"]:
                if op["dma"]:
                    op["inc_val"] = op["cpos"] * 16
                    op["sem"] = op["ckey"]
                else:
                    cnt[op["eng"]] += 1
                    op["inc_val"] = cnt[op["eng"]]
                    op["sem"] = op["eng"]
        per_engine = {e: [] for e in ENGINES}
        for op in self.ops:
            per_engine[op["eng"]].append(op)

        def mk_body(eng_ops):
            def body(eh):
                for op in eng_ops:
                    if op["drain"]:
                        eh.drain()
                    if op["pre_wait"] is not None:
                        eh.wait_ge(sems[op["pre_wait"][0]], op["pre_wait"][1])
                    wmax = {}
                    for j in op["waits"]:
                        oj = self.ops[j]
                        sk = oj["sem"]
                        wmax[sk] = max(wmax.get(sk, 0), oj["inc_val"])
                    for key, v in wmax.items():
                        eh.wait_ge(sems[key], v)
                    inst = op["fn"](eh)
                    if op["inc"]:
                        assert inst is not None, op["name"]
                        inst.then_inc(sems[op["sem"]], 16 if op["dma"] else 1)
            return body

        for e in ENGINES:
            if per_engine[e]:
                getattr(blk, e)(mk_body(per_engine[e]))

        finals = {}
        for op in self.ops:
            if op.get("inc_val"):
                finals[op["sem"]] = max(finals.get(op["sem"], 0), op["inc_val"])

        def final_body(eh):
            for sk, v in finals.items():
                eh.wait_ge(sems[sk], v)
        getattr(blk, final_queue)(final_body)


def rotary_tables(S):
    dim = HD
    SCALE_BASE = 512
    inv_freq = 1.0 / 10000.0 ** (np.arange(0, dim, 2, dtype=np.float64) / dim)
    t = np.arange(S, dtype=np.float64)
    freqs = np.einsum("i,j->ij", t, inv_freq)
    freqs = np.concatenate((freqs, freqs), axis=-1)
    scale_base = (np.arange(0, dim, 2, dtype=np.float64) + 0.4 * dim) / (1.4 * dim)
    power = (t - S // 2) / SCALE_BASE
    scale = scale_base[None, :] ** power[:, None]
    scale = np.concatenate((scale, scale), axis=-1)
    pos32 = freqs.astype(np.float32).astype(np.float64)
    sc32 = scale.astype(np.float32).astype(np.float64)
    sw = 1.0 / math.sqrt(HD)
    Aq = (np.cos(pos32) * sc32 * sw).astype(np.float32).T.copy()
    Bq = (np.sin(pos32) * sc32 * sw).astype(np.float32).T.copy()
    Ak = (np.cos(pos32) / sc32 * sw).astype(np.float32).T.copy()
    Bk = (np.sin(pos32) / sc32 * sw).astype(np.float32).T.copy()
    return Aq, Bq, Ak, Bk


def rot_half_matrix():
    R = np.zeros((HD, HD), dtype=np.float32)
    for hp in range(64):
        R[hp + 64, hp] = -1.0
    for hp in range(64, HD):
        R[hp - 64, hp] = 1.0
    return np.asarray(R.astype(BF))


def causal_mask1(SBv):
    pj = np.arange(128)[:, None]
    i = np.arange(SBv)[None, :]
    m0 = np.where(pj > i, np.float32(NEG_BIG), np.float32(0.0))
    full = np.full((128, 384), np.float32(NEG_BIG))
    return np.concatenate([full, m0], axis=1).astype(BF)


def host_constants(S, SBv):
    Aq, Bq, Ak, Bk = rotary_tables(S)
    return {
        "c_rotAq": Aq, "c_rotBq": Bq, "c_rotAk": Ak, "c_rotBk": Bk,
        "c_rotR": rot_half_matrix(),
        "c_mask1": np.asarray(causal_mask1(SBv)),
        "c_ones_bf": np.ones((128, 1), dtype=BF),
        "c_ones_row": np.ones((1, 128), dtype=np.float32),
        "c_ones_row_bf": np.ones((1, 128), dtype=BF),
        "c_iota8": np.broadcast_to(np.arange(8, dtype=np.float32), (128, 8)).copy(),
        "c_ident_f32": np.eye(128, dtype=np.float32),
        "c_ident_bf": np.eye(128, dtype=BF),
        "c_eps": np.broadcast_to(np.array([1e-5, 1.1920929e-07], dtype=np.float32),
                                 (128, 2)).copy(),
    }


class Core:
    def __init__(self, S):
        self.S = S
        self.SB = min(512, S)
        self.NSC = S // self.SB
        self.JC = S // 128
        self.nc = bass.Bass("TRN2", target_bir_lowering=False)
        self.sch = Sched(self.nc)
        self.dram = {}
        self.sb = {}
        self.pool = [self.nc.alloc_psum_tensor(f"pp{i}", [128, 512], F32)
                     for i in range(8)]
        for p in self.pool:
            _PS_NAMES.add(p.name)
        self.pool_bf = [p.ap().bitcast(BF16) for p in self.pool]
        self.used_inputs = []
        self.base = self.nc.bump_sbuf(212352, align_bytes=512)[0]
        self._cursor = 0

    def din(self, name, shape, dt=F32):
        if name not in self.dram:
            self.dram[name] = self.nc.dram_tensor(name, list(shape), dt,
                                                  kind="ExternalInput").ap()
            self.used_inputs.append(name)
        return self.dram[name]

    def dout(self, name, shape, dt=F32):
        self.dram[name] = self.nc.dram_tensor(name, list(shape), dt,
                                              kind="ExternalOutput").ap()
        return self.dram[name]

    def dtmp(self, name, shape, dt=F32):
        if name not in self.dram:
            self.dram[name] = self.nc.dram_tensor(name, list(shape), dt,
                                                  kind="Internal").ap()
        return self.dram[name]

    def sb_at(self, name, shape, dt, offset):
        assert offset % 128 == 0, (name, offset)
        off = self.base + offset
        h = self.nc.alloc_sbuf_tensor_at(name, list(shape), dt, offset=off)
        self.sb[name] = h
        _SB_OFF[name] = off
        _SB_OFF[h.name] = off
        return h

    def alloc(self, name, shape, dt):
        n = shape[-1] * _ESZ[dt]
        for s in shape[1:-1]:
            n *= s
        n = (n + 127) // 128 * 128
        h = self.sb_at(name, shape, dt, self._cursor)
        self._cursor += n
        return h


def build_program(S=2048, layers=(0, 1), dbg=()):
    core = Core(S)
    nc, sch = core.nc, core.sch
    SB, NSC, JC = core.SB, core.NSC, core.JC
    pool = core.pool
    dbg = set(dbg)

    xn = core.sb_at("xn", [128, DC * S], BF16, 0)
    off_big = 32 * S
    heads = core.sb_at("heads", [128, DC * S], BF16, off_big)
    SH = S // 2
    y_moe = core.sb_at("y_moe", [128, DC * SH], F32, off_big)
    y_l1 = core.sb_at("y_l1", [128, DC * S], F32, 0)
    R3 = off_big + 32 * S
    rotA_q = core.sb_at("rotA_q", [128, S], F32, R3)
    rotB_q = core.sb_at("rotB_q", [128, S], F32, R3 + 4 * S)
    rotA_k = core.sb_at("rotA_k", [128, S], F32, R3 + 8 * S)
    rotB_k = core.sb_at("rotB_k", [128, S], F32, R3 + 12 * S)
    slabs = core.sb_at("slabs", [128, 16 * 1024], BF16, R3)
    pT_sb = core.sb_at("pT_sb", [128, 8192], BF16, R3)
    rms_row = core.sb_at("rms_row", [1, 2048], F32, R3 + 16384)
    l_r3 = core.sb_at("l_r3", [128, 1024], F32, R3 + 24576)
    l_r3b = core.sb_at("l_r3b", [128, 2048], BF16, R3 + 24576)
    dn_sc = core.sb_at("dn_sc", [128, 96], F32, R3 + 28672)
    dn_scb = core.sb_at("dn_scb", [128, 96], BF16, R3 + 29184)
    W = R3 + 32768
    ln_xa = core.sb_at("ln_xa", [128, 2 * S], F32, W)
    ln_xbf = core.sb_at("ln_xbf", [128, S], BF16, W + 8 * S)
    ln_xsq = core.sb_at("ln_xsq", [128, S], BF16, W + 10 * S)
    ln_rows = core.sb_at("ln_rows", [1, 2 * S], F32, W + 12 * S)
    ln_tile = core.sb_at("ln_tile", [128, 512], F32, W + 20 * S)
    at_k = core.sb_at("at_k", [128, S], BF16, W)
    at_v = core.sb_at("at_v", [128, S], BF16, W + 2 * S)
    l1_v2 = core.sb_at("l1_v2", [128, 2 * S], BF16, W + 2 * S)
    at_q = core.sb_at("at_q", [128, S], BF16, W + 4 * S)
    at_qr = core.sb_at("at_qr", [128, S], BF16, W + 6 * S)
    at_sq = core.sb_at("at_sq", [128, S], BF16, W + 4 * S)
    FX = W + 8 * S
    at_sh = core.sb_at("at_sh", [128, 1024], F32, FX)
    at_pt = core.sb_at("at_pt", [128, 1024], BF16, FX + 4096)
    at_negc = core.sb_at("at_negc", [128, 512], F32, FX + 6144)
    at_rb = core.sb_at("at_rb", [128, 512], F32, FX + 8192)
    at_rows = core.sb_at("at_rows", [1, 512], F32, FX + 10240)
    rows_bf = core.sb_at("rows_bf", [1, 1024], BF16, FX + 10240)
    at_rows2 = core.sb_at("at_rows2", [1, 512], F32, FX + 12288)
    at_wslab = core.sb_at("at_wslab", [128, 4096], BF16, FX + 14336)
    l1_tc1 = core.sb_at("l1_tc1", [128, 512], F32, FX + 22528)
    l1_tc2 = core.sb_at("l1_tc2", [128, 512], F32, FX + 24576)
    gh = core.sb_at("gh", [128, 8 * 1024], BF16, W)
    ep_g = core.sb_at("ep_g", [128, 1024], F32, W + 16384)
    ep_s = core.sb_at("ep_s", [128, 1024], F32, W + 20480)
    ep_x = core.sb_at("ep_x", [128, 1024], F32, W + 24576)
    web = core.sb_at("web", [128, 1024], F32, W + 28672)
    hf = core.sb_at("hf", [128, 1024], F32, W + 32768)
    gt_w = core.sb_at("gt_w", [128, 8], F32, W + 36864)
    gt_t = core.sb_at("gt_t", [128, 48], F32, W + 37376)
    gt_r = core.sb_at("gt_r", [128, 4], F32, W + 37888)
    wT8 = core.sb_at("wT8", [8, S], F32, W + 16384)
    f1_wbA = core.sb_at("f1_wbA", [128, 4096], BF16, W)
    f1_wbB = core.sb_at("f1_wbB", [128, 4096], BF16, W + 8192)
    f1_gl = core.sb_at("f1_gl", [128, 1024], F32, W + 16384)
    f1_glb = core.sb_at("f1_glb", [128, 2048], BF16, W + 16384)
    f1_wh = core.sb_at("f1_wh", [128, 4096], BF16, W + 20480)
    f1_t1 = core.sb_at("f1_t1", [128, 2048], F32, W + 28672)
    f1_hq = core.sb_at("f1_hq", [128, 2048], BF16, W + 36864)
    f1_sc = core.sb_at("f1_sc", [128, 96], F32, W + 40960)
    f1_scb = core.sb_at("f1_scb", [128, 96], BF16, W + 41472)
    dn_w = core.sb_at("dn_w", [128, 4096], BF16, W + 16384)
    hn_sb = core.sb_at("hn_sb", [128, 8192], BF16, W + 24576)
    dn_g = core.sb_at("dn_g", [128, 512], F32, W + 40960)
    CW = W + 43008
    core._cursor = CW
    mask1 = core.alloc("mask1", [128, 384 + SB], BF16)
    rotR = core.alloc("rotR", [128, 128], BF16)
    ident = core.alloc("ident", [128, 128], F32)
    identb = core.alloc("identb", [128, 128], BF16)
    ones_bf = core.alloc("ones_bf", [128, 1], BF16)
    ones_row = core.alloc("ones_row", [1, 128], F32)
    iota8 = core.alloc("iota8", [128, 8], F32)
    b_col = core.alloc("b_col", [128, 16], F32)
    wg_sb = core.alloc("wg_sb", [128, 128], BF16)
    lam_sb = core.alloc("lam_sb", [128, 32], F32)
    srow_sb = core.alloc("srow_sb", [1, 64], F32)
    ones_row_bf = core.alloc("ones_row_bf", [1, 128], BF16)
    ceps = core.alloc("ceps", [128, 2], F32)
    assert core._cursor <= 212352, core._cursor

    CONST_SHAPES = {
        "c_rotR": ([128, 128], BF16, rotR), "c_ident_f32": ([128, 128], F32, ident),
        "c_ident_bf": ([128, 128], BF16, identb), "c_ones_bf": ([128, 1], BF16, ones_bf),
        "c_ones_row": ([1, 128], F32, ones_row), "c_iota8": ([128, 8], F32, iota8),
        "c_eps": ([128, 2], F32, ceps), "c_mask1": ([128, 384 + SB], BF16, mask1),
        "c_ones_row_bf": ([1, 128], BF16, ones_row_bf),
    }
    for nm, (shp, dt, tile) in CONST_SHAPES.items():
        d = core.din(nm, shp, dt)
        sch.dma("sync", tile[:], d, name=nm)

    def mask_k(k):
        return mask1[:, 384 - 128 * k: 384 - 128 * k + SB]

    def xn_ap(dc, sc=None, s0=None, sl=None):
        if sc is not None:
            return xn[:, dc * S + sc * SB: dc * S + (sc + 1) * SB]
        return xn[:, dc * S + s0: dc * S + s0 + sl]

    def emit_ln(src_name, tag):
        src = core.dram[src_name]
        for dc in range(DC):
            buf = dc % 2
            xav = ln_xa[:, buf * S:(buf + 1) * S]
            sch.dma("sync", xav, src[dc * 128:(dc + 1) * 128, :], name=f"{tag}x{dc}")
            sch.add("vector", lambda e, o=ln_xbf[:], i=xav: e.tensor_copy(o, i),
                    reads=[xav], writes=[ln_xbf[:]], name="ln_cast")
            sch.add("scalar", lambda e, o=ln_xsq[:], i=xav: e.activation(o, i, AF.Square),
                    reads=[xav], writes=[ln_xsq[:]], name="ln_sq")
            for sc in range(NSC):
                o1 = pool[sc % 4][0:1, 0:SB]
                r1 = ln_xbf[:, sc * SB:(sc + 1) * SB]
                sch.add("tensor",
                        lambda e, o=o1, r=r1, st=(dc == 0), sp=(dc == DC - 1):
                        e.matmul(o, ones_bf[:], r, start=st, stop=sp, skip_group_check=True),
                        reads=[r1, ones_bf[:]], writes=[o1], name="ln_mm1")
                o2 = pool[4 + sc % 4][0:1, 0:SB]
                r2 = ln_xsq[:, sc * SB:(sc + 1) * SB]
                sch.add("tensor",
                        lambda e, o=o2, r=r2, st=(dc == 0), sp=(dc == DC - 1):
                        e.matmul(o, ones_bf[:], r, start=st, stop=sp, skip_group_check=True),
                        reads=[r2, ones_bf[:]], writes=[o2], name="ln_mm2")
        for sc in range(NSC):
            mrow = ln_rows[0:1, sc * SB:(sc + 1) * SB]
            vrow = ln_rows[0:1, S + sc * SB: S + (sc + 1) * SB]
            p1 = pool[sc % 4][0:1, 0:SB]
            p2 = pool[4 + sc % 4][0:1, 0:SB]
            sch.add("vector", lambda e, o=mrow, i=p1:
                    e.tensor_scalar(o, i, 1.0 / D, None, ALU.mult),
                    reads=[p1], writes=[mrow], name="ln_m")
            sch.add("vector", lambda e, o=vrow, i=p2:
                    e.tensor_scalar(o, i, 1.0 / D, None, ALU.mult),
                    reads=[p2], writes=[vrow], name="ln_v")
            tr = ln_tile[0:1, 0:SB]
            sch.add("vector", lambda e, o=tr, m=mrow: e.tensor_mul(o, m, m),
                    reads=[mrow], writes=[tr], name="ln_m2")
            sch.add("vector", lambda e, o=vrow, v=vrow, t=tr: e.tensor_sub(o, v, t),
                    reads=[vrow, tr], writes=[vrow], name="ln_var")
            sch.add("scalar", lambda e, o=vrow, i=vrow, b=ceps[0:1, 0:1]:
                    e.activation(o, i, AF.Sqrt, bias=b),
                    reads=[vrow, ceps[0:1, 0:1]], writes=[vrow], name="ln_sqrt")
            sch.add("vector", lambda e, o=vrow, i=vrow: e.reciprocal(o, i),
                    reads=[vrow], writes=[vrow], name="ln_recip")
        for sc in range(NSC):
            mrow = ln_rows[0:1, sc * SB:(sc + 1) * SB]
            vrow = ln_rows[0:1, S + sc * SB: S + (sc + 1) * SB]
            pm = pool[0][:, 0:SB]
            pi = pool[1][:, 0:SB]
            sch.add("tensor", lambda e, o=pm, r=mrow:
                    e.matmul(o, ones_row[:], r, start=True, stop=True),
                    reads=[mrow, ones_row[:]], writes=[pm], name="ln_bm")
            sch.add("tensor", lambda e, o=pi, r=vrow:
                    e.matmul(o, ones_row[:], r, start=True, stop=True),
                    reads=[vrow, ones_row[:]], writes=[pi], name="ln_bi")
            for dc in range(DC):
                buf = dc % 2
                xt = ln_xa[:, buf * S + sc * SB: buf * S + (sc + 1) * SB]
                sch.dma("sync", xt, src[dc * 128:(dc + 1) * 128, sc * SB:(sc + 1) * SB],
                        name=f"{tag}x2_{dc}_{sc}")
                tf = ln_tile[:, 0:SB]
                xo = xn_ap(dc, sc)
                sch.add("vector", lambda e, o=tf, a=xt, b=pm: e.tensor_sub(o, a, b),
                        reads=[xt, pm], writes=[tf], name="ln_n1")
                sch.add("vector", lambda e, o=xo, a=tf, b=pi: e.tensor_mul(o, a, b),
                        reads=[tf, pi], writes=[xo], name="ln_n2")

    def emit_attn0():
        for nm, tile in [("c_rotAq", rotA_q), ("c_rotBq", rotB_q),
                         ("c_rotAk", rotA_k), ("c_rotBk", rotB_k)]:
            d = core.din(nm, [128, S], F32)
            sch.dma("sync", tile[:], d, name=nm)
        wkv = core.din("wkvT_t", [2, D, 128], F32)
        wq_t = core.din("wqT_t", [NH, D, 128], F32)
        wp_t = core.din("wprojT_t", [DC, D, 128], F32)
        bpc = core.din("bproj_col", [128, 16], F32)
        sch.dma("sync", b_col[:], bpc, name="bcol")

        kslab = at_wslab[:, 0:2048]
        vslab = at_wslab[:, 2048:4096]
        sch.dma("gpsimd", kslab.rearrange("p (c n) -> p c n", n=128),
                wkv[0].rearrange("(c p) n -> p c n", p=128), name="wkvk")
        sch.dma("gpsimd", vslab.rearrange("p (c n) -> p c n", n=128),
                wkv[1].rearrange("(c p) n -> p c n", p=128), name="wkvv")

        def rotary(src_bf, dst_bf, tabA, tabB, tag):
            for sc in range(NSC):
                srcv = src_bf[:, sc * SB:(sc + 1) * SB]
                p3 = pool[3][:, 0:SB]
                sch.add("tensor", lambda e, o=p3, r=srcv:
                        e.matmul(o, rotR[:], r, start=True, stop=True),
                        reads=[srcv, rotR[:]], writes=[p3], name=tag + "rot")
                sh = at_sh[:, 0:SB]
                tb = tabB[:, sc * SB:(sc + 1) * SB]
                sch.add("vector", lambda e, o=sh, a=tb, b=p3: e.tensor_mul(o, a, b),
                        reads=[p3, tb], writes=[sh], name=tag + "rb")
                sh2 = at_sh[:, SB:2 * SB]
                ta = tabA[:, sc * SB:(sc + 1) * SB]
                sch.add("vector", lambda e, o=sh2, a=ta, b=srcv: e.tensor_mul(o, a, b),
                        reads=[srcv, ta], writes=[sh2], name=tag + "ra")
                dv = dst_bf[:, sc * SB:(sc + 1) * SB]
                sch.add("vector", lambda e, o=dv, a=sh2, b=sh: e.tensor_add(o, a, b),
                        reads=[sh, sh2], writes=[dv], name=tag + "rs")

        for sc in range(NSC):
            p2 = pool[2][:, 0:SB]
            for dc in range(DC):
                l = kslab[:, dc * 128:(dc + 1) * 128]
                r = xn_ap(dc, sc)
                sch.add("tensor", lambda e, o=p2, l=l, r=r, st=(dc == 0), sp=(dc == DC - 1):
                        e.matmul(o, l, r, start=st, stop=sp),
                        reads=[l, r], writes=[p2], name="kmm")
            kt = at_q[:, sc * SB:(sc + 1) * SB]
            sch.add("scalar", lambda e, o=kt, i=p2: e.activation(o, i, AF.Copy),
                    reads=[p2], writes=[kt], name="kevac")
        rotary(at_q, at_k, rotA_k, rotB_k, "k")
        sch.add("scalar", lambda e, o=at_sq[:], i=at_k[:]: e.activation(o, i, AF.Square),
                reads=[at_k[:]], writes=[at_sq[:]], name="ksq")
        for sc in range(NSC):
            p4 = pool[4][0:1, 0:SB]
            r = at_sq[:, sc * SB:(sc + 1) * SB]
            sch.add("tensor", lambda e, o=p4, r=r:
                    e.matmul(o, ones_bf[:], r, start=True, stop=True),
                    reads=[r, ones_bf[:]], writes=[p4], name="kn2")
            kd = srow_sb[0:1, 1 + sc:2 + sc]
            sch.add("vector", lambda e, o=kd, i=p4: e.reduce_max(o, i, AX.X),
                    reads=[p4], writes=[kd], name="kmax")
        kall = srow_sb[0:1, 1:1 + NSC]
        km = srow_sb[0:1, 0:1]
        sch.add("vector", lambda e, o=km, i=kall: e.reduce_max(o, i, AX.X),
                reads=[kall], writes=[km], name="kmaxall")
        sch.add("scalar", lambda e, o=km, i=km: e.activation(o, i, AF.Sqrt),
                reads=[km], writes=[km], name="kmaxsqrt")
        sch.add("vector", lambda e, o=km, i=km: e.tensor_scalar(o, i, -1.0, None, ALU.mult),
                reads=[km], writes=[km], name="kmaxneg")
        for g in range((JC + 3) // 4):
            n_in_g = min(4, JC - 4 * g)
            p5 = pool[5]
            for j in range(n_in_g):
                jc = 4 * g + j
                for dc in range(DC):
                    l = xn_ap(dc, s0=jc * 128, sl=128)
                    r = vslab[:, dc * 128:(dc + 1) * 128]
                    o = p5[:, j * 128:(j + 1) * 128]
                    sch.add("tensor", lambda e, o=o, l=l, r=r, st=(dc == 0), sp=(dc == DC - 1):
                            e.matmul(o, l, r, start=st, stop=sp, skip_group_check=True),
                            reads=[l, r], writes=[o], name="vmm")
            vv = at_v[:, g * 512:g * 512 + n_in_g * 128]
            pv = p5[:, 0:n_in_g * 128]
            sch.add("vector", lambda e, o=vv, i=pv: e.tensor_copy(o, i),
                    reads=[pv], writes=[vv], name="vevac")
        for n in range(NH):
            qb = n % 2
            qslab = at_wslab[:, qb * 2048:(qb + 1) * 2048]
            sch.dma("gpsimd", qslab.rearrange("p (c n) -> p c n", n=128),
                    wq_t[n].rearrange("(c p) n -> p c n", p=128), name=f"wq{n}")
            for sc in range(NSC):
                p6 = pool[6][:, 0:SB]
                for dc in range(DC):
                    l = qslab[:, dc * 128:(dc + 1) * 128]
                    r = xn_ap(dc, sc)
                    sch.add("tensor", lambda e, o=p6, l=l, r=r, st=(dc == 0), sp=(dc == DC - 1):
                            e.matmul(o, l, r, start=st, stop=sp),
                            reads=[l, r], writes=[p6], name="qmm")
                qt = at_q[:, sc * SB:(sc + 1) * SB]
                sch.add("scalar", lambda e, o=qt, i=p6: e.activation(o, i, AF.Copy),
                        reads=[p6], writes=[qt], name="qevac")
            rotary(at_q, at_qr, rotA_q, rotB_q, "q")
            sch.add("scalar", lambda e, o=at_sq[:], i=at_qr[:]: e.activation(o, i, AF.Square),
                    reads=[at_qr[:]], writes=[at_sq[:]], name="qsq")
            pairs = [tuple(x for x in (a, a + 1) if x < NSC) for a in range(0, NSC, 2)]
            for pair in pairs:
                ncrows = [rows_bf[0:1, 0:SB], rows_bf[0:1, 512:512 + SB]]
                for pi, ib in enumerate(pair):
                    p4r = pool[4][0:1, 0:SB]
                    r = at_sq[:, ib * SB:(ib + 1) * SB]
                    sch.add("tensor", lambda e, o=p4r, r=r:
                            e.matmul(o, ones_bf[:], r, start=True, stop=True),
                            reads=[r, ones_bf[:]], writes=[p4r], name="qn2")
                    nrow = at_rows2[0:1, 0:SB]
                    sch.add("scalar", lambda e, o=nrow, i=p4r: e.activation(o, i, AF.Sqrt),
                            reads=[p4r], writes=[nrow], name="qn")
                    sch.add("vector", lambda e, o=nrow, i=nrow, s=km:
                            e.tensor_scalar(o, i, s, None, ALU.mult),
                            reads=[nrow, km], writes=[nrow], name="negc_row")
                    sch.add("vector", lambda e, o=ncrows[pi], i=nrow: e.tensor_copy(o, i),
                            reads=[nrow], writes=[ncrows[pi]], name="negc_bfc")
                lrows = [pool[2][0:1, 0:SB], pool[4][0:1, 0:SB]]
                OBs = [pool[3][:, 0:SB], pool[5][:, 0:SB]]
                jmaxs = [(ib + 1) * (SB // 128) for ib in pair]

                def emit_sexp0(pi, jc, slot, pair=pair, ncrows=ncrows):
                    ib = pair[pi]
                    sb_ = slot % 2
                    SBANK = pool[sb_][:, 0:SB]
                    ncr = ncrows[pi]
                    sch.add("tensor", lambda e, o=SBANK, r=ncr:
                            e.matmul(o, ones_row_bf[:], r, start=True, stop=False,
                                     skip_group_check=True),
                            reads=[ncr, ones_row_bf[:]], writes=[SBANK], name="sr1")
                    l = at_k[:, jc * 128:(jc + 1) * 128]
                    r = at_qr[:, ib * SB:(ib + 1) * SB]
                    sch.add("tensor", lambda e, o=SBANK, l=l, r=r:
                            e.matmul(o, l, r, start=False, stop=True,
                                     skip_group_check=True),
                            reads=[l, r], writes=[SBANK], name="smm")
                    pt = at_pt[:, sb_ * SB:(sb_ + 1) * SB]
                    diag_k = jc - ib * (SB // 128)
                    if diag_k >= 0:
                        sh = at_sh[:, sb_ * SB:(sb_ + 1) * SB]
                        mk = mask_k(diag_k)
                        sch.add("vector", lambda e, o=sh, a=SBANK, b=mk: e.tensor_add(o, a, b),
                                reads=[SBANK, mk], writes=[sh], name="smask")
                        sch.add("scalar", lambda e, o=pt, i=sh: e.activation(o, i, AF.Exp),
                                reads=[sh], writes=[pt], name="sexp")
                    else:
                        sch.add("scalar", lambda e, o=pt, i=SBANK: e.activation(o, i, AF.Exp),
                                reads=[SBANK], writes=[pt], name="sexp")

                def emit_lo0(pi, jc, slot, jmaxs=jmaxs, lrows=lrows, OBs=OBs):
                    sb_ = slot % 2
                    pt = at_pt[:, sb_ * SB:(sb_ + 1) * SB]
                    lr_ = lrows[pi]
                    sch.add("tensor", lambda e, o=lr_, r=pt, st=(jc == 0),
                            sp=(jc == jmaxs[pi] - 1):
                            e.matmul(o, ones_bf[:], r, start=st, stop=sp,
                                     skip_group_check=True),
                            reads=[pt, ones_bf[:]], writes=[lr_], name="lmm")
                    p3o = OBs[pi]
                    lv = at_v[:, jc * 128:(jc + 1) * 128]
                    sch.add("tensor", lambda e, o=p3o, l=lv, r=pt, st=(jc == 0),
                            sp=(jc == jmaxs[pi] - 1):
                            e.matmul(o, l, r, start=st, stop=sp, skip_group_check=True),
                            reads=[pt, lv], writes=[p3o], name="omm")

                items = []
                mx = max(jmaxs)
                for jc in range(mx):
                    for pi in range(len(pair)):
                        if jc < jmaxs[pi]:
                            items.append((pi, jc))
                for s in range(len(items) + 2):
                    if s >= 2:
                        emit_lo0(items[s - 2][0], items[s - 2][1], s - 2)
                    if s < len(items):
                        emit_sexp0(items[s][0], items[s][1], s)
                for pi, ib in enumerate(pair):
                    rrow = at_rows2[0:1, 0:SB]
                    sch.add("vector", lambda e, o=rrow, i=lrows[pi]:
                            e.tensor_scalar(o, i, 1e-30, None, ALU.add),
                            reads=[lrows[pi]], writes=[rrow], name="laddeps")
                    sch.add("vector", lambda e, o=rrow, i=rrow: e.reciprocal(o, i),
                            reads=[rrow], writes=[rrow], name="lrecip")
                    p6b = pool[6][:, 0:SB]
                    sch.add("tensor", lambda e, o=p6b, r=rrow:
                            e.matmul(o, ones_row[:], r, start=True, stop=True),
                            reads=[rrow, ones_row[:]], writes=[p6b], name="rbc")
                    rbv = at_rb[:, 0:SB]
                    sch.add("vector", lambda e, o=rbv, i=p6b: e.tensor_copy(o, i),
                            reads=[p6b], writes=[rbv], name="rbsb")
                    ho = heads[:, n * S + ib * SB: n * S + (ib + 1) * SB]
                    sch.add("vector", lambda e, o=ho, a=OBs[pi], b=rbv: e.tensor_mul(o, a, b),
                            reads=[OBs[pi], rbv], writes=[ho], name="oscale")

        sT = core.dtmp("sT", [D, S], F32)
        for dc in range(DC):
            wb = dc % 2
            wslab = at_wslab[:, wb * 2048:(wb + 1) * 2048]
            sch.dma("gpsimd", wslab.rearrange("p (c n) -> p c n", n=128),
                    wp_t[dc].rearrange("(c p) n -> p c n", p=128), name=f"wp{dc}")
            for sc in range(NSC):
                bank = pool[5 + sc % 2][:, 0:SB]
                for oc in range(NH):
                    l = wslab[:, oc * 128:(oc + 1) * 128]
                    r = heads[:, oc * S + sc * SB: oc * S + (sc + 1) * SB]
                    sch.add("tensor", lambda e, o=bank, l=l, r=r, st=(oc == 0), sp=(oc == NH - 1):
                            e.matmul(o, l, r, start=st, stop=sp),
                            reads=[l, r], writes=[bank], name="pmm")
                eb = sc % 2
                ev = at_sh[:, eb * SB:(eb + 1) * SB]
                bc = b_col[:, dc:dc + 1]
                sch.add("scalar", lambda e, o=ev, i=bank, b=bc: e.add(o, i, b),
                        reads=[bank, bc], writes=[ev], name="pevac")
                sch.dma("sync", sT[dc * 128:(dc + 1) * 128, sc * SB:(sc + 1) * SB], ev,
                        name=f"sTw{dc}_{sc}")

    def emit_moe0():
        wg = core.din("L0_ffn_W_g", [D, 8], F32)
        sch.dma("gpsimd", wg_sb[:].rearrange("p (c n) -> p c n", n=8),
                wg.rearrange("(c p) n -> p c n", p=128), name="wg")
        gwT = core.dtmp("gw8", [8, S], F32)
        xT = core.dram["xT"]
        sTd = core.dram["sT"]
        x1T = core.dtmp("x1T", [D, S], F32)
        for tc in range(S // 128):
            p0 = pool[0][:, 0:8]
            for dc in range(DC):
                l = xn_ap(dc, s0=tc * 128, sl=128)
                r = wg_sb[:, dc * 8:(dc + 1) * 8]
                sch.add("tensor", lambda e, o=p0, l=l, r=r, st=(dc == 0), sp=(dc == DC - 1):
                        e.matmul(o, l, r, start=st, stop=sp),
                        reads=[l, r], writes=[p0], name="gmm")
            t = [gt_t[:, k * 8:(k + 1) * 8] for k in range(6)]
            ga = t[5]
            r4 = gt_r
            sch.add("vector", lambda e, o=ga, i=p0: e.tensor_copy(o, i),
                    reads=[p0], writes=[ga], name="gcp")
            sch.add("vector", lambda e, o=r4[:, 0:1], i=ga:
                    e.reduce_max(o, i, AX.X, negate=True),
                    reads=[ga], writes=[r4[:, 0:1]], name="gmax1")
            sch.add("scalar", lambda e, o=t[0], i=ga, b=r4[:, 0:1]:
                    e.activation(o, i, AF.Exp, bias=b),
                    reads=[ga, r4[:, 0:1]], writes=[t[0]], name="gexp")
            sch.add("vector", lambda e, o=r4[:, 1:2], i=t[0]:
                    e.tensor_reduce(o, i, AX.X, ALU.add),
                    reads=[t[0]], writes=[r4[:, 1:2]], name="gsum")
            sch.add("vector", lambda e, o=r4[:, 1:2], i=r4[:, 1:2]: e.reciprocal(o, i),
                    reads=[r4[:, 1:2]], writes=[r4[:, 1:2]], name="grec")
            sch.add("vector", lambda e, o=t[1], i=ga: e.tensor_scalar(o, i, -1.0, None, ALU.mult),
                    reads=[ga], writes=[t[1]], name="gneg")
            sch.add("vector", lambda e, o=t[2], i=t[1], s=r4[:, 0:1]:
                    e.tensor_scalar(o, i, s, None, ALU.is_equal),
                    reads=[t[1], r4[:, 0:1]], writes=[t[2]], name="geq1")
            sch.add("vector", lambda e, o=t[3], a=t[2], b=iota8[:]: e.tensor_mul(o, a, b),
                    reads=[t[2], iota8[:]], writes=[t[3]], name="gkey")
            sch.add("vector", lambda e, o=t[3], a=t[2], b=t[3]:
                    e.scalar_tensor_tensor(o, a, -8.0, b, ALU.mult, ALU.add),
                    reads=[t[2], t[3]], writes=[t[3]], name="gkey2")
            sch.add("vector", lambda e, o=t[3], i=t[3]: e.tensor_scalar(o, i, 8.0, None, ALU.add),
                    reads=[t[3]], writes=[t[3]], name="gkey3")
            sch.add("vector", lambda e, o=r4[:, 2:3], i=t[3]:
                    e.tensor_reduce(o, i, AX.X, ALU.min),
                    reads=[t[3]], writes=[r4[:, 2:3]], name="gi1")
            sch.add("vector", lambda e, o=t[2], i=iota8[:], s=r4[:, 2:3]:
                    e.tensor_scalar(o, i, s, None, ALU.is_equal),
                    reads=[iota8[:], r4[:, 2:3]], writes=[t[2]], name="gsel1")
            sch.add("vector", lambda e, o=t[4], a=t[2], b=ga:
                    e.scalar_tensor_tensor(o, a, -1e9, b, ALU.mult, ALU.add),
                    reads=[t[2], ga], writes=[t[4]], name="gmask")
            sch.add("vector", lambda e, o=r4[:, 3:4], i=t[4]:
                    e.reduce_max(o, i, AX.X, negate=True),
                    reads=[t[4]], writes=[r4[:, 3:4]], name="gmax2")
            sch.add("vector", lambda e, o=t[4], i=t[4]:
                    e.tensor_scalar(o, i, -1.0, None, ALU.mult),
                    reads=[t[4]], writes=[t[4]], name="gneg2")
            sch.add("vector", lambda e, o=t[1], i=t[4], s=r4[:, 3:4]:
                    e.tensor_scalar(o, i, s, None, ALU.is_equal),
                    reads=[t[4], r4[:, 3:4]], writes=[t[1]], name="geq2")
            sch.add("vector", lambda e, o=t[4], a=t[1], b=iota8[:]: e.tensor_mul(o, a, b),
                    reads=[t[1], iota8[:]], writes=[t[4]], name="gkey4")
            sch.add("vector", lambda e, o=t[4], a=t[1], b=t[4]:
                    e.scalar_tensor_tensor(o, a, -8.0, b, ALU.mult, ALU.add),
                    reads=[t[1], t[4]], writes=[t[4]], name="gkey5")
            sch.add("vector", lambda e, o=t[4], i=t[4]: e.tensor_scalar(o, i, 8.0, None, ALU.add),
                    reads=[t[4]], writes=[t[4]], name="gkey6")
            sch.add("vector", lambda e, o=r4[:, 2:3], i=t[4]:
                    e.tensor_reduce(o, i, AX.X, ALU.min),
                    reads=[t[4]], writes=[r4[:, 2:3]], name="gi2")
            sch.add("vector", lambda e, o=t[1], i=iota8[:], s=r4[:, 2:3]:
                    e.tensor_scalar(o, i, s, None, ALU.is_equal),
                    reads=[iota8[:], r4[:, 2:3]], writes=[t[1]], name="gsel2")
            sch.add("vector", lambda e, o=t[2], a=t[2], b=t[1]: e.tensor_add(o, a, b),
                    reads=[t[2], t[1]], writes=[t[2]], name="gselb")
            sch.add("vector", lambda e, o=t[0], a=t[0], b=t[2]: e.tensor_mul(o, a, b),
                    reads=[t[0], t[2]], writes=[t[0]], name="gwm")
            sch.add("vector", lambda e, o=gt_w[:], i=t[0], s=r4[:, 1:2]:
                    e.tensor_scalar(o, i, s, None, ALU.mult),
                    reads=[t[0], r4[:, 1:2]], writes=[gt_w[:]], name="gwf")
            p1t = pool[1][0:8, 0:128]
            sch.add("tensor", lambda e, o=p1t, i=gt_w[:]: e.transpose(o, i, ident[:]),
                    reads=[gt_w[:], ident[:]], writes=[pool[1][:, 0:128]], name="gtr")
            wo = wT8[0:8, tc * 128:(tc + 1) * 128]
            sch.add("vector", lambda e, o=wo, i=p1t: e.tensor_copy(o, i),
                    reads=[pool[1][:, 0:128]], writes=[wo], name="gtrev")
        sch.dma("sync", gwT, wT8[:], name="gw8")

        Wsg = core.din("ws_gate_t", [2, D, 1024], F32)
        Wsu = core.din("ws_up_t", [2, D, 1024], F32)
        Wsd = core.din("L0_ffn_Ws_down", [D, D], F32)
        Weg = core.din("L0_ffn_We_gate", [8, D, 1024], F32)
        Weu = core.din("L0_ffn_We_up", [8, D, 1024], F32)
        Wed = core.din("L0_ffn_We_down", [8, 1024, D], F32)
        SHW = min(512, SH)
        NB = SH // SHW
        HSLB = 8192

        def load_gu_half(mat_ap, g, buf, what):
            for q in range(2):
                dst = slabs[:, buf * HSLB + q * 4096: buf * HSLB + (q + 1) * 4096]
                dst = dst.rearrange("p (c n) -> p c n", n=512)
                srcv = mat_ap[q * 1024:(q + 1) * 1024, g * 512:(g + 1) * 512]
                srcv = srcv.rearrange("(c p) n -> p c n", p=128)
                sch.dma("gpsimd", dst, srcv, name=f"m{what}{g}{q}")

        def load_dn_half(mat_ap, g, buf, what):
            for q in range(2):
                dst = slabs[:, buf * HSLB + q * 4096: buf * HSLB + (q + 1) * 4096]
                dst = dst.rearrange("p (c n) -> p c n", n=1024)
                srcv = mat_ap[q * 512:(q + 1) * 512, g * 1024:(g + 1) * 1024]
                srcv = srcv.rearrange("(c p) n -> p c n", p=128)
                sch.dma("gpsimd", dst, srcv, name=f"md{what}{g}{q}")

        def moe_gu_half(kind, g, buf, s0, shared):
            for dc in range(DC):
                for eo_l in range(4):
                    l = slabs[:, buf * HSLB + dc * 512 + eo_l * 128:
                              buf * HSLB + dc * 512 + (eo_l + 1) * 128]
                    for sw in range(NB):
                        o = pool[eo_l * NB + sw][:, 0:SHW]
                        r = xn_ap(dc, s0=s0 + sw * SHW, sl=SHW)
                        sch.add("tensor",
                                lambda e, o=o, l=l, r=r, st=(dc == 0), sp=(dc == DC - 1):
                                e.matmul(o, l, r, start=st, stop=sp, skip_group_check=True),
                                reads=[l, r], writes=[o], name="moe" + kind)
            for eo_l in range(4):
                eo = 4 * g + eo_l
                for sw in range(NB):
                    ghs = gh[:, eo * SH + sw * SHW: eo * SH + (sw + 1) * SHW]
                    pu = pool[eo_l * NB + sw][:, 0:SHW]
                    if kind == "g":
                        sch.add("scalar", lambda e, o=ghs, i=pu: e.activation(o, i, AF.Silu),
                                reads=[pu], writes=[ghs], name="moesilu")
                    elif shared:
                        sch.add("vector", lambda e, o=ghs, a=ghs, b=pu: e.tensor_mul(o, a, b),
                                reads=[pu, ghs], writes=[ghs], name="moeh")
                    else:
                        hfv = hf[:, sw * SHW:(sw + 1) * SHW]
                        sch.add("vector", lambda e, o=hfv, a=ghs, b=pu: e.tensor_mul(o, a, b),
                                reads=[pu, ghs], writes=[hfv], name="moeh1")
                        wv2 = web[:, sw * SHW:(sw + 1) * SHW]
                        sch.add("vector", lambda e, o=ghs, a=hfv, b=wv2: e.tensor_mul(o, a, b),
                                reads=[hfv, wv2], writes=[ghs], name="moeh2")

        def moe_dn_half(g, buf, s0, first):
            for dc in range(8 * g, 8 * g + 8):
                d_l = dc - 8 * g
                bank = (dc % 2) * NB
                for eo in range(8):
                    l = slabs[:, buf * HSLB + eo * 1024 + d_l * 128:
                              buf * HSLB + eo * 1024 + (d_l + 1) * 128]
                    for sw in range(NB):
                        o = pool[bank + sw][:, 0:SHW]
                        r = gh[:, eo * SH + sw * SHW: eo * SH + (sw + 1) * SHW]
                        sch.add("tensor",
                                lambda e, o=o, l=l, r=r, st=(eo == 0), sp=(eo == 7):
                                e.matmul(o, l, r, start=st, stop=sp, skip_group_check=True),
                                reads=[l, r], writes=[o], name="moed")
                for sw in range(NB):
                    yv = y_moe[:, dc * SH + sw * SHW: dc * SH + (sw + 1) * SHW]
                    po = pool[bank + sw][:, 0:SHW]
                    if first:
                        sch.add("vector", lambda e, o=yv, i=po: e.tensor_copy(o, i),
                                reads=[po], writes=[yv], name="ycp")
                    else:
                        sch.add("vector", lambda e, o=yv, a=yv, b=po: e.tensor_add(o, a, b),
                                reads=[po, yv], writes=[yv], name="yadd")

        for half in range(2):
            s0 = half * SH
            tasks = []
            for pe in range(10):
                shared = pe < 2
                if shared:
                    gate_src, up_src = Wsg[pe], Wsu[pe]
                    down_src = Wsd[pe * 1024:(pe + 1) * 1024, :]
                else:
                    gate_src, up_src = Weg[pe - 2], Weu[pe - 2]
                    down_src = Wed[pe - 2]
                for g in range(2):
                    tasks.append(("g", gate_src, g, pe, shared))
                for g in range(2):
                    tasks.append(("u", up_src, g, pe, shared))
                for g in range(2):
                    tasks.append(("d", down_src, g, pe, shared))

            def load_task(i):
                kind, mat, g, pe, shared = tasks[i]
                if kind == "d":
                    load_dn_half(mat, g, i % 2, f"{pe}")
                else:
                    load_gu_half(mat, g, i % 2, f"{kind}{pe}")

            load_task(0)
            for i, (kind, mat, g, pe, shared) in enumerate(tasks):
                if pe >= 2 and kind == "g" and g == 0:
                    er = pe - 2
                    wv = web[:, 0:SH]
                    sch.dma("sync", wv,
                            gwT[er:er + 1, s0:s0 + SH].partition_broadcast(128),
                            name=f"web{er}_{half}")
                if i + 1 < len(tasks):
                    load_task(i + 1)
                if kind == "d":
                    moe_dn_half(g, i % 2, s0, pe == 0)
                else:
                    moe_gu_half(kind, g, i % 2, s0, shared)
            for dc in range(DC):
                for sw in range(NB):
                    eb = sw % 2
                    stile = ep_s[:, eb * 512:eb * 512 + SHW]
                    xtile = ep_x[:, eb * 512:eb * 512 + SHW]
                    gtile = ep_g[:, eb * 512:eb * 512 + SHW]
                    sch.dma("sync", stile,
                            sTd[dc * 128:(dc + 1) * 128, s0 + sw * SHW:s0 + (sw + 1) * SHW],
                            name="eplds")
                    sch.dma("sync", xtile,
                            xT[dc * 128:(dc + 1) * 128, s0 + sw * SHW:s0 + (sw + 1) * SHW],
                            name="epldx")
                    yv = y_moe[:, dc * SH + sw * SHW: dc * SH + (sw + 1) * SHW]
                    sch.add("scalar", lambda e, o=gtile, i=yv: e.activation(o, i, AF.Silu),
                            reads=[yv], writes=[gtile], name="epsilu")
                    sch.add("vector", lambda e, o=gtile, a=gtile, b=stile: e.tensor_mul(o, a, b),
                            reads=[gtile, stile], writes=[gtile], name="epmul")
                    sch.add("vector", lambda e, o=xtile, a=xtile, b=gtile: e.tensor_add(o, a, b),
                            reads=[gtile, xtile], writes=[xtile], name="epadd")
                    sch.dma("sync",
                            x1T[dc * 128:(dc + 1) * 128, s0 + sw * SHW:s0 + (sw + 1) * SHW],
                            xtile, name="epst")

    def emit_l1():
        pool_bf = core.pool_bf
        emit_ln("x1T", "ln1")
        wq_t1 = core.din("wq1_t", [NH, D, 128], F32)
        wk_t1 = core.din("wk1_t", [NH, D, 128], F32)
        wv_t1 = core.din("wv1_t", [8, D, 256], F32)
        wo_t1 = core.din("wo1_t", [DC, D, 128], F32)
        lam_d = core.din("lam_row", [1, 16], F32)
        sTd = core.dram["sT"]
        x1T = core.dram["x1T"]
        outT = core.dout("outT", [D, S], F32)

        lr = lam_sb[0:1, 0:16]
        sch.dma("sync", lr, lam_d, name="lam")
        sch.add("scalar", lambda e, o=lr, i=lr: e.activation(o, i, AF.Sigmoid),
                reads=[lr], writes=[lr], name="lamsig")
        sch.add("vector", lambda e, o=lr, i=lr: e.tensor_scalar(o, i, -1.0, None, ALU.mult),
                reads=[lr], writes=[lr], name="lamneg")
        pl = pool[7][:, 0:16]
        sch.add("tensor", lambda e, o=pl, r=lr: e.matmul(o, ones_row[:], r, start=True, stop=True),
                reads=[lr, ones_row[:]], writes=[pool[7][:, 0:512]], name="lambc")
        nlam = lam_sb[:, 16:32]
        sch.add("vector", lambda e, o=nlam, i=pl: e.tensor_copy(o, i),
                reads=[pool[7][:, 0:512]], writes=[nlam], name="lamsb")

        for pr in range(NH // 2):
            vslab = at_wslab[:, 0:4096]
            sch.dma("gpsimd", vslab.rearrange("p (c n) -> p c n", n=256),
                    wv_t1[pr].rearrange("(c p) n -> p c n", p=128), name=f"wv{pr}")
            for jc in range(JC):
                pv = pool[6 + jc % 2][:, 0:256]
                for dc in range(DC):
                    l = xn_ap(dc, s0=jc * 128, sl=128)
                    r = vslab[:, dc * 256:(dc + 1) * 256]
                    sch.add("tensor", lambda e, o=pv, l=l, r=r, st=(dc == 0), sp=(dc == DC - 1):
                            e.matmul(o, l, r, start=st, stop=sp, skip_group_check=True),
                            reads=[l, r], writes=[pv], name="v1mm")
                vv = l1_v2[:, jc * 256:(jc + 1) * 256]
                sch.add("vector", lambda e, o=vv, i=pv: e.tensor_copy(o, i),
                        reads=[pv], writes=[vv], name="v1evac")
            for hp in range(2):
                n = 2 * pr + hp
                for wi, (wt, dst) in enumerate(((wq_t1, at_qr), (wk_t1, at_k))):
                    slab = at_wslab[:, wi * 2048:(wi + 1) * 2048]
                    sch.dma("gpsimd", slab.rearrange("p (c n) -> p c n", n=128),
                            wt[n].rearrange("(c p) n -> p c n", p=128), name=f"w1_{n}_{wi}")
                    for sc in range(NSC):
                        p6 = pool[6][:, 0:SB]
                        for dc in range(DC):
                            l = slab[:, dc * 128:(dc + 1) * 128]
                            r = xn_ap(dc, sc)
                            sch.add("tensor",
                                    lambda e, o=p6, l=l, r=r, st=(dc == 0), sp=(dc == DC - 1):
                                    e.matmul(o, l, r, start=st, stop=sp),
                                    reads=[l, r], writes=[p6], name="qk1mm")
                        dv = dst[:, sc * SB:(sc + 1) * SB]
                        if wi == 0:
                            sch.add("scalar", lambda e, o=dv, i=p6:
                                    e.activation(o, i, AF.Copy, scale=0.125),
                                    reads=[p6], writes=[dv], name="q1evac")
                        else:
                            sch.add("scalar", lambda e, o=dv, i=p6: e.activation(o, i, AF.Copy),
                                    reads=[p6], writes=[dv], name="k1evac")
                sqc = at_pt[:, 0:512]
                km2 = [srow_sb[0:1, 8 + 2 * h:9 + 2 * h] for h in range(2)]
                tmp = srow_sb[0:1, 12:13]
                for h in range(2):
                    P = slice(0, 64) if h == 0 else slice(64, 128)
                    acc = km2[h]
                    for sc in range(NSC):
                        ksl = at_k[P, sc * SB:(sc + 1) * SB]
                        sqv = sqc[P, 0:SB]
                        sch.add("scalar", lambda e, o=sqv, i=ksl: e.activation(o, i, AF.Square),
                                reads=[ksl], writes=[sqv], name="k1sq")
                        p4 = pool[4][0:1, 0:SB]
                        r = sqv
                        lw = ones_bf[P, 0:1]
                        sch.add("tensor", lambda e, o=p4, l=lw, r=r:
                                e.matmul(o, l, r, start=True, stop=True),
                                reads=[r, lw], writes=[p4], name="kn21")
                        if sc == 0:
                            sch.add("vector", lambda e, o=acc, i=p4: e.reduce_max(o, i, AX.X),
                                    reads=[p4], writes=[acc], name="km1")
                        else:
                            sch.add("vector", lambda e, o=tmp, i=p4: e.reduce_max(o, i, AX.X),
                                    reads=[p4], writes=[tmp], name="km1b")
                            sch.add("vector", lambda e, o=acc, a=acc, b=tmp:
                                    e.tensor_tensor(o, a, b, ALU.max),
                                    reads=[acc, tmp], writes=[acc], name="km1c")
                    sch.add("scalar", lambda e, o=acc, i=acc: e.activation(o, i, AF.Sqrt),
                            reads=[acc], writes=[acc], name="km1sqrt")
                    sch.add("vector", lambda e, o=acc, i=acc:
                            e.tensor_scalar(o, i, -1.0, None, ALU.mult),
                            reads=[acc], writes=[acc], name="km1neg")
                for ib in range(NSC):
                    jmax = (ib + 1) * (SB // 128)
                    combs = []
                    ncrows = [rows_bf[0:1, 0:SB], rows_bf[0:1, 512:512 + SB]]
                    for h in range(2):
                        P = slice(0, 64) if h == 0 else slice(64, 128)
                        qsl = at_qr[P, ib * SB:(ib + 1) * SB]
                        sqv = sqc[P, 0:SB]
                        sch.add("scalar", lambda e, o=sqv, i=qsl: e.activation(o, i, AF.Square),
                                reads=[qsl], writes=[sqv], name="q1sq")
                        p4r = pool[4][0:1, 0:SB]
                        r = sqv
                        lw = ones_bf[P, 0:1]
                        sch.add("tensor", lambda e, o=p4r, l=lw, r=r:
                                e.matmul(o, l, r, start=True, stop=True),
                                reads=[r, lw], writes=[p4r], name="qn21")
                        nrow = at_rows2[0:1, 0:SB]
                        sch.add("scalar", lambda e, o=nrow, i=p4r: e.activation(o, i, AF.Sqrt),
                                reads=[p4r], writes=[nrow], name="qn1")
                        sch.add("vector", lambda e, o=nrow, i=nrow, s=km2[h]:
                                e.tensor_scalar(o, i, s, None, ALU.mult),
                                reads=[nrow, km2[h]], writes=[nrow], name="negc1r")
                        sch.add("vector", lambda e, o=ncrows[h], i=nrow: e.tensor_copy(o, i),
                                reads=[nrow], writes=[ncrows[h]], name="negc1c")
                    lrows = [pool[2][0:1, 0:SB], pool[4][0:1, 0:SB]]
                    OBs = [pool[3][:, 0:SB], pool[5][:, 0:SB]]
                    Ps = [slice(0, 64), slice(64, 128)]

                    def emit_sexp1(h, jc, slot, ib=ib):
                        sb_ = slot % 2
                        SBANK = pool[sb_][:, 0:SB]
                        ncr = ncrows[h]
                        sch.add("tensor", lambda e, o=SBANK, r=ncr:
                                e.matmul(o, ones_row_bf[:], r, start=True, stop=False,
                                         skip_group_check=True),
                                reads=[ncr, ones_row_bf[:]], writes=[SBANK], name="s1r1")
                        l = at_k[Ps[h], jc * 128:(jc + 1) * 128]
                        rq = at_qr[Ps[h], ib * SB:(ib + 1) * SB]
                        sch.add("tensor", lambda e, o=SBANK, l=l, r=rq:
                                e.matmul(o, l, r, start=False, stop=True,
                                         skip_group_check=True),
                                reads=[l, rq], writes=[SBANK], name="s1mm")
                        pt = at_pt[:, sb_ * SB:(sb_ + 1) * SB]
                        diag_k = jc - ib * (SB // 128)
                        if diag_k >= 0:
                            sh = at_sh[:, sb_ * SB:(sb_ + 1) * SB]
                            mk = mask_k(diag_k)
                            sch.add("vector", lambda e, o=sh, a=SBANK, b=mk:
                                    e.tensor_add(o, a, b),
                                    reads=[SBANK, mk], writes=[sh], name="s1mask")
                            sch.add("scalar", lambda e, o=pt, i=sh: e.activation(o, i, AF.Exp),
                                    reads=[sh], writes=[pt], name="s1exp")
                        else:
                            sch.add("scalar", lambda e, o=pt, i=SBANK: e.activation(o, i, AF.Exp),
                                    reads=[SBANK], writes=[pt], name="s1exp")

                    def emit_lo1(h, jc, slot, jmax=jmax, hp=hp):
                        sb_ = slot % 2
                        pt = at_pt[:, sb_ * SB:(sb_ + 1) * SB]
                        lr_ = lrows[h]
                        sch.add("tensor", lambda e, o=lr_, r=pt, st=(jc == 0),
                                sp=(jc == jmax - 1):
                                e.matmul(o, ones_bf[:], r, start=st, stop=sp,
                                         skip_group_check=True),
                                reads=[pt, ones_bf[:]], writes=[lr_], name="l1mm")
                        lv = l1_v2[:, jc * 256 + hp * 128: jc * 256 + hp * 128 + 128]
                        OB = OBs[h]
                        sch.add("tensor", lambda e, o=OB, l=lv, r=pt, st=(jc == 0),
                                sp=(jc == jmax - 1):
                                e.matmul(o, l, r, start=st, stop=sp, skip_group_check=True),
                                reads=[pt, lv], writes=[OB], name="o1mm")

                    items = []
                    for jc in range(jmax):
                        items.append((0, jc))
                        items.append((1, jc))
                    for s in range(len(items) + 2):
                        if s >= 2:
                            emit_lo1(items[s - 2][0], items[s - 2][1], s - 2)
                        if s < len(items):
                            emit_sexp1(items[s][0], items[s][1], s)
                    for h in range(2):
                        rrow = at_rows2[0:1, 0:SB]
                        sch.add("vector", lambda e, o=rrow, i=lrows[h]:
                                e.tensor_scalar(o, i, 1e-30, None, ALU.add),
                                reads=[lrows[h]], writes=[rrow], name="l1eps")
                        sch.add("vector", lambda e, o=rrow, i=rrow: e.reciprocal(o, i),
                                reads=[rrow], writes=[rrow], name="l1rec")
                        p6b = pool[6][:, 0:SB]
                        sch.add("tensor", lambda e, o=p6b, r=rrow:
                                e.matmul(o, ones_row[:], r, start=True, stop=True),
                                reads=[rrow, ones_row[:]], writes=[p6b], name="r1bc")
                        rbv = at_rb[:, 0:SB]
                        sch.add("vector", lambda e, o=rbv, i=p6b: e.tensor_copy(o, i),
                                reads=[p6b], writes=[rbv], name="r1sb")
                        tc = (l1_tc1 if h == 0 else at_negc)[:, 0:SB]
                        sch.add("vector", lambda e, o=tc, a=OBs[h], b=rbv:
                                e.tensor_mul(o, a, b),
                                reads=[OBs[h], rbv], writes=[tc], name="o1s")
                        combs.append(tc)
                    ho = heads[:, n * S + ib * SB: n * S + (ib + 1) * SB]
                    nl = nlam[:, n:n + 1]
                    sch.add("vector", lambda e, o=ho, a=combs[1], s=nl, b=combs[0]:
                            e.scalar_tensor_tensor(o, a, s, b, ALU.mult, ALU.add),
                            reads=[combs[0], combs[1], nl], writes=[ho], name="dcomb")

        for dc in range(DC):
            wb = dc % 2
            wslab = at_wslab[:, wb * 2048:(wb + 1) * 2048]
            sch.dma("gpsimd", wslab.rearrange("p (c n) -> p c n", n=128),
                    wo_t1[dc].rearrange("(c p) n -> p c n", p=128), name=f"wo{dc}")
            for sc in range(NSC):
                bank = pool[5 + sc % 2][:, 0:SB]
                for oc in range(NH):
                    l = wslab[:, oc * 128:(oc + 1) * 128]
                    r = heads[:, oc * S + sc * SB: oc * S + (sc + 1) * SB]
                    sch.add("tensor", lambda e, o=bank, l=l, r=r, st=(oc == 0), sp=(oc == NH - 1):
                            e.matmul(o, l, r, start=st, stop=sp),
                            reads=[l, r], writes=[bank], name="womm")
                eb = sc % 2
                ev = at_sh[:, eb * SB:(eb + 1) * SB]
                sch.add("scalar", lambda e, o=ev, i=bank: e.activation(o, i, AF.Copy),
                        reads=[bank], writes=[ev], name="woevac")
                sch.dma("sync", sTd[dc * 128:(dc + 1) * 128, sc * SB:(sc + 1) * SB], ev,
                        name=f"s1Tw{dc}_{sc}")

        wgu = core.din("L1_ffn_w_gate_up", [16384, D], F32)
        wp1 = core.din("wproj1_t", [16, D, 512], F32)
        h_d = core.dtmp("h_l1", [8192, S], BF16)
        HC = 64
        SW = min(512, S)
        NSW = S // SW

        def quant_prefix(src_wb, g_buf, l_buf, scf, scb, G, tag):
            ng = G // 64
            s32a = scf[:, 0:min(ng, 32)]
            v3 = src_wb.rearrange("p (g k) -> p g k", k=64)
            if ng <= 32:
                sch.add("vector", lambda e, o=s32a, i=v3:
                        e.tensor_reduce(o, i, AX.X, ALU.add, apply_absolute_value=True),
                        reads=[src_wb], writes=[s32a], name=tag + "qs")
                sch.add("vector", lambda e, o=s32a, i=s32a:
                        e.tensor_scalar(o, i, 1.0 / 64, 1e-8, ALU.mult, ALU.max),
                        reads=[s32a], writes=[s32a], name=tag + "qc")
                sbf = scb[:, 0:ng]
                sch.add("vector", lambda e, o=sbf, i=s32a: e.tensor_copy(o, i),
                        reads=[s32a], writes=[sbf], name=tag + "qb")
                th = scf[:, 32:32 + ng]
                sch.add("vector", lambda e, o=th, i=sbf:
                        e.tensor_scalar(o, i, 0.501953125, None, ALU.mult),
                        reads=[sbf], writes=[th], name=tag + "qt")
                nth = scf[:, 64:64 + ng]
                sch.add("vector", lambda e, o=nth, i=th:
                        e.tensor_scalar(o, i, -1.0, None, ALU.mult),
                        reads=[th], writes=[nth], name=tag + "qn")
            else:
                raise NotImplementedError
            g_ = g_buf[:, 0:G]
            l_ = l_buf[:, 0:G]
            thb = th.unsqueeze(2).to_broadcast([128, ng, 64])
            nthb = nth.unsqueeze(2).to_broadcast([128, ng, 64])
            sch.add("vector", lambda e, o=g_.rearrange("p (g k) -> p g k", k=64), a=v3, b=thb:
                    e.tensor_tensor(o, a, b, ALU.is_gt),
                    reads=[src_wb, th], writes=[g_], name=tag + "qg")
            sch.add("vector", lambda e, o=l_.rearrange("p (g k) -> p g k", k=64), a=v3, b=nthb:
                    e.tensor_tensor(o, a, b, ALU.is_lt),
                    reads=[src_wb, nth], writes=[l_], name=tag + "ql")
            sch.add("vector", lambda e, o=g_, a=g_, b=l_: e.tensor_sub(o, a, b),
                    reads=[g_, l_], writes=[g_], name=tag + "qd")

        def quant_final(dst_wh, g_buf, scb, G, tag):
            ng = G // 64
            sbf = scb[:, 0:ng]
            g_ = g_buf[:, 0:G]
            sbb = sbf.unsqueeze(2).to_broadcast([128, ng, 64])
            sch.add("vector", lambda e, o=dst_wh.rearrange("p (g k) -> p g k", k=64),
                    a=g_.rearrange("p (g k) -> p g k", k=64), b=sbb:
                    e.tensor_tensor(o, a, b, ALU.mult),
                    reads=[g_, sbf], writes=[dst_wh], name=tag + "qw")

        def quant(dst_wh, src_wb, g_buf, l_buf, scf, scb, G, tag):
            quant_prefix(src_wb, g_buf, l_buf, scf, scb, G, tag)
            quant_final(dst_wh, g_buf, scb, G, tag)

        def transpose_tiles(src_bf, dst_bf, tiles, tag, evac_view=None,
                            psum_perm=False):
            for gi, i in enumerate(range(0, len(tiles), 8)):
                bk = 5 if gi % 2 == 0 else 7
                grp = tiles[i:i + 8]
                for j, (scol, dcol) in enumerate(grp):
                    o = pool_bf[bk][:, j * 128:(j + 1) * 128]
                    iv = src_bf[:, scol:scol + 128]
                    sch.add("tensor", lambda e, o=o, i=iv: e.transpose(o, i, identb[:]),
                            reads=[iv, identb[:]], writes=[pool[bk][:, 0:512]],
                            name=tag + "tr")
                if evac_view is not None:
                    ov = evac_view(i)
                    pv = pool_bf[bk][:, 0:128 * len(grp)].rearrange(
                        "p (t j n) -> p t j n", t=2, n=128)
                else:
                    ov = dst_bf[:, grp[0][1]: grp[0][1] + 128 * len(grp)]
                    pv = pool_bf[bk][:, 0:128 * len(grp)]
                sch.add("scalar", lambda e, o=ov, i=pv: e.copy(o, i),
                        reads=[pool[bk][:, 0:512]], writes=[ov], name=tag + "te")

        def gu_load(hc):
            wb = (f1_wbA, f1_wbB)[hc % 2]
            sch.dma("gpsimd", wb[:, 0:2048], wgu[hc * 128:(hc + 1) * 128, :],
                    name=f"gw{hc}")
            sch.dma("gpsimd", wb[:, 2048:4096],
                    wgu[8192 + hc * 128: 8192 + (hc + 1) * 128, :], name=f"uw{hc}")

        def gu_q_pre(hc, part):
            wb = (f1_wbA, f1_wbB)[hc % 2]
            off = part * 2048
            quant_prefix(wb[:, off:off + 2048], f1_glb, l_r3b, f1_sc, f1_scb,
                         2048, "gu"[part])

        def gu_q_fin_tr(hc, part):
            off = part * 2048
            tag = "gu"[part]
            quant_final(f1_wh[:, off:off + 2048], f1_glb, f1_scb, 2048, tag)
            transpose_tiles(f1_wh, f1_wh,
                            [(off + i * 128, off + i * 128) for i in range(16)], tag)

        def gu_mm_half(hc, half):
            off = half * 2048
            for dc in range(DC):
                l = f1_wh[:, off + dc * 128: off + (dc + 1) * 128]
                for sw in range(NSW):
                    o = pool[sw][:, 0:SW]
                    r = xn[:, dc * S + sw * SW: dc * S + (sw + 1) * SW]
                    sch.add("tensor",
                            lambda e, o=o, l=l, r=r, st=(dc == 0), sp=(dc == DC - 1):
                            e.matmul(o, l, r, start=st, stop=sp, skip_group_check=True),
                            reads=[l, r], writes=[o], name="gu1mm")
            if half == 0:
                for sw in range(NSW):
                    o = f1_t1[:, sw * SW:(sw + 1) * SW]
                    i = pool[sw][:, 0:SW]
                    sch.add("scalar", lambda e, o=o, i=i: e.activation(o, i, AF.Silu),
                            reads=[i], writes=[o], name="gu1silu")
            else:
                for sw in range(NSW):
                    o = f1_hq[:, sw * SW:(sw + 1) * SW]
                    a = f1_t1[:, sw * SW:(sw + 1) * SW]
                    b = pool[sw][:, 0:SW]
                    sch.add("vector", lambda e, o=o, a=a, b=b: e.tensor_mul(o, a, b),
                            reads=[a, b], writes=[o], name="gu1h")

        def gu_tail(hc):
            sch.dma("sync", h_d[hc * 128:(hc + 1) * 128, :], f1_hq[:, 0:S], name=f"hst{hc}")
            hsq = f1_hq[:, 0:S]
            sch.add("scalar", lambda e, o=hsq, i=hsq: e.activation(o, i, AF.Square),
                    reads=[hsq], writes=[hsq], name="h1sq")
            for sw in range(NSW):
                p6r = pool[6][0:1, 0:SW]
                r = hsq[:, sw * SW:(sw + 1) * SW]
                sch.add("tensor", lambda e, o=p6r, r=r:
                        e.matmul(o, ones_bf[:], r, start=True, stop=True),
                        reads=[r, ones_bf[:]], writes=[p6r], name="st1mm")
                dstr = rms_row[0:1, sw * SW:(sw + 1) * SW]
                if hc == 0:
                    sch.add("vector", lambda e, o=dstr, i=p6r: e.tensor_copy(o, i),
                            reads=[p6r], writes=[dstr], name="st1cp")
                else:
                    sch.add("vector", lambda e, o=dstr, a=dstr, b=p6r: e.tensor_add(o, a, b),
                            reads=[p6r, dstr], writes=[dstr], name="st1ad")

        gu_load(0)
        gu_q_pre(0, 0)
        gu_q_fin_tr(0, 0)
        gu_q_pre(0, 1)
        gu_q_fin_tr(0, 1)
        for hc in range(HC):
            if hc + 1 < HC:
                gu_load(hc + 1)
                gu_q_pre(hc + 1, 0)
            gu_mm_half(hc, 0)
            if hc + 1 < HC:
                gu_q_fin_tr(hc + 1, 0)
                gu_q_pre(hc + 1, 1)
            gu_mm_half(hc, 1)
            if hc + 1 < HC:
                gu_q_fin_tr(hc + 1, 1)
            gu_tail(hc)
        rr = rms_row[0:1, 0:S]
        sch.add("vector", lambda e, o=rr, i=rr:
                e.tensor_scalar(o, i, 1.0 / 8192, None, ALU.mult),
                reads=[rr], writes=[rr], name="rmsm")
        sch.add("scalar", lambda e, o=rr, i=rr, b=ceps[0:1, 1:2]:
                e.activation(o, i, AF.Sqrt, bias=b),
                reads=[rr, ceps[0:1, 1:2]], writes=[rr], name="rmssq")
        sch.add("vector", lambda e, o=rr, i=rr: e.reciprocal(o, i),
                reads=[rr], writes=[rr], name="rmsrec")

        def dn_pw(kg, g2, t):
            dst = dn_w[:, t * 512:(t + 1) * 512]
            srcw = wp1[kg][g2 * 1024 + t * 128: g2 * 1024 + (t + 1) * 128, :]
            sch.dma("gpsimd", dst, srcw, name=f"pw{kg}_{g2}_{t}")

        def dn_q1(kg, g2, t):
            dc = g2 * 8 + t
            quant(dn_w[:, t * 512:(t + 1) * 512], dn_w[:, t * 512:(t + 1) * 512],
                  dn_g, l_r3, dn_sc, dn_scb, 512, f"p{dc}")

        def dn_quant_raw(kg, g2):
            for t in range(8):
                dn_pw(kg, g2, t)
            for t in range(8):
                dn_q1(kg, g2, t)

        def dn_tr(kg, g2):
            tiles = []
            for t in range(8):
                dc = g2 * 8 + t
                for j in range(4):
                    tiles.append((t * 512 + j * 128, j * 2048 + dc * 128))
            pT4 = pT_sb[:].rearrange("p (j dc n) -> p dc j n", j=4, n=128)

            def dn_evac(i, g2=g2, pT4=pT4):
                dc0 = g2 * 8 + i // 4
                return pT4[:, dc0:dc0 + 2, :, :]
            transpose_tiles(dn_w, pT_sb, tiles, "p", evac_view=dn_evac,
                            psum_perm=True)

        dn_quant_raw(0, 0)
        dn_tr(0, 0)
        dn_quant_raw(0, 1)
        dn_tr(0, 1)
        for kg in range(16):
            for c in range(4):
                hcg = kg * 4 + c
                hb = (f1_wbA, f1_wbB)[c % 2][:, (c // 2) * 2048:(c // 2) * 2048 + S]
                sch.dma("sync", hb, h_d[hcg * 128:(hcg + 1) * 128, :], name=f"hld{hcg}")
                for sw in range(NSW):
                    p6b = pool[6][:, 0:SW]
                    irow = rms_row[0:1, sw * SW:(sw + 1) * SW]
                    sch.add("tensor", lambda e, o=p6b, r=irow:
                            e.matmul(o, ones_row[:], r, start=True, stop=True),
                            reads=[irow, ones_row[:]], writes=[p6b], name="invbc")
                    o = hn_sb[:, c * 2048 + sw * SW: c * 2048 + (sw + 1) * SW]
                    a = hb[:, sw * SW:(sw + 1) * SW]
                    sch.add("vector", lambda e, o=o, a=a, b=p6b: e.tensor_mul(o, a, b),
                            reads=[a, p6b], writes=[o], name="hn1")
            if kg + 1 < 16:
                for t in range(8):
                    dn_pw(kg + 1, 0, t)
            for dc in range(DC):
                for swd in range(NSW):
                    bk = (dc * NSW + swd) % 4
                    for c in range(4):
                        l = pT_sb[:, c * 2048 + dc * 128: c * 2048 + (dc + 1) * 128]
                        o = pool[bk][:, 0:SW]
                        r = hn_sb[:, c * 2048 + swd * SW: c * 2048 + (swd + 1) * SW]
                        sch.add("tensor",
                                lambda e, o=o, l=l, r=r, st=(c == 0), sp=(c == 3):
                                e.matmul(o, l, r, start=st, stop=sp, skip_group_check=True),
                                reads=[l, r], writes=[o], name="dn1mm")
                    yv = y_l1[:, dc * S + swd * SW: dc * S + (swd + 1) * SW]
                    po = pool[bk][:, 0:SW]
                    if kg == 0:
                        sch.add("vector", lambda e, o=yv, i=po: e.tensor_copy(o, i),
                                reads=[po], writes=[yv], name="y1cp")
                    else:
                        sch.add("vector", lambda e, o=yv, a=yv, b=po: e.tensor_add(o, a, b),
                                reads=[po, yv], writes=[yv], name="y1ad")
                if kg + 1 < 16:
                    if dc == 0:
                        dn_q1(kg + 1, 0, 0)
                        dn_q1(kg + 1, 0, 1)
                    elif dc < 7:
                        dn_q1(kg + 1, 0, dc + 1)
                    elif dc == 7:
                        dn_tr(kg + 1, 0)
                        for t in range(8):
                            dn_pw(kg + 1, 1, t)
                    elif dc < 12:
                        dn_q1(kg + 1, 1, 2 * (dc - 8))
                        dn_q1(kg + 1, 1, 2 * (dc - 8) + 1)
            if kg + 1 < 16:
                dn_tr(kg + 1, 1)

        for dc in range(DC):
            for sc in range(NSC):
                eb = sc % 2
                stile = ep_s[:, eb * 512:eb * 512 + SB]
                xtile = ep_x[:, eb * 512:eb * 512 + SB]
                gtile = ep_g[:, eb * 512:eb * 512 + SB]
                sch.dma("sync", stile, sTd[dc * 128:(dc + 1) * 128, sc * SB:(sc + 1) * SB],
                        name="ep1s")
                sch.dma("sync", xtile, x1T[dc * 128:(dc + 1) * 128, sc * SB:(sc + 1) * SB],
                        name="ep1x")
                yv = y_l1[:, dc * S + sc * SB: dc * S + (sc + 1) * SB]
                sch.add("scalar", lambda e, o=gtile, i=yv: e.activation(o, i, AF.Silu),
                        reads=[yv], writes=[gtile], name="ep1silu")
                sch.add("vector", lambda e, o=gtile, a=gtile, b=stile: e.tensor_mul(o, a, b),
                        reads=[gtile, stile], writes=[gtile], name="ep1mul")
                sch.add("vector", lambda e, o=xtile, a=xtile, b=gtile: e.tensor_add(o, a, b),
                        reads=[gtile, xtile], writes=[xtile], name="ep1add")
                sch.dma("sync", outT[dc * 128:(dc + 1) * 128, sc * SB:(sc + 1) * SB],
                        xtile, name="ep1st")

    xT = core.din("xT", [D, S], F32)
    if 0 in layers:
        emit_ln("xT", "ln0")
        if "xn" in dbg:
            o = core.dout("dbg_xn", [D, S], BF16)
            for dc in range(DC):
                sch.dma("sync", o[dc * 128:(dc + 1) * 128, :], xn[:, dc * S:(dc + 1) * S],
                        name="dbgxn")
        emit_attn0()
        emit_moe0()
    else:
        core.dram["x1T"] = core.din("x1T_in", [D, S], F32)
        core.dram["xT"] = core.dram["x1T"]

    if 1 in layers:
        emit_l1()
    else:
        x1o = core.dout("x1T_out", [D, S], F32)
        for dc in range(DC):
            sch.dma("sync", x1o[dc * 128:(dc + 1) * 128, :],
                    core.dram["x1T"][dc * 128:(dc + 1) * 128, :], name="x1out")

    if "sT" in dbg:
        o = core.dout("dbg_sT", [D, S], F32)
        for dc in range(DC):
            sch.dma("sync", o[dc * 128:(dc + 1) * 128, :],
                    core.dram["sT"][dc * 128:(dc + 1) * 128, :], name="dbgsT")

    with core.nc.Block() as blk:
        sch.emit(blk)
    return core


def prep_core_inputs(inputs, b, S, layers=(0, 1)):
    c = host_constants(S, min(512, S))
    m = dict(c)
    x = np.asarray(inputs["L0_attn_hidden_states"])[b][:S]
    m["xT"] = np.ascontiguousarray(x.T)
    if 0 in layers:
        wq = np.asarray(inputs["L0_attn_w_q"])
        m["wqT_t"] = np.ascontiguousarray(wq.T.reshape(D, NH, HD).transpose(1, 0, 2))
        wkv = np.asarray(inputs["L0_attn_w_kv"])
        m["wkvT_t"] = np.ascontiguousarray(wkv.T.reshape(D, 2, HD).transpose(1, 0, 2))
        wp = np.asarray(inputs["L0_attn_w_proj"])
        m["wprojT_t"] = np.ascontiguousarray(wp.T.reshape(D, DC, 128).transpose(1, 0, 2))
        m["bproj_col"] = np.ascontiguousarray(
            np.asarray(inputs["L0_attn_b_proj"]).reshape(16, 128).T)
        m["L0_ffn_W_g"] = np.asarray(inputs["L0_ffn_W_g"])
        m["ws_gate_t"] = np.ascontiguousarray(
            np.asarray(inputs["L0_ffn_Ws_gate"]).reshape(D, 2, 1024).transpose(1, 0, 2))
        m["ws_up_t"] = np.ascontiguousarray(
            np.asarray(inputs["L0_ffn_Ws_up"]).reshape(D, 2, 1024).transpose(1, 0, 2))
        m["L0_ffn_Ws_down"] = np.asarray(inputs["L0_ffn_Ws_down"])
        m["L0_ffn_We_gate"] = np.asarray(inputs["L0_ffn_We_gate"])
        m["L0_ffn_We_up"] = np.asarray(inputs["L0_ffn_We_up"])
        m["L0_ffn_We_down"] = np.asarray(inputs["L0_ffn_We_down"])
    if 1 in layers:
        wq = np.asarray(inputs["L1_attn_Wq"])
        m["wq1_t"] = np.ascontiguousarray(wq.reshape(D, NH, HD).transpose(1, 0, 2))
        wk = np.asarray(inputs["L1_attn_Wk"])
        m["wk1_t"] = np.ascontiguousarray(wk.reshape(D, NH, HD).transpose(1, 0, 2))
        wv = np.asarray(inputs["L1_attn_Wv"])
        m["wv1_t"] = np.ascontiguousarray(wv.reshape(D, 8, 256).transpose(1, 0, 2))
        wo = np.asarray(inputs["L1_attn_Wo"])
        m["wo1_t"] = np.ascontiguousarray(wo.reshape(D, DC, 128).transpose(1, 0, 2))
        m["lam_row"] = np.ascontiguousarray(
            np.asarray(inputs["L1_attn_lambda_init"]).reshape(1, 16))
        m["L1_ffn_w_gate_up"] = np.asarray(inputs["L1_ffn_w_gate_up"])
        m["wproj1_t"] = np.ascontiguousarray(
            np.asarray(inputs["L1_ffn_w_proj"]).reshape(D, 16, 512).transpose(1, 0, 2))
    return m


def kernel(**inputs):
    S = 2048
    core = build_program(S=S, layers=(0, 1))
    in_maps = []
    for b in range(8):
        m = prep_core_inputs(inputs, b, S)
        in_maps.append({k: np.ascontiguousarray(v) for k, v in m.items()
                        if k in core.used_inputs})
    missing = [k for k in core.used_inputs if k not in in_maps[0]]
    assert not missing, missing
    res = run_bass_kernel_spmd(core.nc, in_maps, core_ids=list(range(8)))
    outs = [np.ascontiguousarray(np.asarray(r["outT"]).T) for r in res.results]
    return np.stack(outs).astype(np.float32)
